# version 02
# baseline (speedup 1.0000x reference)
import os
import sys
if '/opt/trn_rl_repo' not in sys.path:
    sys.path.insert(0, '/opt/trn_rl_repo')
import numpy as np
import concourse.bass as bass
import concourse.mybir as mybir
from concourse.bass_utils import run_bass_kernel_spmd
from contextlib import ExitStack

F32 = mybir.dt.float32
BF16 = mybir.dt.bfloat16
AF = mybir.ActivationFunctionType
ALU = mybir.AluOpType
AXX = mybir.AxisListType.X

T = 2048
TT = 1024
HALO = 3
DM = 1024
DI = 2048
NH = 32
HP = 64
NS = 16
CH = 64
NCH_T = TT // CH
CONVD = 2080
DH = 4096
TC1 = 8
TC2 = 16
NZ = T // TC1
NQ = NZ // TC2

DBG = ""
NCC = 8

class B:
    def __init__(self, nc, block):
        self.nc = nc
        self.block = block
        self.semc = {}
        self.stack = ExitStack()
        self._sb = {}
        self.engines = ["tensor", "vector", "scalar", "gpsimd", "sync"]
        for e in self.engines:
            self.sem("prog_" + e)

    def sem(self, name):
        h = self.stack.enter_context(self.nc.semaphore(name))
        self.semc[name] = [h, 0]
        return name

    def inc(self, inst, name, by=1):
        self.semc[name][0] and inst.then_inc(self.semc[name][0], by)
        self.semc[name][1] += by
        return self.semc[name][1]

    def bump(self, name, by):
        self.semc[name][1] += by
        return self.semc[name][1]

    def val(self, name):
        return self.semc[name][1]

    def h(self, name):
        return self.semc[name][0]

    def wait(self, eng, name, value=None):
        eng.wait_ge(self.h(name), self.val(name) if value is None else value)

    def dma(self, q, out, in_, sem):
        q.dma_start(out=out, in_=in_).then_inc(self.h(sem), 16)
        return self.bump(sem, 16)

    def init_pool(self, cols):
        self.pool_guard = self.nc.sbuf_tensor("pool", [128, cols], F32)
        h = self.pool_guard.__enter__()
        self.pf = h.ap()
        self.pb = self.pf.bitcast(BF16)
        self.free_list = [(0, cols)]
        self.allocs = {}

    def sb(self, name, shape, dtype, top=False):
        assert name not in self.allocs, name
        n_inner = 1
        for s in shape[1:]:
            n_inner *= s
        if dtype == F32:
            units = n_inner
        else:
            units = (n_inner + 1) // 2
        units = (units + 3) & ~3
        order = reversed(list(enumerate(self.free_list))) if top else enumerate(self.free_list)
        for i, (off, size) in order:
            if size >= units:
                if top:
                    off = off + size - units
                    if size == units:
                        self.free_list.pop(i)
                    else:
                        self.free_list[i] = (self.free_list[i][0], size - units)
                elif size == units:
                    self.free_list.pop(i)
                else:
                    self.free_list[i] = (off + units, size - units)
                self.allocs[name] = (off, units)
                base = self.pf if dtype == F32 else self.pb
                scale = 1 if dtype == F32 else 2
                v = base[0:shape[0], off*scale: off*scale + n_inner]
                if len(shape) > 2:
                    pat = "p (" + " ".join(f"d{j}" for j in range(len(shape)-1)) + ") -> p " +                           " ".join(f"d{j}" for j in range(len(shape)-1))
                    kw = {f"d{j}": shape[j+1] for j in range(len(shape)-2)}
                    v = v.rearrange(pat, **kw)
                return v
        raise MemoryError(f"pool OOM for {name}: need {units}, free={self.free_list}, allocs={ {k: v for k, v in sorted(self.allocs.items(), key=lambda x: x[1][0])} }")

    def free(self, name):
        off, units = self.allocs.pop(name)
        self.free_list.append((off, units))
        self.free_list.sort()
        merged = []
        for o, s in self.free_list:
            if merged and merged[-1][0] + merged[-1][1] == o:
                merged[-1] = (merged[-1][0], merged[-1][1] + s)
            else:
                merged.append((o, s))
        self.free_list = [tuple(x) for x in merged]

    def gsync(self, g, sem):
        self.inc(g.nop(), sem)
        g.wait_ge(self.h(sem), self.val(sem))

    def ch(self, E, inst, sem):
        self.inc(inst, sem)
        E.wait_ge(self.h(sem), self.val(sem))

    def barrier(self):
        targets = {}
        for e in self.engines:
            @getattr(self.block, e)
            def _(E, e=e):
                self.inc(E.drain(), "prog_" + e)
            targets[e] = self.val("prog_" + e)
        for e in self.engines:
            @getattr(self.block, e)
            def _(E, e=e):
                for o in self.engines:
                    if o != e:
                        E.wait_ge(self.h("prog_" + o), targets[o])


def emit_ln(b, src_hbm, dst_bf, width, scratch, tag, u_buf=None):
    scratch = dict(scratch)
    scratch["u"] = u_buf if u_buf is not None else b.sb("lnu_" + tag, [128, 8, width], F32)
    scratch["ubf"] = b.sb("lnubf_" + tag, [128, 8, width], BF16)
    scratch["sqbf"] = b.sb("lnsq_" + tag, [128, 8, width], BF16)
    scratch["m_row"] = b.sb("lnmr_" + tag, [1, width], F32)
    scratch["r_row"] = b.sb("lnrr_" + tag, [1, width], F32)
    scratch["t1_row"] = b.sb("lnt1_" + tag, [1, width], F32)
    scratch["mb"] = b.sb("lnmb_" + tag, [128, 512], F32)
    scratch["rb"] = b.sb("lnrb_" + tag, [128, 512], F32)
    scratch["tw"] = b.sb("lntw_" + tag, [128, 512], F32)
    """LN over d=1024 of a feature-major fp32 HBM tensor src_hbm [128,8,width].
    Writes dst_bf [128,8,width] bf16 = (x - m) * rsqrt(var+eps).
    scratch: dict with persistent tiles (created by caller once):
      u (f32 [128,8,width_max]), ubf, sqbf (bf16 same), rows (f32 [1,width_max*?]),
      mb, rb (f32 [128, 512])
    Loads src from HBM (fp32 via sync + bf16 cast via gpsimd).
    """
    nc, block = b.nc, b.block
    u = scratch["u"]; ubf = scratch["ubf"]; sqbf = scratch["sqbf"]
    m_row = scratch["m_row"]; r_row = scratch["r_row"]
    t1_row = scratch["t1_row"]
    ones_bf = scratch["ones_bf"]; ones_f32r = scratch["ones_f32r"]
    mb = scratch["mb"]; rb = scratch["rb"]
    sD, sDs, sT, sV, sS = scratch["sems"]

    @block.sync
    def _(sy):
        b.dma(sy, u[:, :, 0:width], src_hbm, sDs)
    @block.gpsimd
    def _(g):
        b.dma(g, ubf[:, :, 0:width], src_hbm, sD)

    @block.scalar
    def _(s):
        b.wait(s, sDs)
        for kc in range(8):
            ii = s.activation(sqbf[:, kc, 0:width], u[:, kc, 0:width], AF.Square, bias=0.0, scale=1.0)
        b.inc(ii, sS)

    wins = [(o, min(512, width - o)) for o in range(0, width, 512)]
    with nc.psum_tensor("psst_" + tag, [1, 2560], F32) as ps_stat, \
         nc.psum_tensor("psbc_" + tag, [128, 1024], F32) as ps_bc:
        for ridx, srcb in ((0, ubf), (1, sqbf)):
            @block.tensor
            def _(t, ridx=ridx, srcb=srcb):
                if ridx == 0:
                    b.wait(t, sD)
                    b.wait(t, sS)
                else:
                    b.wait(t, sV)
                for (o, n) in wins:
                    for kc in range(8):
                        ii = t.matmul(ps_stat[0:1, o:o+n], ones_bf[:, 0:1],
                                      srcb[:, kc, o:o+n], start=(kc == 0), stop=(kc == 7))
                b.inc(ii, sT)

            @block.vector
            def _(v, ridx=ridx):
                b.wait(v, sT)
                b.inc(v.tensor_copy((m_row if ridx == 0 else r_row)[0:1, 0:width], ps_stat[0:1, 0:width]), sV)

        @block.vector
        def _(v):
            b.wait(v, sV)
            b.ch(v, v.tensor_scalar_mul(m_row[0:1, 0:width], m_row[0:1, 0:width], 1.0 / DM), sV)
            b.ch(v, v.tensor_scalar_mul(r_row[0:1, 0:width], r_row[0:1, 0:width], 1.0 / DM), sV)
            b.ch(v, v.tensor_mul(t1_row[0:1, 0:width], m_row[0:1, 0:width], m_row[0:1, 0:width]), sV)
            b.ch(v, v.tensor_sub(r_row[0:1, 0:width], r_row[0:1, 0:width], t1_row[0:1, 0:width]), sV)
            b.inc(v.tensor_scalar_add(r_row[0:1, 0:width], r_row[0:1, 0:width], 1e-5), sV)
        @block.scalar
        def _(s):
            b.wait(s, sV)
            b.inc(s.activation(r_row[0:1, 0:width], r_row[0:1, 0:width], AF.Sqrt, bias=0.0, scale=1.0), sS)
        @block.vector
        def _(v):
            b.wait(v, sS)
            b.inc(v.reciprocal(r_row[0:1, 0:width], r_row[0:1, 0:width]), sV)

        for (o, n) in wins:
            @block.tensor
            def _(t, o=o, n=n):
                b.wait(t, sV)
                if o > 0:
                    b.wait(t, sV)
                t.matmul(ps_bc[:, 0:n], ones_f32r[0:1, :], m_row[0:1, o:o+n], start=True, stop=True)
                b.inc(t.matmul(ps_bc[:, 512:512+n], ones_f32r[0:1, :], r_row[0:1, o:o+n], start=True, stop=True), sT)
            @block.vector
            def _(v, o=o, n=n):
                b.wait(v, sT)
                b.ch(v, v.tensor_copy(mb[:, 0:n], ps_bc[:, 0:n]), sV)
                b.ch(v, v.tensor_copy(rb[:, 0:n], ps_bc[:, 512:512+n]), sV)
                for kc in range(8):
                    b.ch(v, v.tensor_sub(scratch["tw"][:, 0:n], u[:, kc, o:o+n], mb[:, 0:n]), sV)
                    ii = v.tensor_mul(dst_bf[:, kc, o:o+n], scratch["tw"][:, 0:n], rb[:, 0:n])
                    if kc < 7:
                        b.ch(v, ii, sV)
                b.inc(ii, sV)
    b.barrier()
    for nm in ("lnubf_", "lnsq_", "lnmr_", "lnrr_", "lnt1_", "lnmb_", "lnrb_", "lntw_"):
        b.free(nm + tag)
    if u_buf is None:
        b.free("lnu_" + tag)


def emit_mamba(b, io, x_ln, C, hbm, dbg):
    nc, block = b.nc, b.block
    sD, sDs, sT, sV, sS, sG, sCC = C["sems"]
    sDw = C["sDw"]; sDso = C["sDso"]
    ones_bf = C["ones_bf"]; ones_f32r = C["ones_f32r"]; id_bf = C["id_bf"]
    id_f32 = C["id_f32"]; maskST = C["maskST"]; conv_w = C["conv_w"]
    conv_b = C["conv_b"]; dt_bias = C["dt_bias"]; A_col = C["A_col"]
    D0 = C["D0"]; norm_w = C["norm_w"]; m8_64 = C["m8_64"]

    acs2 = b.sb("acs2", [32, 2, TT], F32)
    acsT = b.sb("acsT", [128, 16, 32], F32)
    dtT = b.sb("dtT", [128, 16, 32], F32)
    dexT = b.sb("dexT", [128, 16, 32], F32)
    asum = b.sb("asum", [32, 32], F32)
    state_in = b.sb("state_in", [64, 512], F32)
    ST3 = [b.sb("ST3a", [64, 8192], F32, top=True), b.sb("ST3b", [64, 8192], F32, top=True)]
    adt_gate = b.sb("adt_gate", [32, TT], F32)

    @block.vector
    def _(v):
        b.ch(v, v.memset(adt_gate[:], 1.0), sV)
        for c in range(NCH_T):
            ii = v.memset(adt_gate[:, c * CH:c * CH + 1], 0.0)
        b.inc(ii, sV)

    for tt in range(2):
        o_t = HALO + tt * TT
        xcv = b.sb("xcv", [128, 17, TT], BF16)
        xbc = b.sb("xbc", [128, 17, HALO + TT], BF16)
        wb = [b.sb(f"wbA{i}", [128, 8, 128], BF16) for i in range(2)]
        dtraw = b.sb("dtraw", [32, TT], F32)
        conv_acc = b.sb("conv_acc", [128, TT], F32)

        tcounts = {}
        loop_t0 = b.val(sT)
        with nc.psum_tensor(f"ps_xbc{tt}", [128, 1536], F32) as ps_xbc, \
             nc.psum_tensor(f"ps_dt{tt}", [32, 1024], F32) as ps_dt:
            for si, m in enumerate(range(16, 33)):
                buf = wb[si % 2]
                @block.gpsimd
                def _(g, m=m, buf=buf, si=si):
                    g.wait_ge(b.h(sT), tcounts[si - 2] if si >= 2 else loop_t0)
                    if si < 2:
                        b.wait(g, sV)
                    b.dma(g, buf[:], io["wins"][m].rearrange("(k p) j -> p k j", p=128), sDw[si % 2])
                @block.tensor
                def _(t, buf=buf, si=si):
                    b.wait(t, sDw[si % 2])
                    if si >= 1:
                        b.wait(t, sV)
                    for k in range(8):
                        t.matmul(ps_xbc[:, 0:HALO], buf[:, k, :], x_ln[:, k, o_t-HALO:o_t],
                                 start=(k == 0), stop=(k == 7))
                        t.matmul(ps_xbc[:, 512:1024], buf[:, k, :], x_ln[:, k, o_t:o_t+512],
                                 start=(k == 0), stop=(k == 7))
                        ii = t.matmul(ps_xbc[:, 1024:1536], buf[:, k, :], x_ln[:, k, o_t+512:o_t+1024],
                                      start=(k == 0), stop=(k == 7))
                    tcounts[si] = b.inc(ii, sT)
                @block.vector
                def _(v, m=m):
                    b.wait(v, sT)
                    cc = m - 16
                    v.tensor_copy(xbc[:, cc, 0:HALO], ps_xbc[:, 0:HALO])
                    b.inc(v.tensor_copy(xbc[:, cc, HALO:HALO+TT], ps_xbc[:, 512:1536]), sV)
            @block.gpsimd
            def _(g):
                g.wait_ge(b.h(sT), tcounts[16])
                b.dma(g, wb[1][:, :, 0:32], io["windt"].rearrange("(k p) j -> p k j", p=128), sDw[0])
            @block.tensor
            def _(t):
                b.wait(t, sDw[0])
                b.wait(t, sV)
                for k in range(8):
                    t.matmul(ps_dt[:, 0:512], wb[1][:, k, 0:32], x_ln[:, k, o_t:o_t+512],
                             start=(k == 0), stop=(k == 7))
                    ii = t.matmul(ps_dt[:, 512:1024], wb[1][:, k, 0:32], x_ln[:, k, o_t+512:o_t+1024],
                                  start=(k == 0), stop=(k == 7))
                b.inc(ii, sT)
            @block.vector
            def _(v):
                b.wait(v, sT)
                b.inc(v.tensor_copy(dtraw[:], ps_dt[:]), sV)
        b.barrier()
        b.free("wbA0"); b.free("wbA1")

        for cc in range(17):
            @block.vector
            def _(v, cc=cc):
                if cc == 0:
                    b.wait(v, sV)
                else:
                    b.wait(v, sS)
                b.ch(v, v.tensor_scalar(conv_acc[:], xbc[:, cc, 0:TT], conv_w[:, cc, 0:1], None, ALU.mult), sV)
                for j in range(1, 4):
                    b.ch(v, v.scalar_tensor_tensor(conv_acc[:], xbc[:, cc, j:j+TT], conv_w[:, cc, j:j+1],
                                                   conv_acc[:], ALU.mult, ALU.add), sV)
            @block.scalar
            def _(s, cc=cc):
                b.wait(s, sV)
                b.inc(s.activation(xcv[:, cc, :], conv_acc[:], AF.Silu,
                                   bias=conv_b[:, cc:cc+1], scale=1.0), sS)
        @block.gpsimd
        def _(g):
            b.wait(g, sS)
            b.dma(g, hbm["xcv"][:, :, tt*TT:(tt+1)*TT], xcv[:], sD)
            g.wait_ge(b.h(sD), b.val(sD))
        b.barrier()
        b.free("xbc"); b.free("conv_acc")
        x_dt_T = b.sb("x_dt_T", [128, 8, 2048], BF16)
        dtv = b.sb("dtv", [32, TT], F32)
        rw1 = b.sb("rw1", [32, TT], F32)
        dex_fm = b.sb("dex_fm", [32, TT], F32)

        @block.vector
        def _(v):
            b.wait(v, sV)
            b.inc(v.tensor_scalar(dtraw[:], dtraw[:], dt_bias[:, 0:1], None, ALU.add), sV)
        @block.scalar
        def _(s):
            b.wait(s, sV)
            b.ch(s, s.activation(rw1[:], dtraw[:], AF.Abs, bias=0.0, scale=1.0), sS)
            b.inc(s.activation(rw1[:], rw1[:], AF.Exp, bias=0.0, scale=-1.0), sS)
        @block.vector
        def _(v):
            b.wait(v, sS)
            b.inc(v.tensor_scalar_add(rw1[:], rw1[:], 1.0), sV)
        @block.scalar
        def _(s):
            b.wait(s, sV)
            b.inc(s.activation(rw1[:], rw1[:], AF.Ln, bias=0.0, scale=1.0), sS)
        @block.vector
        def _(v):
            b.wait(v, sS)
            b.ch(v, v.tensor_scalar_max(dtv[:], dtraw[:], 0.0), sV)
            b.ch(v, v.tensor_add(dtv[:], dtv[:], rw1[:]), sV)
            b.ch(v, v.tensor_scalar(rw1[:], dtv[:], A_col[:, 0:1], None, ALU.mult), sV)
            b.ch(v, v.tensor_tensor_scan(acs2[:, tt, :], adt_gate[:], rw1[:], 0.0,
                                         ALU.mult, ALU.add), sV)
            b.ch(v, v.tensor_copy(asum[:, tt*16:(tt+1)*16], acs2[:, tt, CH-1:TT:CH]), sV)
            for c in range(NCH_T):
                ii = v.tensor_scalar(dex_fm[:, c*CH:(c+1)*CH], acs2[:, tt, c*CH:(c+1)*CH],
                                     acs2[:, tt, (c+1)*CH-1:(c+1)*CH], None, ALU.subtract)
            b.inc(ii, sV)
        @block.scalar
        def _(s):
            b.wait(s, sV)
            b.inc(s.activation(dex_fm[:], dex_fm[:], AF.Exp, bias=0.0, scale=-1.0), sS)

        b.barrier()
        for nm in ("dtraw", "rw1"):
            b.free(nm)

        with nc.psum_tensor(f"ps_trf{tt}", [128, 512], F32) as ps_trf, \
             nc.psum_tensor(f"ps_trb{tt}", [128, 512], BF16) as ps_trb:
            for tb in range(8):
                @block.tensor
                def _(t, tb=tb):
                    b.wait(t, sS)
                    b.wait(t, sV)
                    t.transpose(ps_trf[:, 0:32], acs2[:, tt, tb*128:(tb+1)*128], id_f32[0:32, 0:32])
                    t.transpose(ps_trf[:, 32:64], dtv[:, tb*128:(tb+1)*128], id_f32[0:32, 0:32])
                    b.inc(t.transpose(ps_trf[:, 64:96], dex_fm[:, tb*128:(tb+1)*128], id_f32[0:32, 0:32]), sT)
                @block.vector
                def _(v, tb=tb):
                    b.wait(v, sT)
                    v.tensor_copy(acsT[:, tt*8+tb, :], ps_trf[:, 0:32])
                    v.tensor_copy(dtT[:, tt*8+tb, :], ps_trf[:, 32:64])
                    b.inc(v.tensor_copy(dexT[:, tt*8+tb, :], ps_trf[:, 64:96]), sV)
            b.barrier()
            b.free("dtv"); b.free("dex_fm")
            B_T = b.sb("B_T", [128, 8, 16], BF16)
            Bd_T = b.sb("Bd_T", [128, 8, 512], BF16)
            for tb in range(8):
                for kg in range(4):
                    @block.tensor
                    def _(t, tb=tb, kg=kg):
                        b.wait(t, sV)
                        for j in range(4):
                            ii = t.transpose(ps_trb[:, j*128:(j+1)*128],
                                             xcv[:, kg*4+j, tb*128:(tb+1)*128], id_bf[:, :])
                        b.inc(ii, sT)
                    @block.vector
                    def _(v, tb=tb, kg=kg):
                        b.wait(v, sT)
                        o4 = x_dt_T[:, tb, kg*512:(kg+1)*512].rearrange("p (a h2 c) -> p a h2 c", a=4, h2=2)
                        i4 = ps_trb[:].rearrange("p (a h2 c) -> p a h2 c", a=4, h2=2)
                        d4 = dtT[:, tt*8+tb, kg*8:kg*8+8].rearrange("p (a h2) -> p a h2", a=4) \
                            .unsqueeze(3).broadcast_to([128, 4, 2, 64])
                        b.inc(v.tensor_tensor(o4, i4, d4, ALU.mult), sV)
            for tb in range(8):
                @block.tensor
                def _(t, tb=tb):
                    b.wait(t, sV)
                    b.inc(t.transpose(ps_trb[:, 0:16], xcv[0:16, 16, tb*128:(tb+1)*128],
                                      id_bf[0:16, 0:16]), sT)
                @block.vector
                def _(v, tb=tb):
                    b.wait(v, sT)
                    b.inc(v.tensor_copy(B_T[:, tb, :], ps_trb[:, 0:16]), sV)
            for tb in range(8):
                @block.vector
                def _(v, tb=tb):
                    if tb == 0:
                        b.wait(v, sV)
                    o3 = Bd_T[:, tb, :].rearrange("p (h n) -> p h n", h=32)
                    i0 = B_T[:, tb, :].unsqueeze(1).broadcast_to([128, 32, 16])
                    i1 = dexT[:, tt*8+tb, :].unsqueeze(2).broadcast_to([128, 32, 16])
                    ii = v.tensor_tensor(o3, i0, i1, ALU.mult)
                    if tb == 7:
                        b.inc(ii, sV)

        b.barrier()
        with nc.psum_tensor(f"ps_st{tt}", [128, 4096], F32) as ps_st:
            for half in range(2):
                @block.tensor
                def _(t, half=half):
                    b.wait(t, sV)
                    if half == 1:
                        b.wait(t, sV)
                    for cl in range(8):
                        c_local = half*8 + cl
                        rb_, tb = (c_local % 2)*64, c_local // 2
                        for h in range(NH):
                            ii = t.matmul(ps_st[0:64, cl*512 + h*16: cl*512 + (h+1)*16],
                                          x_dt_T[rb_:rb_+64, tb, h*64:(h+1)*64],
                                          Bd_T[rb_:rb_+64, tb, h*16:(h+1)*16],
                                          start=True, stop=True)
                    b.inc(ii, sT)
                @block.vector
                def _(v, half=half):
                    b.wait(v, sT)
                    for cl in range(8):
                        gc = tt*16 + half*8 + cl
                        for hh in range(2):
                            dst = ST3[hh][:, :].rearrange("p (h n c) -> p h n c", h=16, n=16)[:, :, :, gc]
                            src = ps_st[0:64, cl*512+hh*256:cl*512+(hh+1)*256].rearrange("p (h n) -> p h n", h=16)
                            ii = v.tensor_copy(dst, src)
                    b.inc(ii, sV)

        for nm in ("xcv", "x_dt_T", "B_T", "Bd_T"):
            b.free(nm)
        b.barrier()

    if dbg is not None and DBG == "st3":
        @block.gpsimd
        def _(g):
            b.dma(g, dbg[0:64, 0:8192], ST3[0][:], sD)
            b.dma(g, dbg[0:64, 8192:16384], ST3[1][:], sD)
            g.wait_ge(b.h(sD), b.val(sD))
        b.barrier()
        return

    eas = b.sb("eas", [32, 32], F32)
    dprA = b.sb("dprA", [32, 32], F32)
    scanpat = b.sb("scanpat", [64, 1024], F32)
    dpref = b.sb("dpref", [64, 1024], F32)
    spfull = [b.sb("spfulla", [64, 8192], F32, top=True), b.sb("spfullb", [64, 8192], F32, top=True)]
    SC = ST3

    @block.scalar
    def _(s):
        b.wait(s, sV)
        b.inc(s.activation(eas[:], asum[:], AF.Exp, bias=0.0, scale=1.0), sS)
    @block.vector
    def _(v):
        b.wait(v, sV)
        b.ch(v, v.memset(dprA[:, 0:1], 0.0), sV)
        b.ch(v, v.tensor_copy(dprA[:, 1:32], asum[:, 0:31]), sV)
        b.ch(v, v.tensor_tensor_scan(dprA[:], adt_gate[:, 1:33], dprA[:], 0.0, ALU.mult, ALU.add), sV)
    @block.scalar
    def _(s):
        b.wait(s, sV)
        b.inc(s.activation(dprA[:], dprA[:], AF.Exp, bias=0.0, scale=1.0), sS)
    @block.gpsimd
    def _(g):
        b.wait(g, sS)
        b.dma(g, hbm["flat"][0:1, 0:1024], eas[:], sD)
        b.dma(g, hbm["flat"][0:1, 1024:2048], dprA[:], sD)
        g.wait_ge(b.h(sD), b.val(sD))
        b.dma(g, scanpat[0:1, :], hbm["flat"][0:1, 0:1024], sD)
        b.dma(g, dpref[0:1, :], hbm["flat"][0:1, 1024:2048], sD)
    with nc.psum_tensor("ps_bcg", [128, 1024], F32) as ps_bcg:
        @block.tensor
        def _(t):
            b.wait(t, sD)
            t.matmul(ps_bcg[0:64, 0:512], ones_f32r[0:1, 0:64], scanpat[0:1, 0:512], start=True, stop=True)
            b.inc(t.matmul(ps_bcg[0:64, 512:1024], ones_f32r[0:1, 0:64], scanpat[0:1, 512:1024], start=True, stop=True), sT)
        @block.vector
        def _(v):
            b.wait(v, sT)
            b.inc(v.tensor_copy(scanpat[:, :], ps_bcg[0:64, :]), sV)
        @block.tensor
        def _(t):
            b.wait(t, sV)
            t.matmul(ps_bcg[0:64, 0:512], ones_f32r[0:1, 0:64], dpref[0:1, 0:512], start=True, stop=True)
            b.inc(t.matmul(ps_bcg[0:64, 512:1024], ones_f32r[0:1, 0:64], dpref[0:1, 512:1024], start=True, stop=True), sT)
        @block.vector
        def _(v):
            b.wait(v, sT)
            b.ch(v, v.tensor_copy(dpref[:, :], ps_bcg[0:64, :]), sV)
            zc = scanpat[:, :].rearrange("p (h c) -> p h c", h=32)[:, :, 0:1]
            b.ch(v, v.memset(zc, 0.0), sV)
            for hh in range(2):
                o4 = spfull[hh][:, :].rearrange("p (h n c) -> p h n c", h=16, n=16)
                i4 = scanpat[:, :].rearrange("p (h c) -> p h c", h=32)[:, hh*16:(hh+1)*16, :] \
                    .unsqueeze(2).broadcast_to([64, 16, 16, 32])
                b.ch(v, v.tensor_copy(o4, i4), sV)
                ii = v.tensor_tensor_scan(SC[hh][:], spfull[hh][:], ST3[hh][:], 0.0, ALU.mult, ALU.add)
                b.ch(v, ii, sV)

    cstage = b.sb("cstage", [64, 512], F32)
    allg = b.sb("allg", [64, 2, 512], F32)
    @block.vector
    def _(v):
        b.wait(v, sV)
        for hh in range(2):
            ii = v.tensor_copy(cstage[:, hh*256:(hh+1)*256].rearrange("p (h n) -> p h n", h=16),
                               SC[hh][:, :].rearrange("p (h n c) -> p h n c", h=16, n=16)[:, :, :, 31])
        b.inc(ii, sV)
    @block.sync
    def _(sy):
        b.wait(sy, sV)
        b.dma(sy, hbm["cc_min"], cstage[:], sDso)
    @block.gpsimd
    def _(g):
        b.wait(g, sDso)
        b.inc(g.collective_compute("AllGather", ALU.bypass, replica_groups=[list(range(NCC))],
                                   ins=[hbm["cc_min"]], outs=[hbm["cc_mout"]]), sCC)
    @block.vector
    def _(v):
        b.ch(v, v.memset(state_in[:], 0.0), sV)
    for r in range(NCC):
        @block.gpsimd
        def _(g, r=r):
            if r == 0:
                b.wait(g, sCC)
            else:
                b.wait(g, sV)
            b.dma(g, allg[:, r % 2, :], hbm["cc_mout"][r*64:(r+1)*64, :], sD)
        @block.vector
        def _(v, r=r):
            b.wait(v, sD)
            b.ch(v, v.scalar_tensor_tensor(state_in[:], allg[:, r % 2, :], m8_64[:, r:r+1],
                                           state_in[:], ALU.mult, ALU.add), sV)

    SCP = spfull
    SCP_bf = [None, None]
    for hh in range(2):
        @block.vector
        def _(v, hh=hh):
            b.wait(v, sV)
            v4 = SCP[hh][:, :].rearrange("p (h n c) -> p h n c", h=16, n=16)
            s4 = SC[hh][:, :].rearrange("p (h n c) -> p h n c", h=16, n=16)
            b.ch(v, v.memset(v4[:, :, :, 0:1], 0.0), sV)
            b.ch(v, v.tensor_copy(v4[:, :, :, 1:32], s4[:, :, :, 0:31]), sV)
            i0 = state_in[:, hh*256:(hh+1)*256].rearrange("p (h n) -> p h n", h=16) \
                .unsqueeze(3).broadcast_to([64, 16, 16, 32])
            i1 = dpref[:, :].rearrange("p (h c) -> p h c", h=32)[:, hh*16:(hh+1)*16, :] \
                .unsqueeze(2).broadcast_to([64, 16, 16, 32])
            b.ch(v, v.tensor_tensor(s4, i0, i1, ALU.mult), sV)
            b.ch(v, v.tensor_add(SCP[hh][:], SCP[hh][:], SC[hh][:]), sV)
        b.free("ST3a" if hh == 0 else "ST3b")
        SCP_bf[hh] = b.sb("SCP_bfa" if hh == 0 else "SCP_bfb", [64, 8192], BF16, top=True)
        @block.vector
        def _(v, hh=hh):
            b.ch(v, v.tensor_copy(SCP_bf[hh][:], SCP[hh][:]), sV)

    if dbg is not None and DBG == "scp":
        @block.gpsimd
        def _(g):
            b.wait(g, sV)
            b.dma(g, dbg[0:64, 0:8192], SCP[0][:], sD)
            b.dma(g, dbg[0:64, 8192:16384], SCP[1][:], sD)
            g.wait_ge(b.h(sD), b.val(sD))
        b.barrier()
        return

    for nm in ("spfulla", "spfullb", "scanpat", "dpref", "cstage", "allg", "eas", "dprA", "adt_gate"):
        b.free(nm)
    b.barrier()

    z_both = b.sb("z_both", [128, 32, TT], BF16)
    for tz in range(2):
        wb = [b.sb(f"wbZ{i}", [128, 8, 128], BF16) for i in range(2)]
        tcounts = {}
        loop_t0 = b.val(sT)
        with nc.psum_tensor(f"ps_zz{tz}", [128, 1024], F32) as ps_z:
            for m in range(16):
                buf = wb[m % 2]
                @block.gpsimd
                def _(g, m=m, buf=buf, tz=tz):
                    g.wait_ge(b.h(sT), tcounts[m - 2] if m >= 2 else loop_t0)
                    if m < 2:
                        b.wait(g, sV)
                    b.dma(g, buf[:], io["wins"][m].rearrange("(k p) j -> p k j", p=128), sDw[m % 2])
                @block.tensor
                def _(t, m=m, buf=buf, tz=tz):
                    b.wait(t, sDw[m % 2])
                    b.wait(t, sS)
                    o_t = HALO + tz * TT
                    for k in range(8):
                        t.matmul(ps_z[:, 0:512], buf[:, k, :], x_ln[:, k, o_t:o_t+512],
                                 start=(k == 0), stop=(k == 7))
                        ii = t.matmul(ps_z[:, 512:1024], buf[:, k, :], x_ln[:, k, o_t+512:o_t+1024],
                                      start=(k == 0), stop=(k == 7))
                    tcounts[m] = b.inc(ii, sT)
                @block.scalar
                def _(s, m=m, tz=tz):
                    b.wait(s, sT)
                    b.inc(s.activation(z_both[:, tz*16 + m, :], ps_z[:], AF.Silu, bias=0.0, scale=1.0), sS)
        b.free("wbZ0"); b.free("wbZ1")
    b.barrier()
    b.free("x_ln")

    for tt in range(2):
        y_bf = b.sb("y_bf", [128, 16, TT], BF16)
        wb = [b.sb(f"wbB{i}", [128, 8, 128], BF16) for i in range(2)]
        xp = b.sb("xp", [128, 17, 128], BF16)
        xdtp = b.sb("xdtp", [128, 4096], BF16)
        c0tmp = b.sb("c0tmp", [128, 16], BF16)
        C0p = b.sb("C0p", [16, 128], BF16)
        R1E = b.sb("R1E", [128, 2048], F32)
        Gt = b.sb("Gt", [128, 2048], BF16)
        ssm = b.sb("ssm", [128, 64], F32)
        smt = b.sb("smt", [64, 128], F32)
        Cdp = b.sb("Cdp", [16, 4096], BF16)
        ebcp = b.sb("ebcp", [16, 4096], BF16)
        stTp = b.sb("stTp", [16, 4096], BF16)

        @block.vector
        def _(v):
            b.ch(v, v.memset(xdtp[:], 0.0), sV)
            b.ch(v, v.memset(stTp[:], 0.0), sV)

        eacs = b.sb("eacs", [32, TT], BF16)
        @block.scalar
        def _(s):
            b.wait(s, sV)
            b.inc(s.activation(eacs[:], acs2[:, tt, :], AF.Exp, bias=0.0, scale=1.0), sS)
        @block.gpsimd
        def _(g):
            b.wait(g, sS)
            b.dma(g, hbm["acsf"][0:1, :].rearrange("a (c h l) -> (a h) c l", c=16, h=32),
                  acs2[:, tt, :].rearrange("p (c l) -> p c l", c=16), sD)
            b.dma(g, hbm["acsf2"][0:1, :].rearrange("a (c h l) -> (a h) c l", c=16, h=32),
                  eacs[:, :].rearrange("p (c l) -> p c l", c=16), sD)
            g.wait_ge(b.h(sD), b.val(sD))
            b.inc(g.nop(), sG)
        b.free("eacs")

        with nc.psum_tensor(f"ps_sc{tt}", [64, 128], F32) as ps_sc, \
             nc.psum_tensor(f"ps_trp{tt}", [128, 512], BF16) as ps_trp, \
             nc.psum_tensor(f"ps_y{tt}", [128, 2048], F32) as ps_y, \
             nc.psum_tensor(f"ps_sT{tt}", [16, 512], BF16) as ps_sT:
            for cp in range(8):
                @block.gpsimd
                def _(g, cp=cp):
                    b.wait(g, sV)
                    b.wait(g, sG)
                    b.wait(g, sD)
                    b.dma(g, xp[:], hbm["xcv"][:, :, tt*TT + cp*128: tt*TT + (cp+1)*128], sD)
                    b.dma(g, ebcp[:, 0:2048], hbm["acsf2"][0:1, (cp*2)*2048:(cp*2+1)*2048].broadcast_to([16, 2048]), sD)
                    b.dma(g, ebcp[:, 2048:4096], hbm["acsf2"][0:1, (cp*2+1)*2048:(cp*2+2)*2048].broadcast_to([16, 2048]), sD)
                    b.dma(g, R1E[0:64, :], hbm["acsf"][0:1, (cp*2)*2048:(cp*2+1)*2048].broadcast_to([64, 2048]), sD)
                    b.dma(g, R1E[64:128, :], hbm["acsf"][0:1, (cp*2+1)*2048:(cp*2+2)*2048].broadcast_to([64, 2048]), sD)
                for kg in range(4):
                    @block.tensor
                    def _(t, cp=cp, kg=kg):
                        b.wait(t, sD)
                        b.wait(t, sV)
                        for j in range(4):
                            ii = t.transpose(ps_trp[:, j*128:(j+1)*128], xp[:, kg*4+j, :], id_bf[:, :])
                        b.inc(ii, sT)
                    @block.vector
                    def _(v, cp=cp, kg=kg):
                        b.wait(v, sT)
                        base = kg * 1024
                        for h2 in range(2):
                            o3 = xdtp[:, base:base+1024].rearrange("p (a q) -> p a q", a=4)[:, :, h2*192:h2*192+64]
                            i3 = ps_trp[:, :].rearrange("p (a q) -> p a q", a=4)[:, :, h2*64:(h2+1)*64]
                            d3 = dtT[:, tt*8+cp, kg*8 + h2: kg*8 + 8: 2].unsqueeze(2).broadcast_to([128, 4, 64])
                            ii = v.tensor_tensor(o3, i3, d3, ALU.mult)
                        b.inc(ii, sV)
                @block.tensor
                def _(t, cp=cp):
                    b.wait(t, sV)
                    b.inc(t.transpose(ps_trp[:, 0:16], xp[32:48, 16, :], id_bf[32:48, 32:48]), sT)
                @block.vector
                def _(v, cp=cp):
                    b.wait(v, sT)
                    b.inc(v.tensor_copy(c0tmp[:], ps_trp[:, 0:16]), sV)
                @block.tensor
                def _(t, cp=cp):
                    b.wait(t, sV)
                    b.inc(t.transpose(ps_trp[0:16, 128:256], c0tmp[:], id_bf[:, :]), sT)
                @block.vector
                def _(v, cp=cp):
                    b.wait(v, sT)
                    b.inc(v.tensor_copy(C0p[:], ps_trp[0:16, 128:256]), sV)
                @block.tensor
                def _(t, cp=cp):
                    b.wait(t, sV)
                    b.wait(t, sG)
                    for ci in range(2):
                        ii = t.matmul(ps_sc[0:64, ci*64:(ci+1)*64],
                                      xp[0:16, 16, ci*64:(ci+1)*64],
                                      C0p[0:16, ci*64:(ci+1)*64],
                                      start=True, stop=True)
                    b.inc(ii, sT)
                @block.vector
                def _(v, cp=cp):
                    b.wait(v, sT)
                    b.inc(v.tensor_tensor(smt[:], ps_sc[0:64, :], maskST[:], ALU.mult), sV)
                @block.gpsimd
                def _(g, cp=cp):
                    b.wait(g, sV)
                    b.dma(g, hbm["flat"][0:1, 8192:16384].rearrange("a (s l) -> (a s) l", s=64), smt[:], sD)
                    g.wait_ge(b.h(sD), b.val(sD))
                    b.dma(g, ssm[:],
                          hbm["flat"][0:1, 8192:16384].rearrange("a (s c l) -> a s c l", s=64, c=2)
                          .transpose([0, 2, 1, 3]).squeeze(0), sD)
                    g.wait_ge(b.h(sD), b.val(sD))
                    b.inc(g.nop(), sG)
                @block.gpsimd
                def _(g, cp=cp):
                    b.wait(g, sD)
                    b.wait(g, sV)
                    o3 = Cdp[:, :].rearrange("p (ci h l) -> p ci h l", ci=2, h=32)
                    i0 = C0p[:, :].rearrange("p (ci l) -> p ci l", ci=2).unsqueeze(2).broadcast_to([16, 2, 32, 64])
                    i1 = ebcp[:, :].rearrange("p (ci h l) -> p ci h l", ci=2, h=32)
                    b.inc(g.tensor_tensor(o3, i0, i1, ALU.mult), sG)
                @block.vector
                def _(v, cp=cp):
                    b.wait(v, sD)
                    b.wait(v, sV)
                    i1 = acsT[:, tt*8+cp, :].unsqueeze(2).broadcast_to([128, 32, 64])
                    b.inc(v.tensor_tensor(R1E[:, :].rearrange("p (h l) -> p h l", h=32),
                                          R1E[:, :].rearrange("p (h l) -> p h l", h=32), i1,
                                          ALU.subtract), sV)
                @block.gpsimd
                def _(g, cp=cp):
                    b.wait(g, sV)
                    b.inc(g.tensor_scalar_min(R1E[:], R1E[:], 0.0), sG)
                @block.scalar
                def _(s, cp=cp):
                    b.wait(s, sG)
                    b.inc(s.activation(R1E[:], R1E[:], AF.Exp, bias=0.0, scale=1.0), sS)
                @block.vector
                def _(v, cp=cp):
                    b.wait(v, sS)
                    b.wait(v, sG)
                    i1 = ssm[:, :].unsqueeze(1).broadcast_to([128, 32, 64])
                    b.inc(v.tensor_tensor(Gt[:, :].rearrange("p (h l) -> p h l", h=32),
                                          R1E[:, :].rearrange("p (h l) -> p h l", h=32), i1,
                                          ALU.mult), sV)
                for ci in range(2):
                    gc = tt*16 + cp*2 + ci
                    for hg in range(4):
                        @block.tensor
                        def _(t, gc=gc, hg=hg):
                            b.wait(t, sV)
                            for hh in range(8):
                                h = hg*8 + hh
                                src = SCP_bf[h // 16][:, :].rearrange("p (h n c) -> p h n c", h=16, n=16)[:, h % 16, :, gc]
                                ii = t.transpose(ps_sT[0:16, hh*64:(hh+1)*64], src, id_bf[0:64, 0:64])
                            b.inc(ii, sT)
                        @block.vector
                        def _(v, hg=hg):
                            b.wait(v, sT)
                            for h2 in range(2):
                                dsl = stTp[0:16, hg*1024:(hg+1)*1024] \
                                    .rearrange("p (hp q) -> p hp q", hp=4)[:, :, h2*192:h2*192+64]
                                ssl = ps_sT[0:16, :].rearrange("p (hp q) -> p hp q", hp=4)[:, :, h2*64:(h2+1)*64]
                                ii = v.tensor_copy(dsl, ssl)
                            b.inc(ii, sV)
                    @block.tensor
                    def _(t, cp=cp, ci=ci):
                        b.wait(t, sV)
                        b.wait(t, sG)
                        for hp in range(16):
                            co = ci*1024 + hp*64
                            for h2 in range(2):
                                h = 2*hp + h2
                                t.matmul(ps_y[:, co:co+64],
                                         xdtp[ci*64:(ci+1)*64, hp*256 + h2*128: hp*256 + (h2+1)*128],
                                         Gt[ci*64:(ci+1)*64, h*64:(h+1)*64],
                                         start=(h2 == 0), stop=False)
                                ii = t.matmul(ps_y[:, co:co+64],
                                              stTp[0:16, hp*256 + h2*128: hp*256 + (h2+1)*128],
                                              Cdp[0:16, ci*2048 + h*64: ci*2048 + (h+1)*64],
                                              start=False, stop=(h2 == 1))
                        b.inc(ii, sT)
                @block.vector
                def _(v, cp=cp):
                    b.wait(v, sT)
                    for hp in range(16):
                        src = ps_y[:, :].rearrange("p (ci q l) -> p ci q l", ci=2, q=16)[:, :, hp, :]
                        ysl = y_bf[:, hp, cp*128:(cp+1)*128].rearrange("p (ci l) -> p ci l", ci=2)
                        xsl = xp[:, hp, :].rearrange("p (ci l) -> p ci l", ci=2)
                        zsl = z_both[:, tt*16 + hp, cp*128:(cp+1)*128].rearrange("p (ci l) -> p ci l", ci=2)
                        b.ch(v, v.scalar_tensor_tensor(ysl, xsl, D0[:, hp:hp+1], src, ALU.mult, ALU.add), sV)
                        b.ch(v, v.tensor_tensor(ysl, ysl, zsl, ALU.mult), sV)

        if dbg is not None and DBG == "pair0" and tt == 0:
            @block.gpsimd
            def _(g):
                b.wait(g, sV)
                b.dma(g, dbg[:, 0:2048], R1E[:], sD)
                b.dma(g, dbg[:, 2048:4096], Gt[:], sD)
                b.dma(g, dbg[0:16, 4096:8192], Cdp[:], sD)
                b.dma(g, dbg[0:16, 8192:12288], stTp[:], sD)
                b.dma(g, dbg[:, 12288:14336], xdtp[:], sD)
                b.dma(g, dbg[:, 14336:14400], ssm[:], sD)
                b.dma(g, dbg[0:16, 14400:14528], C0p[:], sD)
                for hp in range(4):
                    b.dma(g, dbg[:, 14528 + hp*128: 14528 + (hp+1)*128], y_bf[:, hp, 0:128], sD)
                b.dma(g, dbg[:, 15040: 15040 + 17*128], xp[:], sD)
                b.dma(g, dbg[0:32, 17216:17216+1024], acs2[:, 0, :], sD)
                g.wait_ge(b.h(sD), b.val(sD))
            b.barrier()
            return

        if dbg is not None and DBG == "y" and tt == 0:
            @block.gpsimd
            def _(g):
                b.wait(g, sV)
                for kc in range(16):
                    b.dma(g, dbg[:, kc*1024:(kc+1)*1024], y_bf[:, kc, :], sD)
                g.wait_ge(b.h(sD), b.val(sD))
            b.barrier()
            return

        for nm in ("xp", "xdtp", "c0tmp", "C0p", "R1E", "Gt", "ssm", "smt",
                   "Cdp", "ebcp", "stTp"):
            b.free(nm)
        sqb = b.sb("sqbB", [128, 16, TT], BF16)
        rrow = b.sb("rrowB", [1, TT], F32)
        rmsb = b.sb("rmsb", [128, TT], F32)
        @block.scalar
        def _(s):
            b.wait(s, sV)
            for kc in range(16):
                ii = s.activation(sqb[:, kc, :], y_bf[:, kc, :], AF.Square, bias=0.0, scale=1.0)
            b.inc(ii, sS)
        with nc.psum_tensor(f"ps_r{tt}", [1, 1024], F32) as ps_r, \
             nc.psum_tensor(f"ps_rb{tt}", [128, 1024], F32) as ps_rb:
            @block.tensor
            def _(t):
                b.wait(t, sS)
                for (o, n) in ((0, 512), (512, 512)):
                    for kc in range(16):
                        ii = t.matmul(ps_r[0:1, o:o+n], ones_bf[:, 0:1], sqb[:, kc, o:o+n],
                                      start=(kc == 0), stop=(kc == 15))
                b.inc(ii, sT)
            @block.vector
            def _(v):
                b.wait(v, sT)
                b.ch(v, v.tensor_copy(rrow[:], ps_r[:]), sV)
                b.ch(v, v.tensor_scalar_mul(rrow[:], rrow[:], 1.0 / DI), sV)
                b.inc(v.tensor_scalar_add(rrow[:], rrow[:], 1e-5), sV)
            @block.scalar
            def _(s):
                b.wait(s, sV)
                b.inc(s.activation(rrow[:], rrow[:], AF.Sqrt, bias=0.0, scale=1.0), sS)
            @block.vector
            def _(v):
                b.wait(v, sS)
                b.inc(v.reciprocal(rrow[:], rrow[:]), sV)
            @block.tensor
            def _(t):
                b.wait(t, sV)
                t.matmul(ps_rb[:, 0:512], ones_f32r[0:1, :], rrow[0:1, 0:512], start=True, stop=True)
                b.inc(t.matmul(ps_rb[:, 512:1024], ones_f32r[0:1, :], rrow[0:1, 512:1024], start=True, stop=True), sT)
            @block.vector
            def _(v):
                b.wait(v, sT)
                b.ch(v, v.tensor_copy(rmsb[:], ps_rb[:]), sV)
                for kc in range(16):
                    b.ch(v, v.scalar_tensor_tensor(y_bf[:, kc, :], y_bf[:, kc, :],
                                                   norm_w[:, kc:kc+1], rmsb[:], ALU.mult, ALU.mult), sV)

        for nm in ("sqbB", "rrowB", "rmsb"):
            b.free(nm)
        u_t = b.sb("u_tB", [128, 8, TT], F32)
        x1sb = b.sb("x1sb", [128, TT], F32)
        wo = [b.sb(f"woB{i}", [128, 16, 128], BF16) for i in range(2)]
        @block.sync
        def _(sy):
            b.wait(sy, sV)
            b.wait(sy, sS)
            b.dma(sy, u_t[:], io["u_fm"][:, :, HALO + tt*TT: HALO + (tt+1)*TT], sDs)
        tcounts = {}
        loop_t0 = b.val(sT)
        with nc.psum_tensor(f"ps_wo{tt}", [128, 1024], F32) as ps_wo:
            for m in range(8):
                buf = wo[m % 2]
                @block.gpsimd
                def _(g, m=m, buf=buf):
                    g.wait_ge(b.h(sT), tcounts[m - 2] if m >= 2 else loop_t0)
                    if m < 2:
                        b.wait(g, sV)
                    b.dma(g, buf[:], io["wouts"][m].rearrange("(k p) j -> p k j", p=128), sDw[m % 2])
                @block.tensor
                def _(t, m=m, buf=buf):
                    b.wait(t, sDw[m % 2])
                    if m >= 1:
                        b.wait(t, sV)
                    for k in range(16):
                        t.matmul(ps_wo[:, 0:512], buf[:, k, :], y_bf[:, k, 0:512],
                                 start=(k == 0), stop=(k == 15))
                        ii = t.matmul(ps_wo[:, 512:1024], buf[:, k, :], y_bf[:, k, 512:1024],
                                      start=(k == 0), stop=(k == 15))
                    tcounts[m] = b.inc(ii, sT)
                @block.vector
                def _(v, m=m):
                    b.wait(v, sT)
                    b.wait(v, sDs)
                    b.wait(v, sDso)
                    b.inc(v.tensor_add(x1sb[:], ps_wo[:], u_t[:, m, :]), sV)
                @block.sync
                def _(sy, m=m):
                    b.wait(sy, sV)
                    b.dma(sy, hbm["x1"][:, m, tt*TT:(tt+1)*TT], x1sb[:], sDso)

        @block.sync
        def _(sy):
            b.wait(sy, sDso)
        for nm in ("y_bf", "wbB0", "wbB1", "u_tB", "x1sb", "woB0", "woB1"):
            b.free(nm)
        b.barrier()

    for nm in ("acs2", "acsT", "dtT", "dexT", "asum", "state_in", "SCP_bfa", "SCP_bfb", "z_both"):
        b.free(nm)
    b.barrier()

def emit_ffn0(b, io, hbm, C, lns):
    nc, block = b.nc, b.block
    sD, sDs, sT, sV, sS, sG, sCC = C["sems"]
    sDw = C["sDw"]; sDso = C["sDso"]

    s1c = b.sb("s1c", [128, 32], F32)
    s2c = b.sb("s2c", [128, 32], F32)
    sh1c = b.sb("sh1c", [128, 32], F32)
    sh2c = b.sb("sh2c", [128, 32], F32)
    s3c = b.sb("s3c", [128, 8], F32)

    @block.gpsimd
    def _(g):
        for dst, src in ((s1c, "s1"), (s2c, "s2"), (sh1c, "sh1"), (sh2c, "sh2"), (s3c, "s3")):
            b.dma(g, dst[:], io[src], sD)
        g.wait_ge(b.h(sD), b.val(sD))
        b.inc(g.nop(), sG)
    b.barrier()

    for tt in range(2):
        x_ln2 = b.sb("f0xln", [128, 8, TT], BF16)
        emit_ln(b, hbm["x1"][:, :, tt*TT:(tt+1)*TT], x_ln2, TT, lns, f"lf0{tt}")

        hsb = b.sb("f0h", [128, 32, TT], BF16)
        wA = [b.sb(f"f0wa{i}", [128, 8, 128], BF16) for i in range(2)]
        wB = [b.sb(f"f0wb{i}", [128, 8, 128], BF16) for i in range(2)]
        sa_t = b.sb("f0sa", [128, 2, TT], F32)
        tcounts = {}; vcounts = {}; scounts = {}
        t0_ = b.val(sT)
        with nc.psum_tensor(f"f0psA{tt}", [128, 2048], F32) as psA, \
             nc.psum_tensor(f"f0psB{tt}", [128, 2048], F32) as psB:
            for m in range(32):
                mh = m % 2
                @block.gpsimd
                def _(g, m=m, mh=mh):
                    g.wait_ge(b.h(sT), tcounts[m-2] if m >= 2 else t0_)
                    b.dma(g, wA[mh][:], io["w1s"][m].rearrange("(k p) j -> p k j", p=128), sDw[mh])
                    b.dma(g, wB[mh][:], io["w2s"][m].rearrange("(k p) j -> p k j", p=128), sDw[mh])
                @block.tensor
                def _(t, m=m, mh=mh):
                    b.wait(t, sDw[mh])
                    if m >= 2:
                        t.wait_ge(b.h(sS), scounts[m-2])
                        t.wait_ge(b.h(sV), vcounts[m-2])
                    for th in range(2):
                        o = mh*1024 + th*512
                        for k in range(8):
                            t.matmul(psA[:, o:o+512], wA[mh][:, k, :],
                                     x_ln2[:, k, th*512:(th+1)*512], start=(k == 0), stop=(k == 7))
                    for th in range(2):
                        o = mh*1024 + th*512
                        for k in range(8):
                            ii = t.matmul(psB[:, o:o+512], wB[mh][:, k, :],
                                          x_ln2[:, k, th*512:(th+1)*512], start=(k == 0), stop=(k == 7))
                    tcounts[m] = b.inc(ii, sT)
                @block.scalar
                def _(s, m=m, mh=mh):
                    b.wait(s, sT)
                    if m >= 2:
                        s.wait_ge(b.h(sV), vcounts[m-2])
                    scounts[m] = b.inc(s.activation(sa_t[:, mh, :], psA[:, mh*1024:(mh+1)*1024],
                                                    AF.Silu, bias=0.0, scale=s1c[:, m:m+1]), sS)
                @block.vector
                def _(v, m=m, mh=mh):
                    b.wait(v, sS)
                    vcounts[m] = b.inc(v.scalar_tensor_tensor(hsb[:, m, :], psB[:, mh*1024:(mh+1)*1024],
                                                              s2c[:, m:m+1], sa_t[:, mh, :],
                                                              ALU.mult, ALU.mult), sV)
        b.barrier()
        for nm in ("f0xln", "f0wa0", "f0wa1", "f0wb0", "f0wb1", "f0sa"):
            b.free(nm)

        prev = hsb
        for (ph, wkey, sc) in (("B", "h1s", sh1c), ("C", "h2s", sh2c)):
            gout = b.sb(f"f0g{ph}", [128, 32, TT], BF16)
            wH = [b.sb(f"f0wh{ph}{i}", [128, 32, 128], BF16) for i in range(2)]
            tcounts = {}; scounts = {}
            t0_ = b.val(sT)
            with nc.psum_tensor(f"f0psg{ph}{tt}", [128, 2048], F32) as psG:
                for m in range(32):
                    mh = m % 2
                    @block.gpsimd
                    def _(g, m=m, mh=mh, wkey=wkey):
                        g.wait_ge(b.h(sT), tcounts[m-2] if m >= 2 else t0_)
                        b.dma(g, wH[mh][:], io[wkey][m].rearrange("(k p) j -> p k j", p=128), sDw[mh])
                    @block.tensor
                    def _(t, m=m, mh=mh, prev=prev):
                        b.wait(t, sDw[mh])
                        if m >= 2:
                            t.wait_ge(b.h(sS), scounts[m-2])
                        for th in range(2):
                            o = mh*1024 + th*512
                            for k in range(32):
                                ii = t.matmul(psG[:, o:o+512], wH[mh][:, k, :],
                                              prev[:, k, th*512:(th+1)*512], start=(k == 0), stop=(k == 31))
                        tcounts[m] = b.inc(ii, sT)
                    @block.scalar
                    def _(s, m=m, mh=mh, sc=sc, gout=gout):
                        b.wait(s, sT)
                        scounts[m] = b.inc(s.activation(gout[:, m, :], psG[:, mh*1024:(mh+1)*1024],
                                                        AF.Gelu, bias=0.0, scale=sc[:, m:m+1]), sS)
            b.barrier()
            b.free("f0h" if ph == "B" else "f0gB")
            b.free(f"f0wh{ph}0"); b.free(f"f0wh{ph}1")
            prev = gout

        g2 = prev
        x1t = b.sb("f0x1t", [128, 8, TT], F32)
        x1o = [b.sb(f"f0x1o{i}", [128, TT], F32) for i in range(2)]
        wO = [b.sb(f"f0wo{i}", [128, 32, 128], BF16) for i in range(2)]
        @block.sync
        def _(sy):
            b.dma(sy, x1t[:], hbm["x1"][:, :, tt*TT:(tt+1)*TT], sDs)
        tcounts = {}; vcounts = {}; socounts = {}
        t0_ = b.val(sT)
        with nc.psum_tensor(f"f0psO{tt}", [128, 2048], F32) as psO:
            for m in range(8):
                mh = m % 2
                @block.gpsimd
                def _(g, m=m, mh=mh):
                    g.wait_ge(b.h(sT), tcounts[m-2] if m >= 2 else t0_)
                    b.dma(g, wO[mh][:], io["w3s"][m].rearrange("(k p) j -> p k j", p=128), sDw[mh])
                @block.tensor
                def _(t, m=m, mh=mh):
                    b.wait(t, sDw[mh])
                    if m >= 2:
                        t.wait_ge(b.h(sV), vcounts[m-2])
                    for th in range(2):
                        o = mh*1024 + th*512
                        for k in range(32):
                            ii = t.matmul(psO[:, o:o+512], wO[mh][:, k, :],
                                          g2[:, k, th*512:(th+1)*512], start=(k == 0), stop=(k == 31))
                    tcounts[m] = b.inc(ii, sT)
                @block.vector
                def _(v, m=m, mh=mh):
                    b.wait(v, sT)
                    b.wait(v, sDs)
                    if m >= 2:
                        b.wait(v, sDso, socounts[m-2])
                    vcounts[m] = b.inc(v.scalar_tensor_tensor(x1o[mh][:], psO[:, mh*1024:(mh+1)*1024],
                                                              s3c[:, m:m+1], x1t[:, m, :],
                                                              ALU.mult, ALU.add), sV)
                @block.sync
                def _(sy, m=m, mh=mh):
                    b.wait(sy, sV)
                    if m >= 1:
                        sy.wait_ge(b.h(sDso), socounts[m-1])
                    socounts[m] = b.dma(sy, hbm["x1"][:, m, tt*TT:(tt+1)*TT], x1o[mh][:], sDso)
        @block.sync
        def _(sy):
            b.wait(sy, sDso)
        b.barrier()
        for nm in ("f0gC", "f0x1t", "f0x1o0", "f0x1o1", "f0wo0", "f0wo1"):
            b.free(nm)
    for nm in ("s1c", "s2c", "sh1c", "sh2c", "s3c"):
        b.free(nm)
    b.barrier()


def emit_lru(b, io, hbm, C, lns):
    nc, block = b.nc, b.block
    sD, sDs, sT, sV, sS, sG, sCC = C["sems"]
    sDso = C["sDso"]
    l1A = C["l1A"]; l1B = C["l1B"]; l1C = C["l1C"]; l1D = C["l1D"]
    m8_128 = C["m8_128"]

    NZL = T // TC1
    u = b.sb("lru_u", [128, 8, T], F32)
    emit_ln(b, hbm["x1"], u, T, lns, "lnlru", u_buf=u)

    Apr = b.sb("lApr", [128, 18, 8], F32); Api = b.sb("lApi", [128, 18, 8], F32)
    krt = b.sb("lkr", [128, 8, 8], F32)
    Wrt = b.sb("lWr", [128, 8, 8], F32); Wit = b.sb("lWi", [128, 8, 8], F32)
    Prt = b.sb("lPr", [128, 8, 8], F32); Pnt = b.sb("lPn", [128, 8, 8], F32)
    D1r = b.sb("lD1r", [128, 17, 8], F32); D1i = b.sb("lD1i", [128, 17, 8], F32)
    D1n = b.sb("lD1n", [128, 17, 8], F32)
    D2qr = b.sb("lD2qr", [128, 16, 8], F32); D2qi = b.sb("lD2qi", [128, 16, 8], F32)
    cr = b.sb("lcr", [128, 17, 8], F32); ci_ = b.sb("lci", [128, 17, 8], F32)
    t8 = b.sb("lt8", [128, 4, 8], F32)
    jnk = b.sb("ljnk", [128, 8], F32)
    CBr = b.sb("lCBr", [128, 8], F32); CBi = b.sb("lCBi", [128, 8], F32)
    hin = b.sb("lhin", [128, 16], F32)
    Ar = l1A[:, 0:8]; Ai = l1A[:, 8:16]
    Br = l1B[:, 0:8]; Bi = l1B[:, 8:16]
    Cr = l1C[:, 0:8]; Ci = l1C[:, 8:16]

    def cmulv(v, outr, outi, ar, ai, br, bi):
        b.ch(v, v.tensor_mul(t8[:, 0, :], ar, br), sV)
        b.ch(v, v.tensor_mul(t8[:, 1, :], ai, bi), sV)
        b.ch(v, v.tensor_mul(t8[:, 2, :], ar, bi), sV)
        b.ch(v, v.tensor_mul(t8[:, 3, :], ai, br), sV)
        b.ch(v, v.tensor_sub(outr, t8[:, 0, :], t8[:, 1, :]), sV)
        b.ch(v, v.tensor_add(outi, t8[:, 2, :], t8[:, 3, :]), sV)

    @block.vector
    def _(v):
        b.ch(v, v.memset(Apr[:, 0, :], 1.0), sV)
        b.ch(v, v.memset(Api[:, 0, :], 0.0), sV)
        for k in range(1, 18):
            cmulv(v, Apr[:, k, :], Api[:, k, :], Apr[:, k-1, :], Api[:, k-1, :], Ar, Ai)
        cmulv(v, CBr[:], CBi[:], Cr, Ci, Br, Bi)
        for l in range(TC1):
            cmulv(v, krt[:, l, :], jnk[:], CBr[:], CBi[:], Apr[:, l, :], Api[:, l, :])
        for j in range(TC1):
            cmulv(v, Wrt[:, j, :], Wit[:, j, :], Apr[:, TC1-1-j, :], Api[:, TC1-1-j, :], Br, Bi)
        for l in range(TC1):
            cmulv(v, Prt[:, l, :], Pnt[:, l, :], Cr, Ci, Apr[:, l+1, :], Api[:, l+1, :])
        b.ch(v, v.tensor_scalar(Pnt[:], Pnt[:], -1.0, None, ALU.mult), sV)
        b.ch(v, v.memset(D1r[:, 0, :], 1.0), sV)
        b.ch(v, v.memset(D1i[:, 0, :], 0.0), sV)
        for k in range(1, 17):
            cmulv(v, D1r[:, k, :], D1i[:, k, :], D1r[:, k-1, :], D1i[:, k-1, :],
                  Apr[:, TC1, :], Api[:, TC1, :])
        b.ch(v, v.tensor_scalar(D1n[:], D1i[:], -1.0, None, ALU.mult), sV)
        b.ch(v, v.memset(D2qr[:, 0, :], 1.0), sV)
        b.ch(v, v.memset(D2qi[:, 0, :], 0.0), sV)
        for q in range(1, 16):
            cmulv(v, D2qr[:, q, :], D2qi[:, q, :], D2qr[:, q-1, :], D2qi[:, q-1, :],
                  D1r[:, 16, :], D1i[:, 16, :])

    yw = b.sb("lyw", [128, 8, T], F32)
    @block.vector
    def _(v):
        for kc in range(8):
            b.ch(v, v.tensor_scalar(yw[:, kc, :], u[:, kc, :], krt[:, 0, kc:kc+1], None, ALU.mult), sV)
        for l in range(1, TC1):
            for kc in range(8):
                o3 = yw[:, kc, :].rearrange("p (nz l8) -> p nz l8", l8=TC1)[:, :, l:TC1]
                i3 = u[:, kc, :].rearrange("p (nz l8) -> p nz l8", l8=TC1)[:, :, 0:TC1-l]
                b.ch(v, v.scalar_tensor_tensor(o3, i3, krt[:, l, kc:kc+1], o3, ALU.mult, ALU.add), sV)

    Sr = b.sb("lSr", [128, 8, NZL], F32); Si = b.sb("lSi", [128, 8, NZL], F32)
    @block.vector
    def _(v):
        for kc in range(8):
            u3 = u[:, kc, :].rearrange("p (nz l8) -> p nz l8", l8=TC1)
            b.ch(v, v.tensor_scalar(Sr[:, kc, :], u3[:, :, 0], Wrt[:, 0, kc:kc+1], None, ALU.mult), sV)
            b.ch(v, v.tensor_scalar(Si[:, kc, :], u3[:, :, 0], Wit[:, 0, kc:kc+1], None, ALU.mult), sV)
            for j in range(1, TC1):
                b.ch(v, v.scalar_tensor_tensor(Sr[:, kc, :], u3[:, :, j], Wrt[:, j, kc:kc+1],
                                               Sr[:, kc, :], ALU.mult, ALU.add), sV)
                b.ch(v, v.scalar_tensor_tensor(Si[:, kc, :], u3[:, :, j], Wit[:, j, kc:kc+1],
                                               Si[:, kc, :], ALU.mult, ALU.add), sV)

    hr = b.sb("lhr", [128, 8, NZL], F32); hi = b.sb("lhi", [128, 8, NZL], F32)
    @block.vector
    def _(v):
        b.ch(v, v.tensor_copy(hr[:], Sr[:]), sV)
        b.ch(v, v.tensor_copy(hi[:], Si[:]), sV)
        for j in range(1, TC2):
            for kc in range(8):
                S3r = Sr[:, kc, :].rearrange("p (nq c) -> p nq c", c=TC2)[:, :, 0:TC2-j]
                S3i = Si[:, kc, :].rearrange("p (nq c) -> p nq c", c=TC2)[:, :, 0:TC2-j]
                H3r = hr[:, kc, :].rearrange("p (nq c) -> p nq c", c=TC2)[:, :, j:TC2]
                H3i = hi[:, kc, :].rearrange("p (nq c) -> p nq c", c=TC2)[:, :, j:TC2]
                b.ch(v, v.scalar_tensor_tensor(H3r, S3r, D1r[:, j, kc:kc+1], H3r, ALU.mult, ALU.add), sV)
                b.ch(v, v.scalar_tensor_tensor(H3r, S3i, D1n[:, j, kc:kc+1], H3r, ALU.mult, ALU.add), sV)
                b.ch(v, v.scalar_tensor_tensor(H3i, S3i, D1r[:, j, kc:kc+1], H3i, ALU.mult, ALU.add), sV)
                b.ch(v, v.scalar_tensor_tensor(H3i, S3r, D1i[:, j, kc:kc+1], H3i, ALU.mult, ALU.add), sV)

    @block.vector
    def _(v):
        b.ch(v, v.memset(cr[:, 0, :], 0.0), sV)
        b.ch(v, v.memset(ci_[:, 0, :], 0.0), sV)
        for q in range(16):
            gr = hr[:, :, q*TC2 + TC2 - 1]
            gi = hi[:, :, q*TC2 + TC2 - 1]
            b.ch(v, v.tensor_mul(t8[:, 0, :], cr[:, q, :], D1r[:, 16, :]), sV)
            b.ch(v, v.tensor_mul(t8[:, 1, :], ci_[:, q, :], D1i[:, 16, :]), sV)
            b.ch(v, v.tensor_sub(t8[:, 0, :], t8[:, 0, :], t8[:, 1, :]), sV)
            b.ch(v, v.tensor_add(cr[:, q+1, :], t8[:, 0, :], gr), sV)
            b.ch(v, v.tensor_mul(t8[:, 2, :], cr[:, q, :], D1i[:, 16, :]), sV)
            b.ch(v, v.tensor_mul(t8[:, 3, :], ci_[:, q, :], D1r[:, 16, :]), sV)
            b.ch(v, v.tensor_add(t8[:, 2, :], t8[:, 2, :], t8[:, 3, :]), sV)
            b.ch(v, v.tensor_add(ci_[:, q+1, :], t8[:, 2, :], gi), sV)

    cst2 = b.sb("lcst", [128, 16], F32)
    alg2 = b.sb("lalg", [128, 2, 16], F32)
    @block.vector
    def _(v):
        b.ch(v, v.tensor_copy(cst2[:, 0:8], cr[:, 16, :]), sV)
        b.inc(v.tensor_copy(cst2[:, 8:16], ci_[:, 16, :]), sV)
    @block.sync
    def _(sy):
        b.wait(sy, sV)
        b.dma(sy, hbm["cc_lin"], cst2[:], sDso)
    @block.gpsimd
    def _(g):
        b.wait(g, sDso)
        b.inc(g.collective_compute("AllGather", ALU.bypass, replica_groups=[list(range(NCC))],
                                   ins=[hbm["cc_lin"]], outs=[hbm["cc_lout"]]), sCC)
    @block.vector
    def _(v):
        b.ch(v, v.memset(hin[:], 0.0), sV)
    for r in range(NCC):
        @block.gpsimd
        def _(g, r=r):
            if r == 0:
                b.wait(g, sCC)
            else:
                b.wait(g, sV)
            b.dma(g, alg2[:, r % 2, :], hbm["cc_lout"][r*128:(r+1)*128, :], sD)
        @block.vector
        def _(v, r=r):
            b.wait(v, sD)
            b.ch(v, v.scalar_tensor_tensor(hin[:], alg2[:, r % 2, :], m8_128[:, r:r+1],
                                           hin[:], ALU.mult, ALU.add), sV)

    @block.vector
    def _(v):
        for q in range(16):
            b.ch(v, v.tensor_mul(t8[:, 0, :], D2qr[:, q, :], hin[:, 0:8]), sV)
            b.ch(v, v.tensor_mul(t8[:, 1, :], D2qi[:, q, :], hin[:, 8:16]), sV)
            b.ch(v, v.tensor_sub(t8[:, 0, :], t8[:, 0, :], t8[:, 1, :]), sV)
            b.ch(v, v.tensor_add(cr[:, q, :], cr[:, q, :], t8[:, 0, :]), sV)
            b.ch(v, v.tensor_mul(t8[:, 2, :], D2qr[:, q, :], hin[:, 8:16]), sV)
            b.ch(v, v.tensor_mul(t8[:, 3, :], D2qi[:, q, :], hin[:, 0:8]), sV)
            b.ch(v, v.tensor_add(t8[:, 2, :], t8[:, 2, :], t8[:, 3, :]), sV)
            b.ch(v, v.tensor_add(ci_[:, q, :], ci_[:, q, :], t8[:, 2, :]), sV)

    @block.vector
    def _(v):
        for kc in range(8):
            for j in range(TC2):
                H2r = hr[:, kc, :].rearrange("p (nq c) -> p nq c", c=TC2)[:, :, j]
                H2i = hi[:, kc, :].rearrange("p (nq c) -> p nq c", c=TC2)[:, :, j]
                crq = cr[:, 0:16, kc]
                ciq = ci_[:, 0:16, kc]
                b.ch(v, v.scalar_tensor_tensor(H2r, crq, D1r[:, j+1, kc:kc+1], H2r, ALU.mult, ALU.add), sV)
                b.ch(v, v.scalar_tensor_tensor(H2r, ciq, D1n[:, j+1, kc:kc+1], H2r, ALU.mult, ALU.add), sV)
                b.ch(v, v.scalar_tensor_tensor(H2i, ciq, D1r[:, j+1, kc:kc+1], H2i, ALU.mult, ALU.add), sV)
                b.ch(v, v.scalar_tensor_tensor(H2i, crq, D1i[:, j+1, kc:kc+1], H2i, ALU.mult, ALU.add), sV)

    b.free("lSr"); b.free("lSi")
    hpr = b.sb("lhpr", [128, 8, NZL], F32); hpi = b.sb("lhpi", [128, 8, NZL], F32)
    @block.vector
    def _(v):
        for kc in range(8):
            b.ch(v, v.tensor_copy(hpr[:, kc, 1:NZL], hr[:, kc, 0:NZL-1]), sV)
            b.ch(v, v.tensor_copy(hpi[:, kc, 1:NZL], hi[:, kc, 0:NZL-1]), sV)
            b.ch(v, v.tensor_copy(hpr[:, kc, 0:1], hin[:, kc:kc+1]), sV)
            b.ch(v, v.tensor_copy(hpi[:, kc, 0:1], hin[:, 8+kc:9+kc]), sV)
        for l in range(TC1):
            for kc in range(8):
                o3 = yw[:, kc, :].rearrange("p (nz l8) -> p nz l8", l8=TC1)[:, :, l]
                b.ch(v, v.scalar_tensor_tensor(o3, hpr[:, kc, :], Prt[:, l, kc:kc+1], o3, ALU.mult, ALU.add), sV)
                b.ch(v, v.scalar_tensor_tensor(o3, hpi[:, kc, :], Pnt[:, l, kc:kc+1], o3, ALU.mult, ALU.add), sV)
        for kc in range(8):
            b.ch(v, v.scalar_tensor_tensor(yw[:, kc, :], u[:, kc, :], l1D[:, kc:kc+1],
                                           yw[:, kc, :], ALU.mult, ALU.add), sV)

    x1c = [b.sb(f"lx1{i}", [128, T], F32) for i in range(2)]
    ldc = {}; stc = {}; addc = {}
    for kc in range(8):
        @block.sync
        def _(sy, kc=kc):
            if kc == 0:
                b.wait(sy, sV)
            if kc >= 1:
                sy.wait_ge(b.h(sDs), ldc[kc-1])
            if kc >= 2:
                b.wait(sy, sV, addc[kc-2])
            ldc[kc] = b.dma(sy, x1c[kc % 2][:], hbm["x1"][:, kc, :], sDs)
        @block.vector
        def _(v, kc=kc):
            b.wait(v, sDs, ldc[kc])
            addc[kc] = b.inc(v.tensor_add(yw[:, kc, :], yw[:, kc, :], x1c[kc % 2][:]), sV)
        @block.sync
        def _(sy, kc=kc):
            b.wait(sy, sV, addc[kc])
            if kc >= 1:
                sy.wait_ge(b.h(sDso), stc[kc-1])
            stc[kc] = b.dma(sy, hbm["x2"][:, kc, :], yw[:, kc, :], sDso)
    @block.sync
    def _(sy):
        b.wait(sy, sDso)
    b.barrier()
    for nm in ("lru_u", "lApr", "lApi", "lkr", "lWr", "lWi", "lPr", "lPn",
               "lD1r", "lD1i", "lD1n", "lD2qr", "lD2qi", "lcr", "lci", "lt8",
               "ljnk", "lCBr", "lCBi", "lhin", "lyw", "lhr", "lhi",
               "lhpr", "lhpi", "lcst", "lalg", "lx10", "lx11"):
        b.free(nm)
    b.barrier()


def emit_moe(b, io, hbm, C, lns, out_x3):
    nc, block = b.nc, b.block
    sD, sDs, sT, sV, sS, sG, sCC = C["sems"]
    sDw = C["sDw"]; sDw2 = C["sDw2"]; sDso = C["sDso"]
    id_f32 = C["id_f32"]; gate_bias = C["gate_bias"]

    gsb = b.sb("mgw", [128, 8, 8], BF16)
    @block.gpsimd
    def _(g):
        b.dma(g, gsb[:], io["gateT"].rearrange("(k p) j -> p k j", p=128), sD)
        g.wait_ge(b.h(sD), b.val(sD))
        b.inc(g.nop(), sG)
    b.barrier()

    for tt in range(2):
        x_ln3 = b.sb("mxln", [128, 8, TT], BF16)
        emit_ln(b, hbm["x2"][:, :, tt*TT:(tt+1)*TT], x_ln3, TT, lns, f"lnmoe{tt}")

        lg = b.sb("mlg", [8, TT], F32)
        g_em = b.sb("mgem", [8, TT], F32)
        tok = b.sb("mtok", [128, 8], F32)
        s8 = b.sb("ms8", [128, 8], F32)
        bvt = b.sb("mbv", [128, 8], F32)
        msk = b.sb("mmsk", [128, 8], F32)
        b2t = b.sb("mb2", [128, 8], F32)
        rcol = b.sb("mrc", [128, 4], F32)
        with nc.psum_tensor(f"mpslg{tt}", [8, 1024], F32) as ps_lg, \
             nc.psum_tensor(f"mpst{tt}", [128, 8], F32) as ps_t, \
             nc.psum_tensor(f"mps3{tt}", [8, 128], F32) as ps3:
            @block.tensor
            def _(t):
                b.wait(t, sV)
                b.wait(t, sG)
                for th in range(2):
                    for k in range(8):
                        ii = t.matmul(ps_lg[0:8, th*512:(th+1)*512], gsb[:, k, :],
                                      x_ln3[:, k, th*512:(th+1)*512], start=(k == 0), stop=(k == 7))
                b.inc(ii, sT)
            @block.vector
            def _(v):
                b.wait(v, sT)
                b.inc(v.tensor_copy(lg[:], ps_lg[0:8, :]), sV)
            for tb in range(8):
                @block.tensor
                def _(t, tb=tb):
                    b.wait(t, sV)
                    b.inc(t.transpose(ps_t[:], lg[0:8, tb*128:(tb+1)*128], id_f32[0:8, 0:8]), sT)
                @block.vector
                def _(v, tb=tb):
                    b.wait(v, sT)
                    b.ch(v, v.tensor_copy(tok[:], ps_t[:]), sV)
                    b.ch(v, v.reduce_max(rcol[:, 0:1], tok[:], AXX), sV)
                    b.ch(v, v.tensor_scalar(rcol[:, 1:2], rcol[:, 0:1], -1.0, None, ALU.mult), sV)
                @block.scalar
                def _(s, tb=tb):
                    b.wait(s, sV)
                    b.inc(s.activation(tok[:], tok[:], AF.Exp, bias=rcol[:, 1:2], scale=1.0), sS)
                @block.vector
                def _(v, tb=tb):
                    b.wait(v, sS)
                    b.ch(v, v.reduce_sum(rcol[:, 2:3], tok[:], AXX), sV)
                    b.ch(v, v.reciprocal(rcol[:, 3:4], rcol[:, 2:3]), sV)
                    b.ch(v, v.tensor_scalar(s8[:], tok[:], rcol[:, 3:4], None, ALU.mult), sV)
                    b.ch(v, v.tensor_add(bvt[:], s8[:], gate_bias[:, 0:8]), sV)
                    b.ch(v, v.reduce_max(rcol[:, 0:1], bvt[:], AXX), sV)
                    b.ch(v, v.tensor_scalar(msk[:], bvt[:], rcol[:, 0:1], None, ALU.is_ge), sV)
                    b.ch(v, v.tensor_mul(b2t[:], bvt[:], msk[:]), sV)
                    b.ch(v, v.tensor_sub(b2t[:], bvt[:], b2t[:]), sV)
                    b.ch(v, v.tensor_scalar(msk[:], msk[:], -1e30, None, ALU.mult), sV)
                    b.ch(v, v.tensor_add(b2t[:], b2t[:], msk[:]), sV)
                    b.ch(v, v.reduce_max(rcol[:, 0:1], b2t[:], AXX), sV)
                    b.ch(v, v.tensor_scalar(msk[:], bvt[:], rcol[:, 0:1], None, ALU.is_ge), sV)
                    b.ch(v, v.tensor_mul(tok[:], s8[:], msk[:]), sV)
                @block.tensor
                def _(t, tb=tb):
                    b.wait(t, sV)
                    b.inc(t.transpose(ps3[:], tok[:], id_f32[:, :]), sT)
                @block.vector
                def _(v, tb=tb):
                    b.wait(v, sT)
                    b.ch(v, v.tensor_copy(g_em[0:8, tb*128:(tb+1)*128], ps3[0:8, :]), sV)
        @block.gpsimd
        def _(g):
            b.wait(g, sV)
            b.dma(g, hbm["gmoe"][0:8, tt*TT:(tt+1)*TT], g_em[:], sD)
            g.wait_ge(b.h(sD), b.val(sD))
            b.inc(g.nop(), sG)
        for nm in ("mlg", "mtok", "ms8", "mbv", "mmsk", "mb2", "mrc"):
            b.free(nm)

        hs = b.sb("mhs", [128, 16, TT], BF16)
        wA = [b.sb(f"mwa{i}", [128, 8, 128], BF16) for i in range(2)]
        wB = [b.sb(f"mwb{i}", [128, 8, 128], BF16) for i in range(2)]
        sa = b.sb("msa", [128, 2, TT], F32)
        y_acc = b.sb("myacc", [128, 8, TT], F32)
        wC = [b.sb(f"mwc{i}", [128, 16, 128], BF16) for i in range(2)]
        gbc = b.sb("mgbc", [128, 2, TT], F32)
        he = b.sb("mhe", [128, 8, TT], BF16)
        tb1 = b.sb("mtb1", [128, 2, TT], F32)
        wD = [b.sb(f"mwd{i}", [128, 8, 128], BF16) for i in range(2)]
        tA = {}; sA = {}; vA = {}; tB = {}; vB = {}
        t0_ = b.val(sT)
        with nc.psum_tensor(f"mpsA{tt}", [128, 2048], F32) as psA, \
             nc.psum_tensor(f"mpsB{tt}", [128, 2048], F32) as psB:
            for m in range(16):
                mh = m % 2
                @block.gpsimd
                def _(g, m=m, mh=mh):
                    g.wait_ge(b.h(sT), tA[m-2] if m >= 2 else t0_)
                    b.dma(g, wA[mh][:], io["sw1s"][m].rearrange("(k p) j -> p k j", p=128), sDw[mh])
                    b.dma(g, wB[mh][:], io["sw3s"][m].rearrange("(k p) j -> p k j", p=128), sDw[mh])
                @block.tensor
                def _(t, m=m, mh=mh):
                    b.wait(t, sDw[mh])
                    if m >= 2:
                        t.wait_ge(b.h(sS), sA[m-2])
                        t.wait_ge(b.h(sV), vA[m-2])
                    for th in range(2):
                        o = mh*1024 + th*512
                        for k in range(8):
                            t.matmul(psA[:, o:o+512], wA[mh][:, k, :],
                                     x_ln3[:, k, th*512:(th+1)*512], start=(k == 0), stop=(k == 7))
                    for th in range(2):
                        o = mh*1024 + th*512
                        for k in range(8):
                            ii = t.matmul(psB[:, o:o+512], wB[mh][:, k, :],
                                          x_ln3[:, k, th*512:(th+1)*512], start=(k == 0), stop=(k == 7))
                    tA[m] = b.inc(ii, sT)
                @block.scalar
                def _(s, m=m, mh=mh):
                    b.wait(s, sT)
                    if m >= 2:
                        s.wait_ge(b.h(sV), vA[m-2])
                    sA[m] = b.inc(s.activation(sa[:, mh, :], psA[:, mh*1024:(mh+1)*1024],
                                               AF.Silu, bias=0.0, scale=1.0), sS)
                @block.vector
                def _(v, m=m, mh=mh):
                    b.wait(v, sS)
                    vA[m] = b.inc(v.tensor_tensor(hs[:, m, :], psB[:, mh*1024:(mh+1)*1024],
                                                  sa[:, mh, :], ALU.mult), sV)
            t0b = b.val(sT)
            for m in range(8):
                mh = m % 2
                @block.gpsimd
                def _(g, m=m, mh=mh):
                    g.wait_ge(b.h(sT), tB[m-2] if m >= 2 else t0b)
                    b.dma(g, wC[mh][:], io["sw2s"][m].rearrange("(k p) j -> p k j", p=128), sDw2[mh])
                @block.tensor
                def _(t, m=m, mh=mh):
                    b.wait(t, sDw2[mh])
                    if m < 2:
                        b.wait(t, sV)
                        b.wait(t, sS)
                    if m >= 2:
                        t.wait_ge(b.h(sV), vB[m-2])
                    for th in range(2):
                        o = mh*1024 + th*512
                        for k in range(16):
                            ii = t.matmul(psA[:, o:o+512], wC[mh][:, k, :],
                                          hs[:, k, th*512:(th+1)*512], start=(k == 0), stop=(k == 15))
                    tB[m] = b.inc(ii, sT)
                @block.vector
                def _(v, m=m, mh=mh):
                    b.wait(v, sT)
                    vB[m] = b.inc(v.tensor_copy(y_acc[:, m, :], psA[:, mh*1024:(mh+1)*1024]), sV)
            for e in range(8):
                eh = e % 2
                @block.gpsimd
                def _(g, e=e, eh=eh):
                    if e >= 2:
                        g.wait_ge(b.h(sV), vA[16 + (e-2)*8 + 7])
                    else:
                        b.wait(g, sG)
                    b.dma(g, gbc[:, eh, :],
                          hbm["gmoe"][e:e+1, tt*TT:(tt+1)*TT].broadcast_to([128, TT]), sD)
                for m in range(8):
                    aidx = 16 + e*8 + m
                    mh = aidx % 2
                    @block.gpsimd
                    def _(g, e=e, m=m, mh=mh, aidx=aidx):
                        g.wait_ge(b.h(sT), tA[aidx-2])
                        b.dma(g, wA[mh][:], io["ew1s"][e][m].rearrange("(k p) j -> p k j", p=128), sDw[mh])
                        b.dma(g, wB[mh][:], io["ew3s"][e][m].rearrange("(k p) j -> p k j", p=128), sDw[mh])
                    @block.tensor
                    def _(t, e=e, m=m, mh=mh, aidx=aidx):
                        b.wait(t, sDw[mh])
                        if m < 2:
                            b.wait(t, sV)
                        t.wait_ge(b.h(sS), sA[aidx-2])
                        t.wait_ge(b.h(sV), vA[aidx-2])
                        for th in range(2):
                            o = mh*1024 + th*512
                            for k in range(8):
                                t.matmul(psA[:, o:o+512], wA[mh][:, k, :],
                                         x_ln3[:, k, th*512:(th+1)*512], start=(k == 0), stop=(k == 7))
                        for th in range(2):
                            o = mh*1024 + th*512
                            for k in range(8):
                                ii = t.matmul(psB[:, o:o+512], wB[mh][:, k, :],
                                              x_ln3[:, k, th*512:(th+1)*512], start=(k == 0), stop=(k == 7))
                        tA[aidx] = b.inc(ii, sT)
                    @block.scalar
                    def _(s, m=m, mh=mh, aidx=aidx):
                        b.wait(s, sT)
                        s.wait_ge(b.h(sV), vA[aidx-2])
                        sA[aidx] = b.inc(s.activation(sa[:, mh, :], psA[:, mh*1024:(mh+1)*1024],
                                                      AF.Silu, bias=0.0, scale=1.0), sS)
                    @block.vector
                    def _(v, e=e, m=m, mh=mh, eh=eh, aidx=aidx):
                        b.wait(v, sS)
                        if e >= 1 and m == 0:
                            b.wait(v, sD)
                        elif e == 0 and m == 0:
                            b.wait(v, sD)
                        b.ch(v, v.tensor_tensor(tb1[:, mh, :], psB[:, mh*1024:(mh+1)*1024],
                                                gbc[:, eh, :], ALU.mult), sV)
                        vA[aidx] = b.inc(v.tensor_tensor(he[:, m, :], tb1[:, mh, :],
                                                         sa[:, mh, :], ALU.mult), sV)
                for m2 in range(8):
                    bidx = 8 + e*8 + m2
                    mh = bidx % 2
                    @block.gpsimd
                    def _(g, e=e, m2=m2, mh=mh, bidx=bidx):
                        g.wait_ge(b.h(sT), tB[bidx-2])
                        b.dma(g, wD[mh][:], io["ew2s"][e][m2].rearrange("(k p) j -> p k j", p=128), sDw2[mh])
                    @block.tensor
                    def _(t, m2=m2, mh=mh, bidx=bidx):
                        b.wait(t, sDw2[mh])
                        if m2 < 2:
                            b.wait(t, sV)
                            b.wait(t, sS)
                        t.wait_ge(b.h(sV), vB[bidx-2])
                        for th in range(2):
                            o = mh*1024 + th*512
                            for k in range(8):
                                ii = t.matmul(psA[:, o:o+512], wD[mh][:, k, :],
                                              he[:, k, th*512:(th+1)*512], start=(k == 0), stop=(k == 7))
                        tB[bidx] = b.inc(ii, sT)
                    @block.vector
                    def _(v, m2=m2, mh=mh, bidx=bidx):
                        b.wait(v, sT)
                        b.ch(v, v.tensor_add(y_acc[:, m2, :], y_acc[:, m2, :],
                                             psA[:, mh*1024:(mh+1)*1024]), sV)
                        vB[bidx] = b.val(sV)

        x2c = [b.sb(f"mx2{i}", [128, TT], F32) for i in range(2)]
        ldc = {}; stc = {}; adc = {}
        for kc in range(8):
            @block.sync
            def _(sy, kc=kc):
                if kc == 0:
                    b.wait(sy, sV)
                if kc >= 1:
                    sy.wait_ge(b.h(sDs), ldc[kc-1])
                if kc >= 2:
                    b.wait(sy, sV, adc[kc-2])
                ldc[kc] = b.dma(sy, x2c[kc % 2][:, 0:TT], hbm["x2"][:, kc, tt*TT:(tt+1)*TT], sDs)
            @block.vector
            def _(v, kc=kc):
                b.wait(v, sDs, ldc[kc])
                adc[kc] = b.inc(v.tensor_add(y_acc[:, kc, :], y_acc[:, kc, :], x2c[kc % 2][:, 0:TT]), sV)
            @block.sync
            def _(sy, kc=kc):
                b.wait(sy, sV, adc[kc])
                if kc >= 1:
                    sy.wait_ge(b.h(sDso), stc[kc-1])
                stc[kc] = b.dma(sy, out_x3[:, kc, tt*TT:(tt+1)*TT], y_acc[:, kc, :], sDso)
        @block.sync
        def _(sy):
            b.wait(sy, sDso)
        b.barrier()
        for nm in ("mxln", "mgem", "mhs", "mwa0", "mwa1", "mwb0", "mwb1", "msa",
                   "myacc", "mwc0", "mwc1", "mgbc", "mhe", "mtb1", "mwd0", "mwd1",
                   "mx20", "mx21"):
            b.free(nm)
    b.free("mgw")
    b.barrier()


def build_program():
    nc = bass.Bass("TRN2", target_bir_lowering=False)

    def din(name, shape, dt=F32):
        return nc.dram_tensor(name, shape, dt, kind="ExternalInput").ap()

    io = {}
    io["u_fm"] = din("u_fm", [128, 8, HALO + T])
    io["conv_w"] = din("conv_w", [128, 17, 4])
    io["conv_b"] = din("conv_b", [128, 17])
    io["dt_bias"] = din("dt_bias", [32, 1])
    io["A_log"] = din("A_log", [32, 1])
    io["D0"] = din("D0", [128, 16])
    io["norm_w"] = din("norm_w", [128, 16])
    io["w1s"] = din("w1s", [32, 1024, 128])
    io["w2s"] = din("w2s", [32, 1024, 128])
    io["h1s"] = din("h1s", [32, 4096, 128])
    io["h2s"] = din("h2s", [32, 4096, 128])
    io["w3s"] = din("w3s", [8, 4096, 128])
    io["wins"] = din("wins", [33, 1024, 128])
    io["windt"] = din("windt", [1024, 32])
    io["wouts"] = din("wouts", [8, 2048, 128])
    io["s1"] = din("s1", [128, 32])
    io["s2"] = din("s2", [128, 32])
    io["sh1"] = din("sh1", [128, 32])
    io["sh2"] = din("sh2", [128, 32])
    io["s3"] = din("s3", [128, 8])
    io["gateT"] = din("gateT", [1024, 8])
    io["gate_bias"] = din("gate_bias", [128, 8])
    io["ew1s"] = din("ew1s", [8, 8, 1024, 128])
    io["ew3s"] = din("ew3s", [8, 8, 1024, 128])
    io["ew2s"] = din("ew2s", [8, 8, 1024, 128])
    io["sw1s"] = din("sw1s", [16, 1024, 128])
    io["sw3s"] = din("sw3s", [16, 1024, 128])
    io["sw2s"] = din("sw2s", [8, 2048, 128])
    io["l1A"] = din("l1A", [128, 16])
    io["l1B"] = din("l1B", [128, 16])
    io["l1C"] = din("l1C", [128, 16])
    io["l1D"] = din("l1D", [128, 8])
    io["m8_64"] = din("m8_64", [64, 8])
    io["m8_128"] = din("m8_128", [128, 8])

    out_x3 = nc.dram_tensor("x3", [128, 8, T], F32, kind="ExternalOutput").ap()
    dbg = nc.dram_tensor("dbgo", [128, 18600], F32, kind="ExternalOutput").ap() if DBG else None

    x1_hbm = nc.dram_tensor("x1_hbm", [128, 8, T], F32, kind="Internal").ap()
    x2_hbm = nc.dram_tensor("x2_hbm", [128, 8, T], F32, kind="Internal").ap()
    xcv_hbm = nc.dram_tensor("xcv_hbm", [128, 17, T], BF16, kind="Internal").ap()
    acsf_hbm = nc.dram_tensor("acsf_hbm", [1, NCH_T * NH * CH], F32, kind="Internal").ap()
    acsf2_hbm = nc.dram_tensor("acsf2_hbm", [1, NCH_T * NH * CH], BF16, kind="Internal").ap()
    flat_hbm = nc.dram_tensor("flat_hbm", [1, 16384], F32, kind="Internal").ap()
    cc_min = nc.dram_tensor("cc_min", [64, NH * NS], F32, kind="Internal").ap()
    cc_mout = nc.dram_tensor("cc_mout", [NCC * 64, NH * NS], F32, kind="Internal").ap()
    cc_lin = nc.dram_tensor("cc_lin", [128, 16], F32, kind="Internal").ap()
    cc_lout = nc.dram_tensor("cc_lout", [NCC * 128, 16], F32, kind="Internal").ap()
    gmoe_hbm = nc.dram_tensor("gmoe_hbm", [8, T], F32, kind="Internal").ap()

    with nc.Block() as block:
        b = B(nc, block)
        b.init_pool(52000)
        sD = b.sem("dma_g")
        sDs = b.sem("dma_s")
        sT = b.sem("s_t")
        sV = b.sem("s_v")
        sS = b.sem("s_s")
        sG = b.sem("s_g")
        sCC = b.sem("s_cc")
        sDw = [b.sem("dma_w0"), b.sem("dma_w1")]
        sDw2 = [b.sem("dma_w2a"), b.sem("dma_w2b")]
        sDso = b.sem("dma_so")

        id_bf = b.sb("id_bf", [128, 128], BF16)
        id_f32 = b.sb("id_f32", [128, 128], F32)
        ones_bf = b.sb("ones_bf", [128, 1], BF16)
        ones_f32r = b.sb("ones_f32r", [1, 128], F32)
        maskST = b.sb("maskST", [64, 128], F32)
        conv_w = b.sb("conv_wsb", [128, 17, 4], F32)
        conv_b = b.sb("conv_bsb", [128, 17], F32)
        dt_bias = b.sb("dt_biassb", [32, 1], F32)
        A_col = b.sb("A_col", [32, 1], F32)
        D0 = b.sb("D0sb", [128, 16], F32)
        norm_w = b.sb("norm_wsb", [128, 16], F32)
        m8_64 = b.sb("m8_64sb", [64, 8], F32)
        m8_128 = b.sb("m8_128sb", [128, 8], F32)
        gate_bias = b.sb("gate_biassb", [128, 8], F32)
        l1A = b.sb("l1Asb", [128, 16], F32)
        l1B = b.sb("l1Bsb", [128, 16], F32)
        l1C = b.sb("l1Csb", [128, 16], F32)
        l1D = b.sb("l1Dsb", [128, 8], F32)

        @block.vector
        def _(v):
            v.memset(ones_bf[:], 1.0)
            v.memset(ones_f32r[:], 1.0)
            v.memset(id_bf[:], 1.0)
            v.memset(id_f32[:], 1.0)
            b.inc(v.memset(maskST[:], 1.0), sV)

        @block.gpsimd
        def _(g):
            for dst, src in ((conv_w, "conv_w"), (conv_b, "conv_b"), (dt_bias, "dt_bias"),
                             (A_col, "A_log"), (D0, "D0"), (norm_w, "norm_w"),
                             (m8_64, "m8_64"), (m8_128, "m8_128"), (gate_bias, "gate_bias"),
                             (l1A, "l1A"), (l1B, "l1B"), (l1C, "l1C"), (l1D, "l1D")):
                b.dma(g, dst[:], io[src], sD)
            g.wait_ge(b.h(sD), b.val(sD))
            b.wait(g, sV)
            g.affine_select(id_bf[:], id_bf[:], pattern=[[-1, 128]], compare_op=ALU.is_equal,
                            fill=0.0, base=0, channel_multiplier=1)
            g.affine_select(id_f32[:], id_f32[:], pattern=[[-1, 128]], compare_op=ALU.is_equal,
                            fill=0.0, base=0, channel_multiplier=1)
            g.affine_select(maskST[:], maskST[:], pattern=[[0, 2], [1, 64]],
                            compare_op=ALU.is_ge, fill=0.0, base=0, channel_multiplier=-1)
            b.inc(g.engine_nop(), sG)


        @block.scalar
        def _(s):
            b.wait(s, sG)
            b.ch(s, s.activation(A_col[:], A_col[:], AF.Exp, bias=0.0, scale=1.0), sS)
            b.inc(s.mul(A_col[:], A_col[:], -1.0), sS)

        b.barrier()

        W0 = HALO + T
        lns = {
            "ones_bf": ones_bf, "ones_f32r": ones_f32r,
            "sems": (sD, sDs, sT, sV, sS),
        }

        x_ln = b.sb("x_ln", [128, 8, W0], BF16)
        emit_ln(b, io["u_fm"], x_ln, W0, lns, "ln0")
        b.barrier()

        if DBG == "ln0":
            @block.gpsimd
            def _(g):
                b.wait(g, sV)
                for kc in range(8):
                    b.dma(g, dbg[:, kc*2051:(kc+1)*2051], x_ln[:, kc, :], sD)
                g.wait_ge(b.h(sD), b.val(sD))
            b.stack.close()
            return nc

        consts = {
            "sems": (sD, sDs, sT, sV, sS, sG, sCC),
            "sDw": sDw, "sDw2": sDw2, "sDso": sDso,
            "ones_bf": ones_bf, "ones_f32r": ones_f32r, "id_bf": id_bf, "id_f32": id_f32,
            "maskST": maskST, "conv_w": conv_w, "conv_b": conv_b, "dt_bias": dt_bias,
            "A_col": A_col, "D0": D0, "norm_w": norm_w, "m8_64": m8_64, "m8_128": m8_128,
            "gate_bias": gate_bias, "l1A": l1A, "l1B": l1B, "l1C": l1C, "l1D": l1D,
        }
        hbm = {
            "x1": x1_hbm, "x2": x2_hbm, "xcv": xcv_hbm, "acsf": acsf_hbm,
            "acsf2": acsf2_hbm, "flat": flat_hbm, "cc_min": cc_min, "cc_mout": cc_mout,
            "cc_lin": cc_lin, "cc_lout": cc_lout, "gmoe": gmoe_hbm,
        }

        emit_mamba(b, io, x_ln, consts, hbm, dbg)

        lvl = 99 if not DBG else {"ffnin": 1, "lruin": 2, "moein": 3}.get(DBG, 0)

        def dump_x1():
            @block.gpsimd
            def _(g):
                b.dma(g, dbg[:, 0:16384], x1_hbm[:, :, :].rearrange("p a t -> p (a t)"), sD)
                g.wait_ge(b.h(sD), b.val(sD))
                b.inc(g.nop(), sG)
            b.barrier()

        def emit_out(src):
            @block.sync
            def _(sy):
                b.wait(sy, sDso)
                b.dma(sy, out_x3[:], src, sDs)
                sy.wait_ge(b.h(sDs), b.val(sDs))

        if lvl >= 1:
            if DBG == "ffnin":
                dump_x1()
            emit_ffn0(b, io, hbm, consts, lns)
        if lvl == 1:
            emit_out(x1_hbm)
        if lvl >= 2:
            if DBG == "lruin":
                dump_x1()
            emit_lru(b, io, hbm, consts, lns)
        if lvl == 2:
            emit_out(x2_hbm)
        if lvl >= 3:
            if DBG == "moein":
                @block.gpsimd
                def _(g):
                    b.dma(g, dbg[:, 0:16384], x2_hbm[:, :, :].rearrange("p a t -> p (a t)"), sD)
                    g.wait_ge(b.h(sD), b.val(sD))
                    b.inc(g.nop(), sG)
                b.barrier()
            emit_moe(b, io, hbm, consts, lns, out_x3)

        b.stack.close()
        b.pool_guard.__exit__(None, None, None)
    return nc


def _prep_inputs(inputs):
    f = np.float32
    u = inputs["L0_attn_u"]
    Win = inputs["L0_attn_W_in"]
    winT = np.ascontiguousarray(Win.T)
    wins = np.zeros((33, 1024, 128), f)
    for m in range(32):
        c0 = m * 128
        c1 = min(c0 + 128, 4128)
        if c0 < 4128:
            wins[m, :, 0:c1-c0] = winT[:, c0:c1]
    wins[32, :, 0:16] = winT[:, 4096:4112]
    wins[32, :, 32:48] = winT[:, 4112:4128]
    windt = np.ascontiguousarray(winT[:, 4128:4160])
    wout = inputs["L0_attn_W_out"]
    woutT = np.ascontiguousarray(wout.T)
    wouts = np.ascontiguousarray(woutT.reshape(2048, 8, 128).transpose(1, 0, 2))

    def strips(wT, nm):
        return np.ascontiguousarray(wT.reshape(wT.shape[0], nm, 128).transpose(1, 0, 2))

    w1T = np.ascontiguousarray(inputs["L0_ffn_w1_w"].T)
    w2T = np.ascontiguousarray(inputs["L0_ffn_w2_w"].T)
    h1T = np.ascontiguousarray(inputs["L0_ffn_h1_w"].T)
    h2T = np.ascontiguousarray(inputs["L0_ffn_h2_w"].T)
    w3T = np.ascontiguousarray(inputs["L0_ffn_w3_w"].T)

    conv_w = np.zeros((128, 17, 4), f)
    cw = inputs["L0_attn_conv_w"][:, 0, :]
    cwp = np.zeros((17 * 128, 4), f)
    cwp[:2048] = cw[:2048]
    cwp[2048:2064] = cw[2048:2064]
    cwp[2048+32:2048+48] = cw[2064:2080]
    conv_w = np.ascontiguousarray(cwp.reshape(17, 128, 4).transpose(1, 0, 2))
    cbp = np.zeros(17 * 128, f)
    cbp[:2048] = inputs["L0_attn_conv_b"][:2048]
    cbp[2048:2064] = inputs["L0_attn_conv_b"][2048:2064]
    cbp[2048+32:2048+48] = inputs["L0_attn_conv_b"][2064:2080]
    conv_b = np.ascontiguousarray(cbp.reshape(17, 128).T)

    def pack_pf(vec, ncols):
        return np.ascontiguousarray(vec.reshape(ncols, 128).T)

    D0 = pack_pf(np.repeat(inputs["L0_attn_D"], 64), 16)
    norm_w = pack_pf(inputs["L0_attn_norm_w"], 16)
    l1A = np.concatenate([pack_pf(inputs["L1_attn_A_real"], 8),
                          pack_pf(inputs["L1_attn_A_imag"], 8)], axis=1)
    l1B = np.concatenate([pack_pf(inputs["L1_attn_B_real"], 8),
                          pack_pf(inputs["L1_attn_B_imag"], 8)], axis=1)
    l1C = np.concatenate([pack_pf(inputs["L1_attn_C_real"], 8),
                          pack_pf(inputs["L1_attn_C_imag"], 8)], axis=1)
    l1D = pack_pf(inputs["L1_attn_D"], 8)
    gateT = np.ascontiguousarray(inputs["L1_ffn_gate_w"].T)
    gate_bias = np.ascontiguousarray(np.broadcast_to(inputs["L1_ffn_gate_bias"], (128, 8)))
    ew1s = np.stack([strips(np.ascontiguousarray(inputs["L1_ffn_w1"][e].T), 8) for e in range(8)])
    ew3s = np.stack([strips(np.ascontiguousarray(inputs["L1_ffn_w3"][e].T), 8) for e in range(8)])
    ew2s = np.stack([strips(np.ascontiguousarray(inputs["L1_ffn_w2"][e].T), 8) for e in range(8)])
    sw1s = strips(np.ascontiguousarray(inputs["L1_ffn_sw1"].T), 16)
    sw3s = strips(np.ascontiguousarray(inputs["L1_ffn_sw3"].T), 16)
    sw2s = strips(np.ascontiguousarray(inputs["L1_ffn_sw2"].T), 8)

    common = dict(
        wins=wins, windt=windt, wouts=wouts,
        conv_w=conv_w, conv_b=conv_b,
        dt_bias=inputs["L0_attn_dt_bias"].reshape(32, 1).astype(f),
        A_log=inputs["L0_attn_A_log"].reshape(32, 1).astype(f),
        D0=D0, norm_w=norm_w,
        w1s=strips(w1T, 32), w2s=strips(w2T, 32),
        h1s=strips(h1T, 32), h2s=strips(h2T, 32), w3s=strips(w3T, 8),
        s1=pack_pf(inputs["L0_ffn_w1_s"], 32), s2=pack_pf(inputs["L0_ffn_w2_s"], 32),
        sh1=pack_pf(inputs["L0_ffn_h1_s"], 32), sh2=pack_pf(inputs["L0_ffn_h2_s"], 32),
        s3=pack_pf(inputs["L0_ffn_w3_s"], 8),
        gateT=gateT, gate_bias=gate_bias,
        ew1s=ew1s, ew3s=ew3s, ew2s=ew2s, sw1s=sw1s, sw3s=sw3s, sw2s=sw2s,
        l1A=l1A, l1B=l1B, l1C=l1C, l1D=l1D,
    )

    in_maps = []
    for c in range(8):
        bb, sh = c // 2, c % 2
        t0 = sh * T
        u_slice = u[bb, t0:t0 + T]
        halo = u[bb, t0 - HALO:t0] if sh == 1 else np.zeros((HALO, DM), f)
        u_ext = np.concatenate([halo, u_slice], 0)
        u_fm = np.ascontiguousarray(u_ext.T).reshape(8, 128, HALO + T).transpose(1, 0, 2)
        u_fm = np.ascontiguousarray(u_fm)
        m8v_64 = np.zeros((64, 8), f)
        m8v_128 = np.zeros((128, 8), f)
        if sh == 1:
            m8v_64[:, c - 1] = 1.0
            m8v_128[:, c - 1] = 1.0
        m = dict(common)
        m["u_fm"] = u_fm
        m["m8_64"] = m8v_64
        m["m8_128"] = m8v_128
        in_maps.append(m)
    return in_maps


def kernel(**inputs):
    in_maps = _prep_inputs(inputs)
    nc = build_program()
    res = run_bass_kernel_spmd(nc, in_maps, core_ids=list(range(8)))
    out = np.zeros((4, 4096, 1024), np.float32)
    for c in range(8):
        bb, sh = c // 2, c % 2
        x3 = np.asarray(res.results[c]["x3"])
        out[bb, sh*T:(sh+1)*T] = x3.transpose(1, 0, 2).reshape(1024, T).T
    return out


if __name__ == "__main__":
    nc = build_program()
    print("build ok")


# version 03
# speedup vs baseline: 1.1146x; 1.1146x over previous
import os
import sys
if '/opt/trn_rl_repo' not in sys.path:
    sys.path.insert(0, '/opt/trn_rl_repo')
import numpy as np
import concourse.bass as bass
import concourse.mybir as mybir
from concourse.bass_utils import run_bass_kernel_spmd
from contextlib import ExitStack

F32 = mybir.dt.float32
BF16 = mybir.dt.bfloat16
AF = mybir.ActivationFunctionType
ALU = mybir.AluOpType
AXX = mybir.AxisListType.X

T = 2048
TT = 1024
HALO = 3
DM = 1024
DI = 2048
NH = 32
HP = 64
NS = 16
CH = 64
NCH_T = TT // CH
CONVD = 2080
DH = 4096
TC1 = 8
TC2 = 16
NZ = T // TC1
NQ = NZ // TC2

DBG = ""
NCC = 8

class B:
    def __init__(self, nc, block):
        self.nc = nc
        self.block = block
        self.semc = {}
        self.stack = ExitStack()
        self._sb = {}
        self.engines = ["tensor", "vector", "scalar", "gpsimd", "sync"]
        for e in self.engines:
            self.sem("prog_" + e)

    def sem(self, name):
        h = self.stack.enter_context(self.nc.semaphore(name))
        self.semc[name] = [h, 0]
        return name

    def inc(self, inst, name, by=1):
        self.semc[name][0] and inst.then_inc(self.semc[name][0], by)
        self.semc[name][1] += by
        return self.semc[name][1]

    def bump(self, name, by):
        self.semc[name][1] += by
        return self.semc[name][1]

    def val(self, name):
        return self.semc[name][1]

    def h(self, name):
        return self.semc[name][0]

    def wait(self, eng, name, value=None):
        eng.wait_ge(self.h(name), self.val(name) if value is None else value)

    def dma(self, q, out, in_, sem):
        q.dma_start(out=out, in_=in_).then_inc(self.h(sem), 16)
        return self.bump(sem, 16)

    def init_pool(self, cols):
        self.pool_guard = self.nc.sbuf_tensor("pool", [128, cols], F32)
        h = self.pool_guard.__enter__()
        self.pf = h.ap()
        self.pb = self.pf.bitcast(BF16)
        self.free_list = [(0, cols)]
        self.allocs = {}

    def sb(self, name, shape, dtype, top=False):
        assert name not in self.allocs, name
        n_inner = 1
        for s in shape[1:]:
            n_inner *= s
        if dtype == F32:
            units = n_inner
        else:
            units = (n_inner + 1) // 2
        units = (units + 3) & ~3
        order = reversed(list(enumerate(self.free_list))) if top else enumerate(self.free_list)
        for i, (off, size) in order:
            if size >= units:
                if top:
                    off = off + size - units
                    if size == units:
                        self.free_list.pop(i)
                    else:
                        self.free_list[i] = (self.free_list[i][0], size - units)
                elif size == units:
                    self.free_list.pop(i)
                else:
                    self.free_list[i] = (off + units, size - units)
                self.allocs[name] = (off, units)
                base = self.pf if dtype == F32 else self.pb
                scale = 1 if dtype == F32 else 2
                v = base[0:shape[0], off*scale: off*scale + n_inner]
                if len(shape) > 2:
                    pat = "p (" + " ".join(f"d{j}" for j in range(len(shape)-1)) + ") -> p " +                           " ".join(f"d{j}" for j in range(len(shape)-1))
                    kw = {f"d{j}": shape[j+1] for j in range(len(shape)-2)}
                    v = v.rearrange(pat, **kw)
                return v
        raise MemoryError(f"pool OOM for {name}: need {units}, free={self.free_list}, allocs={ {k: v for k, v in sorted(self.allocs.items(), key=lambda x: x[1][0])} }")

    def free(self, name):
        off, units = self.allocs.pop(name)
        self.free_list.append((off, units))
        self.free_list.sort()
        merged = []
        for o, s in self.free_list:
            if merged and merged[-1][0] + merged[-1][1] == o:
                merged[-1] = (merged[-1][0], merged[-1][1] + s)
            else:
                merged.append((o, s))
        self.free_list = [tuple(x) for x in merged]

    def gsync(self, g, sem):
        self.inc(g.nop(), sem)
        g.wait_ge(self.h(sem), self.val(sem))

    def ch(self, E, inst, sem):
        self.inc(inst, sem)
        E.wait_ge(self.h(sem), self.val(sem))

    def barrier(self):
        targets = {}
        for e in self.engines:
            @getattr(self.block, e)
            def _(E, e=e):
                self.inc(E.drain(), "prog_" + e)
            targets[e] = self.val("prog_" + e)
        for e in self.engines:
            @getattr(self.block, e)
            def _(E, e=e):
                for o in self.engines:
                    if o != e:
                        E.wait_ge(self.h("prog_" + o), targets[o])


def emit_ln(b, src_hbm, dst_bf, width, scratch, tag, u_buf=None):
    scratch = dict(scratch)
    scratch["u"] = u_buf if u_buf is not None else b.sb("lnu_" + tag, [128, 8, width], F32)
    scratch["ubf"] = b.sb("lnubf_" + tag, [128, 8, width], BF16)
    scratch["sqbf"] = b.sb("lnsq_" + tag, [128, 8, width], BF16)
    scratch["m_row"] = b.sb("lnmr_" + tag, [1, width], F32)
    scratch["r_row"] = b.sb("lnrr_" + tag, [1, width], F32)
    scratch["t1_row"] = b.sb("lnt1_" + tag, [1, width], F32)
    scratch["mb"] = b.sb("lnmb_" + tag, [128, 512], F32)
    scratch["rb"] = b.sb("lnrb_" + tag, [128, 512], F32)
    scratch["tw"] = b.sb("lntw_" + tag, [128, 512], F32)
    """LN over d=1024 of a feature-major fp32 HBM tensor src_hbm [128,8,width].
    Writes dst_bf [128,8,width] bf16 = (x - m) * rsqrt(var+eps).
    scratch: dict with persistent tiles (created by caller once):
      u (f32 [128,8,width_max]), ubf, sqbf (bf16 same), rows (f32 [1,width_max*?]),
      mb, rb (f32 [128, 512])
    Loads src from HBM (fp32 via sync + bf16 cast via gpsimd).
    """
    nc, block = b.nc, b.block
    u = scratch["u"]; ubf = scratch["ubf"]; sqbf = scratch["sqbf"]
    m_row = scratch["m_row"]; r_row = scratch["r_row"]
    t1_row = scratch["t1_row"]
    ones_bf = scratch["ones_bf"]; ones_f32r = scratch["ones_f32r"]
    mb = scratch["mb"]; rb = scratch["rb"]
    sD, sDs, sT, sV, sS = scratch["sems"]

    @block.sync
    def _(sy):
        b.dma(sy, u[:, :, 0:width], src_hbm, sDs)
    @block.gpsimd
    def _(g):
        b.dma(g, ubf[:, :, 0:width], src_hbm, sD)

    @block.scalar
    def _(s):
        b.wait(s, sDs)
        for kc in range(8):
            ii = s.activation(sqbf[:, kc, 0:width], u[:, kc, 0:width], AF.Square, bias=0.0, scale=1.0)
        b.inc(ii, sS)

    wins = [(o, min(512, width - o)) for o in range(0, width, 512)]
    with nc.psum_tensor("psst_" + tag, [1, 2560], F32) as ps_stat, \
         nc.psum_tensor("psbc_" + tag, [128, 1024], F32) as ps_bc:
        for ridx, srcb in ((0, ubf), (1, sqbf)):
            @block.tensor
            def _(t, ridx=ridx, srcb=srcb):
                if ridx == 0:
                    b.wait(t, sD)
                    b.wait(t, sS)
                else:
                    b.wait(t, sV)
                for (o, n) in wins:
                    for kc in range(8):
                        ii = t.matmul(ps_stat[0:1, o:o+n], ones_bf[:, 0:1],
                                      srcb[:, kc, o:o+n], start=(kc == 0), stop=(kc == 7))
                b.inc(ii, sT)

            @block.vector
            def _(v, ridx=ridx):
                b.wait(v, sT)
                b.inc(v.tensor_copy((m_row if ridx == 0 else r_row)[0:1, 0:width], ps_stat[0:1, 0:width]), sV)

        @block.vector
        def _(v):
            b.wait(v, sV)
            b.ch(v, v.tensor_scalar_mul(m_row[0:1, 0:width], m_row[0:1, 0:width], 1.0 / DM), sV)
            b.ch(v, v.tensor_scalar_mul(r_row[0:1, 0:width], r_row[0:1, 0:width], 1.0 / DM), sV)
            b.ch(v, v.tensor_mul(t1_row[0:1, 0:width], m_row[0:1, 0:width], m_row[0:1, 0:width]), sV)
            b.ch(v, v.tensor_sub(r_row[0:1, 0:width], r_row[0:1, 0:width], t1_row[0:1, 0:width]), sV)
            b.inc(v.tensor_scalar_add(r_row[0:1, 0:width], r_row[0:1, 0:width], 1e-5), sV)
        @block.scalar
        def _(s):
            b.wait(s, sV)
            b.inc(s.activation(r_row[0:1, 0:width], r_row[0:1, 0:width], AF.Sqrt, bias=0.0, scale=1.0), sS)
        @block.vector
        def _(v):
            b.wait(v, sS)
            b.inc(v.reciprocal(r_row[0:1, 0:width], r_row[0:1, 0:width]), sV)

        for (o, n) in wins:
            @block.tensor
            def _(t, o=o, n=n):
                b.wait(t, sV)
                if o > 0:
                    b.wait(t, sV)
                t.matmul(ps_bc[:, 0:n], ones_f32r[0:1, :], m_row[0:1, o:o+n], start=True, stop=True)
                b.inc(t.matmul(ps_bc[:, 512:512+n], ones_f32r[0:1, :], r_row[0:1, o:o+n], start=True, stop=True), sT)
            @block.vector
            def _(v, o=o, n=n):
                b.wait(v, sT)
                b.ch(v, v.tensor_copy(mb[:, 0:n], ps_bc[:, 0:n]), sV)
                b.ch(v, v.tensor_copy(rb[:, 0:n], ps_bc[:, 512:512+n]), sV)
                for kc in range(8):
                    b.ch(v, v.tensor_sub(scratch["tw"][:, 0:n], u[:, kc, o:o+n], mb[:, 0:n]), sV)
                    ii = v.tensor_mul(dst_bf[:, kc, o:o+n], scratch["tw"][:, 0:n], rb[:, 0:n])
                    if kc < 7:
                        b.ch(v, ii, sV)
                b.inc(ii, sV)
    b.barrier()
    for nm in ("lnubf_", "lnsq_", "lnmr_", "lnrr_", "lnt1_", "lnmb_", "lnrb_", "lntw_"):
        b.free(nm + tag)
    if u_buf is None:
        b.free("lnu_" + tag)


def emit_mamba(b, io, x_ln, C, hbm, dbg):
    nc, block = b.nc, b.block
    sD, sDs, sT, sV, sS, sG, sCC = C["sems"]
    sDw = C["sDw"]; sDso = C["sDso"]
    ones_bf = C["ones_bf"]; ones_f32r = C["ones_f32r"]; id_bf = C["id_bf"]
    id_f32 = C["id_f32"]; maskST = C["maskST"]; conv_w = C["conv_w"]
    conv_b = C["conv_b"]; dt_bias = C["dt_bias"]; A_col = C["A_col"]
    D0 = C["D0"]; norm_w = C["norm_w"]; m8_64 = C["m8_64"]; shl_sb = C["shl"]

    acs2 = b.sb("acs2", [32, 2, TT], F32)
    acsT = b.sb("acsT", [128, 16, 32], F32)
    dtT = b.sb("dtT", [128, 16, 32], F32)
    dexT = b.sb("dexT", [128, 16, 32], F32)
    asum = b.sb("asum", [32, 32], F32)
    state_in = b.sb("state_in", [64, 512], F32)
    ST3 = [b.sb("ST3a", [64, 8192], F32, top=True), b.sb("ST3b", [64, 8192], F32, top=True)]
    adt_gate = b.sb("adt_gate", [32, TT], F32)

    @block.vector
    def _(v):
        b.ch(v, v.memset(adt_gate[:], 1.0), sV)
        for c in range(NCH_T):
            ii = v.memset(adt_gate[:, c * CH:c * CH + 1], 0.0)
        b.inc(ii, sV)

    for tt in range(2):
        o_t = HALO + tt * TT
        xcv = b.sb("xcv", [128, 17, TT], BF16)
        xbc = b.sb("xbc", [128, 17, HALO + TT], BF16)
        wb = [b.sb(f"wbA{i}", [128, 8, 128], BF16) for i in range(2)]
        dtraw = b.sb("dtraw", [32, TT], F32)
        conv_acc = b.sb("conv_acc", [128, TT], F32)

        tcounts = {}
        loop_t0 = b.val(sT)
        with nc.psum_tensor(f"ps_xbc{tt}", [128, 1536], F32) as ps_xbc, \
             nc.psum_tensor(f"ps_dt{tt}", [32, 1024], F32) as ps_dt:
            for si, m in enumerate(range(16, 33)):
                buf = wb[si % 2]
                @block.gpsimd
                def _(g, m=m, buf=buf, si=si):
                    g.wait_ge(b.h(sT), tcounts[si - 2] if si >= 2 else loop_t0)
                    if si < 2:
                        b.wait(g, sV)
                    b.dma(g, buf[:], io["wins"][m].rearrange("(k p) j -> p k j", p=128), sDw[si % 2])
                @block.tensor
                def _(t, buf=buf, si=si):
                    b.wait(t, sDw[si % 2])
                    if si >= 1:
                        b.wait(t, sV)
                    for k in range(8):
                        t.matmul(ps_xbc[:, 0:HALO], buf[:, k, :], x_ln[:, k, o_t-HALO:o_t],
                                 start=(k == 0), stop=(k == 7))
                        t.matmul(ps_xbc[:, 512:1024], buf[:, k, :], x_ln[:, k, o_t:o_t+512],
                                 start=(k == 0), stop=(k == 7))
                        ii = t.matmul(ps_xbc[:, 1024:1536], buf[:, k, :], x_ln[:, k, o_t+512:o_t+1024],
                                      start=(k == 0), stop=(k == 7))
                    tcounts[si] = b.inc(ii, sT)
                @block.vector
                def _(v, m=m):
                    b.wait(v, sT)
                    cc = m - 16
                    v.tensor_copy(xbc[:, cc, 0:HALO], ps_xbc[:, 0:HALO])
                    b.inc(v.tensor_copy(xbc[:, cc, HALO:HALO+TT], ps_xbc[:, 512:1536]), sV)
            @block.gpsimd
            def _(g):
                g.wait_ge(b.h(sT), tcounts[16])
                b.dma(g, wb[1][:, :, 0:32], io["windt"].rearrange("(k p) j -> p k j", p=128), sDw[0])
            @block.tensor
            def _(t):
                b.wait(t, sDw[0])
                b.wait(t, sV)
                for k in range(8):
                    t.matmul(ps_dt[:, 0:512], wb[1][:, k, 0:32], x_ln[:, k, o_t:o_t+512],
                             start=(k == 0), stop=(k == 7))
                    ii = t.matmul(ps_dt[:, 512:1024], wb[1][:, k, 0:32], x_ln[:, k, o_t+512:o_t+1024],
                                  start=(k == 0), stop=(k == 7))
                b.inc(ii, sT)
            @block.vector
            def _(v):
                b.wait(v, sT)
                b.inc(v.tensor_copy(dtraw[:], ps_dt[:]), sV)
        b.barrier()
        b.free("wbA0"); b.free("wbA1")

        for cc in range(17):
            @block.vector
            def _(v, cc=cc):
                if cc == 0:
                    b.wait(v, sV)
                else:
                    b.wait(v, sS)
                b.ch(v, v.tensor_scalar(conv_acc[:], xbc[:, cc, 0:TT], conv_w[:, cc, 0:1], None, ALU.mult), sV)
                for j in range(1, 4):
                    b.ch(v, v.scalar_tensor_tensor(conv_acc[:], xbc[:, cc, j:j+TT], conv_w[:, cc, j:j+1],
                                                   conv_acc[:], ALU.mult, ALU.add), sV)
            @block.scalar
            def _(s, cc=cc):
                b.wait(s, sV)
                b.inc(s.activation(xcv[:, cc, :], conv_acc[:], AF.Silu,
                                   bias=conv_b[:, cc:cc+1], scale=1.0), sS)
        @block.gpsimd
        def _(g):
            b.wait(g, sS)
            b.dma(g, hbm["xcv"][:, :, tt*TT:(tt+1)*TT], xcv[:], sD)
            g.wait_ge(b.h(sD), b.val(sD))
        b.barrier()
        b.free("xbc"); b.free("conv_acc")
        x_dt_T = b.sb("x_dt_T", [128, 8, 2048], BF16)
        dtv = b.sb("dtv", [32, TT], F32)
        rw1 = b.sb("rw1", [32, TT], F32)
        dex_fm = b.sb("dex_fm", [32, TT], F32)

        @block.vector
        def _(v):
            b.wait(v, sV)
            b.inc(v.tensor_scalar(dtraw[:], dtraw[:], dt_bias[:, 0:1], None, ALU.add), sV)
        @block.scalar
        def _(s):
            b.wait(s, sV)
            b.ch(s, s.activation(rw1[:], dtraw[:], AF.Abs, bias=0.0, scale=1.0), sS)
            b.inc(s.activation(rw1[:], rw1[:], AF.Exp, bias=0.0, scale=-1.0), sS)
        @block.vector
        def _(v):
            b.wait(v, sS)
            b.inc(v.tensor_scalar_add(rw1[:], rw1[:], 1.0), sV)
        @block.scalar
        def _(s):
            b.wait(s, sV)
            b.inc(s.activation(rw1[:], rw1[:], AF.Ln, bias=0.0, scale=1.0), sS)
        @block.vector
        def _(v):
            b.wait(v, sS)
            b.ch(v, v.tensor_scalar_max(dtv[:], dtraw[:], 0.0), sV)
            b.ch(v, v.tensor_add(dtv[:], dtv[:], rw1[:]), sV)
            b.ch(v, v.tensor_scalar(rw1[:], dtv[:], A_col[:, 0:1], None, ALU.mult), sV)
            b.ch(v, v.tensor_tensor_scan(acs2[:, tt, :], adt_gate[:], rw1[:], 0.0,
                                         ALU.mult, ALU.add), sV)
            b.ch(v, v.tensor_copy(asum[:, tt*16:(tt+1)*16], acs2[:, tt, CH-1:TT:CH]), sV)
            for c in range(NCH_T):
                ii = v.tensor_scalar(dex_fm[:, c*CH:(c+1)*CH], acs2[:, tt, c*CH:(c+1)*CH],
                                     acs2[:, tt, (c+1)*CH-1:(c+1)*CH], None, ALU.subtract)
            b.inc(ii, sV)
        @block.scalar
        def _(s):
            b.wait(s, sV)
            b.inc(s.activation(dex_fm[:], dex_fm[:], AF.Exp, bias=0.0, scale=-1.0), sS)

        b.barrier()
        for nm in ("dtraw", "rw1"):
            b.free(nm)

        with nc.psum_tensor(f"ps_trf{tt}", [128, 512], F32) as ps_trf, \
             nc.psum_tensor(f"ps_trb{tt}", [128, 512], BF16) as ps_trb:
            for tb in range(8):
                @block.tensor
                def _(t, tb=tb):
                    b.wait(t, sS)
                    b.wait(t, sV)
                    t.transpose(ps_trf[:, 0:32], acs2[:, tt, tb*128:(tb+1)*128], id_f32[0:32, 0:32])
                    t.transpose(ps_trf[:, 32:64], dtv[:, tb*128:(tb+1)*128], id_f32[0:32, 0:32])
                    b.inc(t.transpose(ps_trf[:, 64:96], dex_fm[:, tb*128:(tb+1)*128], id_f32[0:32, 0:32]), sT)
                @block.vector
                def _(v, tb=tb):
                    b.wait(v, sT)
                    v.tensor_copy(acsT[:, tt*8+tb, :], ps_trf[:, 0:32])
                    v.tensor_copy(dtT[:, tt*8+tb, :], ps_trf[:, 32:64])
                    b.inc(v.tensor_copy(dexT[:, tt*8+tb, :], ps_trf[:, 64:96]), sV)
            b.barrier()
            b.free("dtv"); b.free("dex_fm")
            B_T = b.sb("B_T", [128, 8, 16], BF16)
            Bd_T = b.sb("Bd_T", [128, 8, 512], BF16)
            for tb in range(8):
                for kg in range(4):
                    @block.tensor
                    def _(t, tb=tb, kg=kg):
                        b.wait(t, sV)
                        for j in range(4):
                            ii = t.transpose(ps_trb[:, j*128:(j+1)*128],
                                             xcv[:, kg*4+j, tb*128:(tb+1)*128], id_bf[:, :])
                        b.inc(ii, sT)
                    @block.vector
                    def _(v, tb=tb, kg=kg):
                        b.wait(v, sT)
                        o4 = x_dt_T[:, tb, kg*512:(kg+1)*512].rearrange("p (a h2 c) -> p a h2 c", a=4, h2=2)
                        i4 = ps_trb[:].rearrange("p (a h2 c) -> p a h2 c", a=4, h2=2)
                        d4 = dtT[:, tt*8+tb, kg*8:kg*8+8].rearrange("p (a h2) -> p a h2", a=4) \
                            .unsqueeze(3).broadcast_to([128, 4, 2, 64])
                        b.inc(v.tensor_tensor(o4, i4, d4, ALU.mult), sV)
            for tb in range(8):
                @block.tensor
                def _(t, tb=tb):
                    b.wait(t, sV)
                    b.inc(t.transpose(ps_trb[:, 0:16], xcv[0:16, 16, tb*128:(tb+1)*128],
                                      id_bf[0:16, 0:16]), sT)
                @block.vector
                def _(v, tb=tb):
                    b.wait(v, sT)
                    b.inc(v.tensor_copy(B_T[:, tb, :], ps_trb[:, 0:16]), sV)
            for tb in range(8):
                @block.vector
                def _(v, tb=tb):
                    if tb == 0:
                        b.wait(v, sV)
                    o3 = Bd_T[:, tb, :].rearrange("p (h n) -> p h n", h=32)
                    i0 = B_T[:, tb, :].unsqueeze(1).broadcast_to([128, 32, 16])
                    i1 = dexT[:, tt*8+tb, :].unsqueeze(2).broadcast_to([128, 32, 16])
                    ii = v.tensor_tensor(o3, i0, i1, ALU.mult)
                    if tb == 7:
                        b.inc(ii, sV)

        b.barrier()
        with nc.psum_tensor(f"ps_st{tt}", [128, 4096], F32) as ps_st:
            for half in range(2):
                @block.tensor
                def _(t, half=half):
                    b.wait(t, sV)
                    if half == 1:
                        b.wait(t, sV)
                    for cl in range(8):
                        c_local = half*8 + cl
                        rb_, tb = (c_local % 2)*64, c_local // 2
                        for h in range(NH):
                            ii = t.matmul(ps_st[0:64, cl*512 + h*16: cl*512 + (h+1)*16],
                                          x_dt_T[rb_:rb_+64, tb, h*64:(h+1)*64],
                                          Bd_T[rb_:rb_+64, tb, h*16:(h+1)*16],
                                          start=True, stop=True)
                    b.inc(ii, sT)
                @block.vector
                def _(v, half=half):
                    b.wait(v, sT)
                    for cl in range(8):
                        gc = tt*16 + half*8 + cl
                        for hh in range(2):
                            dst = ST3[hh][:, :].rearrange("p (h n c) -> p h n c", h=16, n=16)[:, :, :, gc]
                            src = ps_st[0:64, cl*512+hh*256:cl*512+(hh+1)*256].rearrange("p (h n) -> p h n", h=16)
                            ii = v.tensor_copy(dst, src)
                    b.inc(ii, sV)

        for nm in ("xcv", "x_dt_T", "B_T", "Bd_T"):
            b.free(nm)
        b.barrier()

    if dbg is not None and DBG == "st3":
        @block.gpsimd
        def _(g):
            b.dma(g, dbg[0:64, 0:8192], ST3[0][:], sD)
            b.dma(g, dbg[0:64, 8192:16384], ST3[1][:], sD)
            g.wait_ge(b.h(sD), b.val(sD))
        b.barrier()
        return

    eas = b.sb("eas", [32, 32], F32)
    dprA = b.sb("dprA", [32, 32], F32)
    scanpat = b.sb("scanpat", [64, 1024], F32)
    dpref = b.sb("dpref", [64, 1024], F32)
    spfull = [b.sb("spfulla", [64, 8192], F32, top=True), b.sb("spfullb", [64, 8192], F32, top=True)]
    SC = ST3

    @block.scalar
    def _(s):
        b.wait(s, sV)
        b.inc(s.activation(eas[:], asum[:], AF.Exp, bias=0.0, scale=1.0), sS)
    @block.vector
    def _(v):
        b.wait(v, sV)
        b.ch(v, v.memset(dprA[:, 0:1], 0.0), sV)
        b.ch(v, v.tensor_copy(dprA[:, 1:32], asum[:, 0:31]), sV)
        b.ch(v, v.tensor_tensor_scan(dprA[:], adt_gate[:, 1:33], dprA[:], 0.0, ALU.mult, ALU.add), sV)
    @block.scalar
    def _(s):
        b.wait(s, sV)
        b.inc(s.activation(dprA[:], dprA[:], AF.Exp, bias=0.0, scale=1.0), sS)
    @block.gpsimd
    def _(g):
        b.wait(g, sS)
        b.dma(g, hbm["flat"][0:1, 0:1024], eas[:], sD)
        b.dma(g, hbm["flat"][0:1, 1024:2048], dprA[:], sD)
        g.wait_ge(b.h(sD), b.val(sD))
        b.dma(g, scanpat[0:1, :], hbm["flat"][0:1, 0:1024], sD)
        b.dma(g, dpref[0:1, :], hbm["flat"][0:1, 1024:2048], sD)
    with nc.psum_tensor("ps_bcg", [128, 1024], F32) as ps_bcg:
        @block.tensor
        def _(t):
            b.wait(t, sD)
            t.matmul(ps_bcg[0:64, 0:512], ones_f32r[0:1, 0:64], scanpat[0:1, 0:512], start=True, stop=True)
            b.inc(t.matmul(ps_bcg[0:64, 512:1024], ones_f32r[0:1, 0:64], scanpat[0:1, 512:1024], start=True, stop=True), sT)
        @block.vector
        def _(v):
            b.wait(v, sT)
            b.inc(v.tensor_copy(scanpat[:, :], ps_bcg[0:64, :]), sV)
        @block.tensor
        def _(t):
            b.wait(t, sV)
            t.matmul(ps_bcg[0:64, 0:512], ones_f32r[0:1, 0:64], dpref[0:1, 0:512], start=True, stop=True)
            b.inc(t.matmul(ps_bcg[0:64, 512:1024], ones_f32r[0:1, 0:64], dpref[0:1, 512:1024], start=True, stop=True), sT)
        @block.vector
        def _(v):
            b.wait(v, sT)
            b.ch(v, v.tensor_copy(dpref[:, :], ps_bcg[0:64, :]), sV)
            zc = scanpat[:, :].rearrange("p (h c) -> p h c", h=32)[:, :, 0:1]
            b.ch(v, v.memset(zc, 0.0), sV)
            for hh in range(2):
                o4 = spfull[hh][:, :].rearrange("p (h n c) -> p h n c", h=16, n=16)
                i4 = scanpat[:, :].rearrange("p (h c) -> p h c", h=32)[:, hh*16:(hh+1)*16, :] \
                    .unsqueeze(2).broadcast_to([64, 16, 16, 32])
                b.ch(v, v.tensor_copy(o4, i4), sV)
                ii = v.tensor_tensor_scan(SC[hh][:], spfull[hh][:], ST3[hh][:], 0.0, ALU.mult, ALU.add)
                b.ch(v, ii, sV)

    cstage = b.sb("cstage", [64, 512], F32)
    allg = b.sb("allg", [64, 2, 512], F32)
    @block.vector
    def _(v):
        b.wait(v, sV)
        for hh in range(2):
            ii = v.tensor_copy(cstage[:, hh*256:(hh+1)*256].rearrange("p (h n) -> p h n", h=16),
                               SC[hh][:, :].rearrange("p (h n c) -> p h n c", h=16, n=16)[:, :, :, 31])
        b.inc(ii, sV)
    @block.sync
    def _(sy):
        b.wait(sy, sV)
        b.dma(sy, hbm["cc_min"], cstage[:], sDso)
    @block.gpsimd
    def _(g):
        b.wait(g, sDso)
        b.inc(g.collective_compute("AllGather", ALU.bypass, replica_groups=[list(range(NCC))],
                                   ins=[hbm["cc_min"]], outs=[hbm["cc_mout"]]), sCC)
    @block.vector
    def _(v):
        b.ch(v, v.memset(state_in[:], 0.0), sV)
    for r in range(NCC):
        @block.gpsimd
        def _(g, r=r):
            if r == 0:
                b.wait(g, sCC)
            else:
                b.wait(g, sV)
            b.dma(g, allg[:, r % 2, :], hbm["cc_mout"][r*64:(r+1)*64, :], sD)
        @block.vector
        def _(v, r=r):
            b.wait(v, sD)
            b.ch(v, v.scalar_tensor_tensor(state_in[:], allg[:, r % 2, :], m8_64[:, r:r+1],
                                           state_in[:], ALU.mult, ALU.add), sV)

    SCP = spfull
    SCP_bf = [None, None]
    for hh in range(2):
        @block.vector
        def _(v, hh=hh):
            b.wait(v, sV)
            v4 = SCP[hh][:, :].rearrange("p (h n c) -> p h n c", h=16, n=16)
            s4 = SC[hh][:, :].rearrange("p (h n c) -> p h n c", h=16, n=16)
            b.ch(v, v.memset(v4[:, :, :, 0:1], 0.0), sV)
            b.ch(v, v.tensor_copy(v4[:, :, :, 1:32], s4[:, :, :, 0:31]), sV)
            i0 = state_in[:, hh*256:(hh+1)*256].rearrange("p (h n) -> p h n", h=16) \
                .unsqueeze(3).broadcast_to([64, 16, 16, 32])
            i1 = dpref[:, :].rearrange("p (h c) -> p h c", h=32)[:, hh*16:(hh+1)*16, :] \
                .unsqueeze(2).broadcast_to([64, 16, 16, 32])
            b.ch(v, v.tensor_tensor(s4, i0, i1, ALU.mult), sV)
            b.ch(v, v.tensor_add(SCP[hh][:], SCP[hh][:], SC[hh][:]), sV)
        b.free("ST3a" if hh == 0 else "ST3b")
        SCP_bf[hh] = b.sb("SCP_bfa" if hh == 0 else "SCP_bfb", [64, 8192], BF16, top=True)
        @block.vector
        def _(v, hh=hh):
            b.ch(v, v.tensor_copy(SCP_bf[hh][:], SCP[hh][:]), sV)

    if dbg is not None and DBG == "scp":
        @block.gpsimd
        def _(g):
            b.wait(g, sV)
            b.dma(g, dbg[0:64, 0:8192], SCP[0][:], sD)
            b.dma(g, dbg[0:64, 8192:16384], SCP[1][:], sD)
            g.wait_ge(b.h(sD), b.val(sD))
        b.barrier()
        return

    for nm in ("spfulla", "spfullb", "scanpat", "dpref", "cstage", "allg", "eas", "dprA", "adt_gate"):
        b.free(nm)
    b.barrier()

    z_both = b.sb("z_both", [128, 32, TT], BF16)
    for tz in range(2):
        wb = [b.sb(f"wbZ{i}", [128, 8, 128], BF16) for i in range(2)]
        tcounts = {}
        loop_t0 = b.val(sT)
        with nc.psum_tensor(f"ps_zz{tz}", [128, 1024], F32) as ps_z:
            for m in range(16):
                buf = wb[m % 2]
                @block.gpsimd
                def _(g, m=m, buf=buf, tz=tz):
                    g.wait_ge(b.h(sT), tcounts[m - 2] if m >= 2 else loop_t0)
                    if m < 2:
                        b.wait(g, sV)
                    b.dma(g, buf[:], io["wins"][m].rearrange("(k p) j -> p k j", p=128), sDw[m % 2])
                @block.tensor
                def _(t, m=m, buf=buf, tz=tz):
                    b.wait(t, sDw[m % 2])
                    b.wait(t, sS)
                    o_t = HALO + tz * TT
                    for k in range(8):
                        t.matmul(ps_z[:, 0:512], buf[:, k, :], x_ln[:, k, o_t:o_t+512],
                                 start=(k == 0), stop=(k == 7))
                        ii = t.matmul(ps_z[:, 512:1024], buf[:, k, :], x_ln[:, k, o_t+512:o_t+1024],
                                      start=(k == 0), stop=(k == 7))
                    tcounts[m] = b.inc(ii, sT)
                @block.scalar
                def _(s, m=m, tz=tz):
                    b.wait(s, sT)
                    b.inc(s.activation(z_both[:, tz*16 + m, :], ps_z[:], AF.Silu, bias=0.0, scale=1.0), sS)
        b.free("wbZ0"); b.free("wbZ1")
    b.barrier()
    b.free("x_ln")

    for tt in range(2):
        y_bf = b.sb("y_bf", [128, 16, TT], BF16)
        wb = [b.sb(f"wbB{i}", [128, 8, 128], BF16) for i in range(2)]
        xp = b.sb("xp", [128, 17, 128], BF16)
        xdtp = b.sb("xdtp", [128, 4096], BF16)
        c0tmp = b.sb("c0tmp", [128, 16], BF16)
        C0p = b.sb("C0p", [16, 128], BF16)
        R1E = b.sb("R1E", [128, 2048], F32)
        Gt = b.sb("Gt", [128, 2048], BF16)
        ssm = b.sb("ssm", [128, 64], F32)
        smt = b.sb("smt", [64, 128], F32)
        Cdp = b.sb("Cdp", [16, 4096], BF16)
        ebcp = b.sb("ebcp", [16, 4096], BF16)
        stTp = b.sb("stTp", [16, 4096], BF16)

        @block.vector
        def _(v):
            b.ch(v, v.memset(xdtp[:], 0.0), sV)
            b.ch(v, v.memset(stTp[:], 0.0), sV)

        eacs = b.sb("eacs", [32, TT], BF16)
        @block.scalar
        def _(s):
            b.wait(s, sV)
            b.inc(s.activation(eacs[:], acs2[:, tt, :], AF.Exp, bias=0.0, scale=1.0), sS)
        @block.gpsimd
        def _(g):
            b.wait(g, sS)
            b.dma(g, hbm["acsf"][0:1, :].rearrange("a (c h l) -> (a h) c l", c=16, h=32),
                  acs2[:, tt, :].rearrange("p (c l) -> p c l", c=16), sD)
            b.dma(g, hbm["acsf2"][0:1, :].rearrange("a (c h l) -> (a h) c l", c=16, h=32),
                  eacs[:, :].rearrange("p (c l) -> p c l", c=16), sD)
            g.wait_ge(b.h(sD), b.val(sD))
            b.inc(g.nop(), sG)
        b.free("eacs")

        with nc.psum_tensor(f"ps_sc{tt}", [64, 128], F32) as ps_sc, \
             nc.psum_tensor(f"ps_trp{tt}", [128, 512], BF16) as ps_trp, \
             nc.psum_tensor(f"ps_y{tt}", [128, 2048], F32) as ps_y, \
             nc.psum_tensor(f"ps_sm{tt}", [128, 64], F32) as ps_sm, \
             nc.psum_tensor(f"ps_sT{tt}", [16, 512], BF16) as ps_sT:
            for cp in range(8):
                @block.gpsimd
                def _(g, cp=cp):
                    b.wait(g, sV)
                    b.wait(g, sG)
                    b.wait(g, sD)
                    b.dma(g, xp[:], hbm["xcv"][:, :, tt*TT + cp*128: tt*TT + (cp+1)*128], sD)
                    b.dma(g, ebcp[:, 0:2048], hbm["acsf2"][0:1, (cp*2)*2048:(cp*2+1)*2048].broadcast_to([16, 2048]), sD)
                    b.dma(g, ebcp[:, 2048:4096], hbm["acsf2"][0:1, (cp*2+1)*2048:(cp*2+2)*2048].broadcast_to([16, 2048]), sD)
                    b.dma(g, R1E[0:64, :], hbm["acsf"][0:1, (cp*2)*2048:(cp*2+1)*2048].broadcast_to([64, 2048]), sD)
                    b.dma(g, R1E[64:128, :], hbm["acsf"][0:1, (cp*2+1)*2048:(cp*2+2)*2048].broadcast_to([64, 2048]), sD)
                for kg in range(4):
                    @block.tensor
                    def _(t, cp=cp, kg=kg):
                        b.wait(t, sD)
                        b.wait(t, sV)
                        for j in range(4):
                            ii = t.transpose(ps_trp[:, j*128:(j+1)*128], xp[:, kg*4+j, :], id_bf[:, :])
                        b.inc(ii, sT)
                    @block.vector
                    def _(v, cp=cp, kg=kg):
                        b.wait(v, sT)
                        base = kg * 1024
                        for h2 in range(2):
                            o3 = xdtp[:, base:base+1024].rearrange("p (a q) -> p a q", a=4)[:, :, h2*192:h2*192+64]
                            i3 = ps_trp[:, :].rearrange("p (a q) -> p a q", a=4)[:, :, h2*64:(h2+1)*64]
                            d3 = dtT[:, tt*8+cp, kg*8 + h2: kg*8 + 8: 2].unsqueeze(2).broadcast_to([128, 4, 64])
                            ii = v.tensor_tensor(o3, i3, d3, ALU.mult)
                        b.inc(ii, sV)
                @block.tensor
                def _(t, cp=cp):
                    b.wait(t, sV)
                    b.inc(t.transpose(ps_trp[:, 0:16], xp[32:48, 16, :], id_bf[32:48, 32:48]), sT)
                @block.vector
                def _(v, cp=cp):
                    b.wait(v, sT)
                    b.inc(v.tensor_copy(c0tmp[:], ps_trp[:, 0:16]), sV)
                @block.tensor
                def _(t, cp=cp):
                    b.wait(t, sV)
                    b.inc(t.transpose(ps_trp[0:16, 128:256], c0tmp[:], id_bf[:, :]), sT)
                @block.vector
                def _(v, cp=cp):
                    b.wait(v, sT)
                    b.inc(v.tensor_copy(C0p[:], ps_trp[0:16, 128:256]), sV)
                @block.tensor
                def _(t, cp=cp):
                    b.wait(t, sV)
                    b.wait(t, sG)
                    for ci in range(2):
                        ii = t.matmul(ps_sc[0:64, ci*64:(ci+1)*64],
                                      xp[0:16, 16, ci*64:(ci+1)*64],
                                      C0p[0:16, ci*64:(ci+1)*64],
                                      start=True, stop=True)
                    b.inc(ii, sT)
                @block.vector
                def _(v, cp=cp):
                    b.wait(v, sT)
                    b.inc(v.tensor_tensor(smt[:], ps_sc[0:64, :], maskST[:], ALU.mult), sV)
                @block.tensor
                def _(t, cp=cp):
                    b.wait(t, sV)
                    t.matmul(ps_sm[:, 0:64], shl_sb[0:64, 0:128], smt[0:64, 0:64],
                             start=True, stop=False)
                    b.inc(t.matmul(ps_sm[:, 0:64], shl_sb[0:64, 128:256], smt[0:64, 64:128],
                                   start=False, stop=True), sT)
                @block.vector
                def _(v, cp=cp):
                    b.wait(v, sT)
                    b.ch(v, v.tensor_copy(ssm[:], ps_sm[:, 0:64]), sV)
                    b.wait(v, sD)
                    o3 = Cdp[:, :].rearrange("p (ci h l) -> p ci h l", ci=2, h=32)
                    i0 = C0p[:, :].rearrange("p (ci l) -> p ci l", ci=2).unsqueeze(2).broadcast_to([16, 2, 32, 64])
                    i1 = ebcp[:, :].rearrange("p (ci h l) -> p ci h l", ci=2, h=32)
                    b.inc(v.tensor_tensor(o3, i0, i1, ALU.mult), sV)
                @block.vector
                def _(v, cp=cp):
                    b.wait(v, sD)
                    b.wait(v, sV)
                    i1 = acsT[:, tt*8+cp, :].unsqueeze(2).broadcast_to([128, 32, 64])
                    b.ch(v, v.tensor_tensor(R1E[:, :].rearrange("p (h l) -> p h l", h=32),
                                            R1E[:, :].rearrange("p (h l) -> p h l", h=32), i1,
                                            ALU.subtract), sV)
                    b.inc(v.tensor_scalar_min(R1E[:], R1E[:], 0.0), sV)
                @block.scalar
                def _(s, cp=cp):
                    b.wait(s, sV)
                    b.inc(s.activation(R1E[:], R1E[:], AF.Exp, bias=0.0, scale=1.0), sS)
                @block.vector
                def _(v, cp=cp):
                    b.wait(v, sS)
                    i1 = ssm[:, :].unsqueeze(1).broadcast_to([128, 32, 64])
                    b.inc(v.tensor_tensor(Gt[:, :].rearrange("p (h l) -> p h l", h=32),
                                          R1E[:, :].rearrange("p (h l) -> p h l", h=32), i1,
                                          ALU.mult), sV)
                for ci in range(2):
                    gc = tt*16 + cp*2 + ci
                    for hg in range(4):
                        @block.tensor
                        def _(t, gc=gc, hg=hg):
                            b.wait(t, sV)
                            for hh in range(8):
                                h = hg*8 + hh
                                src = SCP_bf[h // 16][:, :].rearrange("p (h n c) -> p h n c", h=16, n=16)[:, h % 16, :, gc]
                                ii = t.transpose(ps_sT[0:16, hh*64:(hh+1)*64], src, id_bf[0:64, 0:64])
                            b.inc(ii, sT)
                        @block.vector
                        def _(v, hg=hg):
                            b.wait(v, sT)
                            for h2 in range(2):
                                dsl = stTp[0:16, hg*1024:(hg+1)*1024] \
                                    .rearrange("p (hp q) -> p hp q", hp=4)[:, :, h2*192:h2*192+64]
                                ssl = ps_sT[0:16, :].rearrange("p (hp q) -> p hp q", hp=4)[:, :, h2*64:(h2+1)*64]
                                ii = v.tensor_copy(dsl, ssl)
                            b.inc(ii, sV)
                    @block.tensor
                    def _(t, cp=cp, ci=ci):
                        b.wait(t, sV)
                        for hp in range(16):
                            co = ci*1024 + hp*64
                            for h2 in range(2):
                                h = 2*hp + h2
                                t.matmul(ps_y[:, co:co+64],
                                         xdtp[ci*64:(ci+1)*64, hp*256 + h2*128: hp*256 + (h2+1)*128],
                                         Gt[ci*64:(ci+1)*64, h*64:(h+1)*64],
                                         start=(h2 == 0), stop=False)
                                ii = t.matmul(ps_y[:, co:co+64],
                                              stTp[0:16, hp*256 + h2*128: hp*256 + (h2+1)*128],
                                              Cdp[0:16, ci*2048 + h*64: ci*2048 + (h+1)*64],
                                              start=False, stop=(h2 == 1))
                        b.inc(ii, sT)
                @block.vector
                def _(v, cp=cp):
                    b.wait(v, sT)
                    for hp in range(16):
                        src = ps_y[:, :].rearrange("p (ci q l) -> p ci q l", ci=2, q=16)[:, :, hp, :]
                        ysl = y_bf[:, hp, cp*128:(cp+1)*128].rearrange("p (ci l) -> p ci l", ci=2)
                        xsl = xp[:, hp, :].rearrange("p (ci l) -> p ci l", ci=2)
                        zsl = z_both[:, tt*16 + hp, cp*128:(cp+1)*128].rearrange("p (ci l) -> p ci l", ci=2)
                        b.ch(v, v.scalar_tensor_tensor(ysl, xsl, D0[:, hp:hp+1], src, ALU.mult, ALU.add), sV)
                        b.ch(v, v.tensor_tensor(ysl, ysl, zsl, ALU.mult), sV)

        if dbg is not None and DBG == "pair0" and tt == 0:
            @block.gpsimd
            def _(g):
                b.wait(g, sV)
                b.dma(g, dbg[:, 0:2048], R1E[:], sD)
                b.dma(g, dbg[:, 2048:4096], Gt[:], sD)
                b.dma(g, dbg[0:16, 4096:8192], Cdp[:], sD)
                b.dma(g, dbg[0:16, 8192:12288], stTp[:], sD)
                b.dma(g, dbg[:, 12288:14336], xdtp[:], sD)
                b.dma(g, dbg[:, 14336:14400], ssm[:], sD)
                b.dma(g, dbg[0:16, 14400:14528], C0p[:], sD)
                for hp in range(4):
                    b.dma(g, dbg[:, 14528 + hp*128: 14528 + (hp+1)*128], y_bf[:, hp, 0:128], sD)
                b.dma(g, dbg[:, 15040: 15040 + 17*128], xp[:], sD)
                b.dma(g, dbg[0:32, 17216:17216+1024], acs2[:, 0, :], sD)
                g.wait_ge(b.h(sD), b.val(sD))
            b.barrier()
            return

        if dbg is not None and DBG == "y" and tt == 0:
            @block.gpsimd
            def _(g):
                b.wait(g, sV)
                for kc in range(16):
                    b.dma(g, dbg[:, kc*1024:(kc+1)*1024], y_bf[:, kc, :], sD)
                g.wait_ge(b.h(sD), b.val(sD))
            b.barrier()
            return

        for nm in ("xp", "xdtp", "c0tmp", "C0p", "R1E", "Gt", "ssm", "smt",
                   "Cdp", "ebcp", "stTp"):
            b.free(nm)
        sqb = b.sb("sqbB", [128, 16, TT], BF16)
        rrow = b.sb("rrowB", [1, TT], F32)
        rmsb = b.sb("rmsb", [128, TT], F32)
        @block.scalar
        def _(s):
            b.wait(s, sV)
            for kc in range(16):
                ii = s.activation(sqb[:, kc, :], y_bf[:, kc, :], AF.Square, bias=0.0, scale=1.0)
            b.inc(ii, sS)
        with nc.psum_tensor(f"ps_r{tt}", [1, 1024], F32) as ps_r, \
             nc.psum_tensor(f"ps_rb{tt}", [128, 1024], F32) as ps_rb:
            @block.tensor
            def _(t):
                b.wait(t, sS)
                for (o, n) in ((0, 512), (512, 512)):
                    for kc in range(16):
                        ii = t.matmul(ps_r[0:1, o:o+n], ones_bf[:, 0:1], sqb[:, kc, o:o+n],
                                      start=(kc == 0), stop=(kc == 15))
                b.inc(ii, sT)
            @block.vector
            def _(v):
                b.wait(v, sT)
                b.ch(v, v.tensor_copy(rrow[:], ps_r[:]), sV)
                b.ch(v, v.tensor_scalar_mul(rrow[:], rrow[:], 1.0 / DI), sV)
                b.inc(v.tensor_scalar_add(rrow[:], rrow[:], 1e-5), sV)
            @block.scalar
            def _(s):
                b.wait(s, sV)
                b.inc(s.activation(rrow[:], rrow[:], AF.Sqrt, bias=0.0, scale=1.0), sS)
            @block.vector
            def _(v):
                b.wait(v, sS)
                b.inc(v.reciprocal(rrow[:], rrow[:]), sV)
            @block.tensor
            def _(t):
                b.wait(t, sV)
                t.matmul(ps_rb[:, 0:512], ones_f32r[0:1, :], rrow[0:1, 0:512], start=True, stop=True)
                b.inc(t.matmul(ps_rb[:, 512:1024], ones_f32r[0:1, :], rrow[0:1, 512:1024], start=True, stop=True), sT)
            @block.vector
            def _(v):
                b.wait(v, sT)
                b.ch(v, v.tensor_copy(rmsb[:], ps_rb[:]), sV)
                for kc in range(16):
                    b.ch(v, v.scalar_tensor_tensor(y_bf[:, kc, :], y_bf[:, kc, :],
                                                   norm_w[:, kc:kc+1], rmsb[:], ALU.mult, ALU.mult), sV)

        for nm in ("sqbB", "rrowB", "rmsb"):
            b.free(nm)
        u_t = b.sb("u_tB", [128, 8, TT], F32)
        x1sb = b.sb("x1sb", [128, TT], F32)
        wo = [b.sb(f"woB{i}", [128, 16, 128], BF16) for i in range(2)]
        @block.sync
        def _(sy):
            b.wait(sy, sV)
            b.wait(sy, sS)
            b.dma(sy, u_t[:], io["u_fm"][:, :, HALO + tt*TT: HALO + (tt+1)*TT], sDs)
        tcounts = {}
        loop_t0 = b.val(sT)
        with nc.psum_tensor(f"ps_wo{tt}", [128, 1024], F32) as ps_wo:
            for m in range(8):
                buf = wo[m % 2]
                @block.gpsimd
                def _(g, m=m, buf=buf):
                    g.wait_ge(b.h(sT), tcounts[m - 2] if m >= 2 else loop_t0)
                    if m < 2:
                        b.wait(g, sV)
                    b.dma(g, buf[:], io["wouts"][m].rearrange("(k p) j -> p k j", p=128), sDw[m % 2])
                @block.tensor
                def _(t, m=m, buf=buf):
                    b.wait(t, sDw[m % 2])
                    if m >= 1:
                        b.wait(t, sV)
                    for k in range(16):
                        t.matmul(ps_wo[:, 0:512], buf[:, k, :], y_bf[:, k, 0:512],
                                 start=(k == 0), stop=(k == 15))
                        ii = t.matmul(ps_wo[:, 512:1024], buf[:, k, :], y_bf[:, k, 512:1024],
                                      start=(k == 0), stop=(k == 15))
                    tcounts[m] = b.inc(ii, sT)
                @block.vector
                def _(v, m=m):
                    b.wait(v, sT)
                    b.wait(v, sDs)
                    b.wait(v, sDso)
                    b.inc(v.tensor_add(x1sb[:], ps_wo[:], u_t[:, m, :]), sV)
                @block.sync
                def _(sy, m=m):
                    b.wait(sy, sV)
                    b.dma(sy, hbm["x1"][:, m, tt*TT:(tt+1)*TT], x1sb[:], sDso)

        @block.sync
        def _(sy):
            b.wait(sy, sDso)
        for nm in ("y_bf", "wbB0", "wbB1", "u_tB", "x1sb", "woB0", "woB1"):
            b.free(nm)
        b.barrier()

    for nm in ("acs2", "acsT", "dtT", "dexT", "asum", "state_in", "SCP_bfa", "SCP_bfb", "z_both"):
        b.free(nm)
    b.barrier()

def emit_ffn0(b, io, hbm, C, lns):
    nc, block = b.nc, b.block
    sD, sDs, sT, sV, sS, sG, sCC = C["sems"]
    sDw = C["sDw"]; sDso = C["sDso"]

    s1c = b.sb("s1c", [128, 32], F32)
    s2c = b.sb("s2c", [128, 32], F32)
    sh1c = b.sb("sh1c", [128, 32], F32)
    sh2c = b.sb("sh2c", [128, 32], F32)
    s3c = b.sb("s3c", [128, 8], F32)

    @block.gpsimd
    def _(g):
        for dst, src in ((s1c, "s1"), (s2c, "s2"), (sh1c, "sh1"), (sh2c, "sh2"), (s3c, "s3")):
            b.dma(g, dst[:], io[src], sD)
        g.wait_ge(b.h(sD), b.val(sD))
        b.inc(g.nop(), sG)
    b.barrier()

    for tt in range(2):
        x_ln2 = b.sb("f0xln", [128, 8, TT], BF16)
        emit_ln(b, hbm["x1"][:, :, tt*TT:(tt+1)*TT], x_ln2, TT, lns, f"lf0{tt}")

        hsb = b.sb("f0h", [128, 32, TT], BF16)
        wA = [b.sb(f"f0wa{i}", [128, 8, 128], BF16) for i in range(2)]
        wB = [b.sb(f"f0wb{i}", [128, 8, 128], BF16) for i in range(2)]
        sa_t = b.sb("f0sa", [128, 2, TT], F32)
        tcounts = {}; vcounts = {}; scounts = {}
        t0_ = b.val(sT)
        with nc.psum_tensor(f"f0psA{tt}", [128, 2048], F32) as psA, \
             nc.psum_tensor(f"f0psB{tt}", [128, 2048], F32) as psB:
            for m in range(32):
                mh = m % 2
                @block.gpsimd
                def _(g, m=m, mh=mh):
                    g.wait_ge(b.h(sT), tcounts[m-2] if m >= 2 else t0_)
                    b.dma(g, wA[mh][:], io["w1s"][m].rearrange("(k p) j -> p k j", p=128), sDw[mh])
                    b.dma(g, wB[mh][:], io["w2s"][m].rearrange("(k p) j -> p k j", p=128), sDw[mh])
                @block.tensor
                def _(t, m=m, mh=mh):
                    b.wait(t, sDw[mh])
                    if m >= 2:
                        t.wait_ge(b.h(sS), scounts[m-2])
                        t.wait_ge(b.h(sV), vcounts[m-2])
                    for th in range(2):
                        o = mh*1024 + th*512
                        for k in range(8):
                            t.matmul(psA[:, o:o+512], wA[mh][:, k, :],
                                     x_ln2[:, k, th*512:(th+1)*512], start=(k == 0), stop=(k == 7))
                    for th in range(2):
                        o = mh*1024 + th*512
                        for k in range(8):
                            ii = t.matmul(psB[:, o:o+512], wB[mh][:, k, :],
                                          x_ln2[:, k, th*512:(th+1)*512], start=(k == 0), stop=(k == 7))
                    tcounts[m] = b.inc(ii, sT)
                @block.scalar
                def _(s, m=m, mh=mh):
                    b.wait(s, sT)
                    if m >= 2:
                        s.wait_ge(b.h(sV), vcounts[m-2])
                    scounts[m] = b.inc(s.activation(sa_t[:, mh, :], psA[:, mh*1024:(mh+1)*1024],
                                                    AF.Silu, bias=0.0, scale=s1c[:, m:m+1]), sS)
                @block.vector
                def _(v, m=m, mh=mh):
                    b.wait(v, sS)
                    vcounts[m] = b.inc(v.scalar_tensor_tensor(hsb[:, m, :], psB[:, mh*1024:(mh+1)*1024],
                                                              s2c[:, m:m+1], sa_t[:, mh, :],
                                                              ALU.mult, ALU.mult), sV)
        b.barrier()
        for nm in ("f0xln", "f0wa0", "f0wa1", "f0wb0", "f0wb1", "f0sa"):
            b.free(nm)

        prev = hsb
        for (ph, wkey, sc) in (("B", "h1s", sh1c), ("C", "h2s", sh2c)):
            gout = b.sb(f"f0g{ph}", [128, 32, TT], BF16)
            wH = [b.sb(f"f0wh{ph}{i}", [128, 32, 128], BF16) for i in range(2)]
            tcounts = {}; scounts = {}
            t0_ = b.val(sT)
            with nc.psum_tensor(f"f0psg{ph}{tt}", [128, 2048], F32) as psG:
                for m in range(32):
                    mh = m % 2
                    @block.gpsimd
                    def _(g, m=m, mh=mh, wkey=wkey):
                        g.wait_ge(b.h(sT), tcounts[m-2] if m >= 2 else t0_)
                        b.dma(g, wH[mh][:], io[wkey][m].rearrange("(k p) j -> p k j", p=128), sDw[mh])
                    @block.tensor
                    def _(t, m=m, mh=mh, prev=prev):
                        b.wait(t, sDw[mh])
                        if m >= 2:
                            t.wait_ge(b.h(sS), scounts[m-2])
                        for th in range(2):
                            o = mh*1024 + th*512
                            for k in range(32):
                                ii = t.matmul(psG[:, o:o+512], wH[mh][:, k, :],
                                              prev[:, k, th*512:(th+1)*512], start=(k == 0), stop=(k == 31))
                        tcounts[m] = b.inc(ii, sT)
                    @block.scalar
                    def _(s, m=m, mh=mh, sc=sc, gout=gout):
                        b.wait(s, sT)
                        scounts[m] = b.inc(s.activation(gout[:, m, :], psG[:, mh*1024:(mh+1)*1024],
                                                        AF.Gelu, bias=0.0, scale=sc[:, m:m+1]), sS)
            b.barrier()
            b.free("f0h" if ph == "B" else "f0gB")
            b.free(f"f0wh{ph}0"); b.free(f"f0wh{ph}1")
            prev = gout

        g2 = prev
        x1t = b.sb("f0x1t", [128, 8, TT], F32)
        x1o = [b.sb(f"f0x1o{i}", [128, TT], F32) for i in range(2)]
        wO = [b.sb(f"f0wo{i}", [128, 32, 128], BF16) for i in range(2)]
        @block.sync
        def _(sy):
            b.dma(sy, x1t[:], hbm["x1"][:, :, tt*TT:(tt+1)*TT], sDs)
        tcounts = {}; vcounts = {}; socounts = {}
        t0_ = b.val(sT)
        with nc.psum_tensor(f"f0psO{tt}", [128, 2048], F32) as psO:
            for m in range(8):
                mh = m % 2
                @block.gpsimd
                def _(g, m=m, mh=mh):
                    g.wait_ge(b.h(sT), tcounts[m-2] if m >= 2 else t0_)
                    b.dma(g, wO[mh][:], io["w3s"][m].rearrange("(k p) j -> p k j", p=128), sDw[mh])
                @block.tensor
                def _(t, m=m, mh=mh):
                    b.wait(t, sDw[mh])
                    if m >= 2:
                        t.wait_ge(b.h(sV), vcounts[m-2])
                    for th in range(2):
                        o = mh*1024 + th*512
                        for k in range(32):
                            ii = t.matmul(psO[:, o:o+512], wO[mh][:, k, :],
                                          g2[:, k, th*512:(th+1)*512], start=(k == 0), stop=(k == 31))
                    tcounts[m] = b.inc(ii, sT)
                @block.vector
                def _(v, m=m, mh=mh):
                    b.wait(v, sT)
                    b.wait(v, sDs)
                    if m >= 2:
                        b.wait(v, sDso, socounts[m-2])
                    vcounts[m] = b.inc(v.scalar_tensor_tensor(x1o[mh][:], psO[:, mh*1024:(mh+1)*1024],
                                                              s3c[:, m:m+1], x1t[:, m, :],
                                                              ALU.mult, ALU.add), sV)
                @block.sync
                def _(sy, m=m, mh=mh):
                    b.wait(sy, sV)
                    if m >= 1:
                        sy.wait_ge(b.h(sDso), socounts[m-1])
                    socounts[m] = b.dma(sy, hbm["x1"][:, m, tt*TT:(tt+1)*TT], x1o[mh][:], sDso)
        @block.sync
        def _(sy):
            b.wait(sy, sDso)
        b.barrier()
        for nm in ("f0gC", "f0x1t", "f0x1o0", "f0x1o1", "f0wo0", "f0wo1"):
            b.free(nm)
    for nm in ("s1c", "s2c", "sh1c", "sh2c", "s3c"):
        b.free(nm)
    b.barrier()


def emit_lru(b, io, hbm, C, lns):
    nc, block = b.nc, b.block
    sD, sDs, sT, sV, sS, sG, sCC = C["sems"]
    sDso = C["sDso"]
    l1A = C["l1A"]; l1B = C["l1B"]; l1C = C["l1C"]; l1D = C["l1D"]
    m8_128 = C["m8_128"]

    NZL = T // TC1
    u = b.sb("lru_u", [128, 8, T], F32)
    emit_ln(b, hbm["x1"], u, T, lns, "lnlru", u_buf=u)

    Apr = b.sb("lApr", [128, 18, 8], F32); Api = b.sb("lApi", [128, 18, 8], F32)
    krt = b.sb("lkr", [128, 8, 8], F32)
    Wr2 = b.sb("lWr", [128, 8, 8], F32); Wi2 = b.sb("lWi", [128, 8, 8], F32)
    Prt = b.sb("lPr", [128, 8, 8], F32); Pnt = b.sb("lPn", [128, 8, 8], F32)
    D1r = b.sb("lD1r", [128, 8, 17], F32); D1i = b.sb("lD1i", [128, 8, 17], F32)
    D2qr = b.sb("lD2qr", [128, 8, 16], F32); D2qi = b.sb("lD2qi", [128, 8, 16], F32)
    ta8 = b.sb("lta8", [128, 8, 8], F32); tb8 = b.sb("ltb8", [128, 8, 8], F32)
    CBr = b.sb("lCBr", [128, 8], F32); CBi = b.sb("lCBi", [128, 8], F32)
    hin = b.sb("lhin", [128, 16], F32)
    Ar = l1A[:, 0:8]; Ai = l1A[:, 8:16]
    Br = l1B[:, 0:8]; Bi = l1B[:, 8:16]
    Cr = l1C[:, 0:8]; Ci = l1C[:, 8:16]

    def cmulb(v, outr, outi, ar, ai, br, bi, ta, tb):
        b.ch(v, v.tensor_tensor(ta, ar, br, ALU.mult), sV)
        b.ch(v, v.tensor_tensor(tb, ai, bi, ALU.mult), sV)
        b.ch(v, v.tensor_sub(outr, ta, tb), sV)
        b.ch(v, v.tensor_tensor(ta, ar, bi, ALU.mult), sV)
        b.ch(v, v.tensor_tensor(tb, ai, br, ALU.mult), sV)
        b.ch(v, v.tensor_add(outi, ta, tb), sV)

    @block.vector
    def _(v):
        b.ch(v, v.memset(Apr[:, 0, :], 1.0), sV)
        b.ch(v, v.memset(Api[:, 0, :], 0.0), sV)
        b.ch(v, v.tensor_copy(Apr[:, 1, :], Ar), sV)
        b.ch(v, v.tensor_copy(Api[:, 1, :], Ai), sV)
        for s in (1, 2, 4, 8, 16):
            n = min(s, 17 - s)
            if n <= 0:
                continue
            br_ = Apr[:, s, :].unsqueeze(1).broadcast_to([128, n, 8])
            bi_ = Api[:, s, :].unsqueeze(1).broadcast_to([128, n, 8])
            cmulb(v, Apr[:, s+1:s+1+n, :], Api[:, s+1:s+1+n, :],
                  Apr[:, 1:1+n, :], Api[:, 1:1+n, :], br_, bi_,
                  ta8[:, 0:n, :], tb8[:, 0:n, :])
        cmulb(v, CBr[:], CBi[:], Cr, Ci, Br, Bi, ta8[:, 0, :], tb8[:, 0, :])
        cbr_b = CBr[:, :].unsqueeze(1).broadcast_to([128, 8, 8])
        cbi_b = CBi[:, :].unsqueeze(1).broadcast_to([128, 8, 8])
        b.ch(v, v.tensor_tensor(krt[:], cbr_b, Apr[:, 0:8, :], ALU.mult), sV)
        b.ch(v, v.tensor_tensor(ta8[:], cbi_b, Api[:, 0:8, :], ALU.mult), sV)
        b.ch(v, v.tensor_sub(krt[:], krt[:], ta8[:]), sV)
        br_b = Br.unsqueeze(1).broadcast_to([128, 8, 8])
        bi_b = Bi.unsqueeze(1).broadcast_to([128, 8, 8])
        cmulb(v, Wr2[:], Wi2[:], Apr[:, 0:8, :], Api[:, 0:8, :], br_b, bi_b, ta8[:], tb8[:])
        cr_b = Cr.unsqueeze(1).broadcast_to([128, 8, 8])
        ci_b = Ci.unsqueeze(1).broadcast_to([128, 8, 8])
        cmulb(v, Prt[:], Pnt[:], cr_b, ci_b, Apr[:, 1:9, :], Api[:, 1:9, :], ta8[:], tb8[:])
        b.ch(v, v.tensor_scalar(Pnt[:], Pnt[:], -1.0, None, ALU.mult), sV)
        b.ch(v, v.memset(D1r[:, :, 0], 1.0), sV)
        b.ch(v, v.memset(D1i[:, :, 0], 0.0), sV)
        b.ch(v, v.tensor_copy(D1r[:, :, 1], Apr[:, 8, :]), sV)
        b.ch(v, v.tensor_copy(D1i[:, :, 1], Api[:, 8, :]), sV)
        for s in (1, 2, 4, 8):
            n = min(s, 16 - s)
            if n <= 0:
                continue
            br_ = D1r[:, :, s].unsqueeze(2).broadcast_to([128, 8, n])
            bi_ = D1i[:, :, s].unsqueeze(2).broadcast_to([128, 8, n])
            cmulb(v, D1r[:, :, s+1:s+1+n], D1i[:, :, s+1:s+1+n],
                  D1r[:, :, 1:1+n], D1i[:, :, 1:1+n], br_, bi_,
                  ta8[:, :, 0:n], tb8[:, :, 0:n])
        b.ch(v, v.memset(D2qr[:, :, 0], 1.0), sV)
        b.ch(v, v.memset(D2qi[:, :, 0], 0.0), sV)
        b.ch(v, v.tensor_copy(D2qr[:, :, 1], D1r[:, :, 16]), sV)
        b.ch(v, v.tensor_copy(D2qi[:, :, 1], D1i[:, :, 16]), sV)
        for s in (1, 2, 4, 8):
            n = min(s, 15 - s)
            if n <= 0:
                continue
            br_ = D2qr[:, :, s].unsqueeze(2).broadcast_to([128, 8, n])
            bi_ = D2qi[:, :, s].unsqueeze(2).broadcast_to([128, 8, n])
            cmulb(v, D2qr[:, :, s+1:s+1+n], D2qi[:, :, s+1:s+1+n],
                  D2qr[:, :, 1:1+n], D2qi[:, :, 1:1+n], br_, bi_,
                  ta8[:, :, 0:n], tb8[:, :, 0:n])

    yw = b.sb("lyw", [128, 8, T], F32)
    @block.vector
    def _(v):
        for kc in range(8):
            b.ch(v, v.tensor_scalar(yw[:, kc, :], u[:, kc, :], krt[:, 0, kc:kc+1], None, ALU.mult), sV)
        for l in range(1, TC1):
            for kc in range(8):
                o3 = yw[:, kc, :].rearrange("p (nz l8) -> p nz l8", l8=TC1)[:, :, l:TC1]
                i3 = u[:, kc, :].rearrange("p (nz l8) -> p nz l8", l8=TC1)[:, :, 0:TC1-l]
                b.ch(v, v.scalar_tensor_tensor(o3, i3, krt[:, l, kc:kc+1], o3, ALU.mult, ALU.add), sV)

    Sr = b.sb("lSr", [128, 8, NZL], F32); Si = b.sb("lSi", [128, 8, NZL], F32)
    tm = [b.sb(f"ltm{i}", [128, 8, NZL], F32) for i in range(4)]
    @block.vector
    def _(v):
        for j in range(TC1):
            u3 = u[:, :, :].rearrange("p kc (nz l8) -> p kc nz l8", l8=TC1)[:, :, :, j]
            wr_b = Wr2[:, TC1-1-j, :].unsqueeze(2).broadcast_to([128, 8, NZL])
            wi_b = Wi2[:, TC1-1-j, :].unsqueeze(2).broadcast_to([128, 8, NZL])
            if j == 0:
                b.ch(v, v.tensor_tensor(Sr[:], u3, wr_b, ALU.mult), sV)
                b.ch(v, v.tensor_tensor(Si[:], u3, wi_b, ALU.mult), sV)
            else:
                b.ch(v, v.tensor_tensor(tm[0][:], u3, wr_b, ALU.mult), sV)
                b.ch(v, v.tensor_add(Sr[:], Sr[:], tm[0][:]), sV)
                b.ch(v, v.tensor_tensor(tm[0][:], u3, wi_b, ALU.mult), sV)
                b.ch(v, v.tensor_add(Si[:], Si[:], tm[0][:]), sV)

    hr = b.sb("lhr", [128, 8, NZL], F32); hi = b.sb("lhi", [128, 8, NZL], F32)
    @block.vector
    def _(v):
        b.ch(v, v.tensor_copy(hr[:], Sr[:]), sV)
        b.ch(v, v.tensor_copy(hi[:], Si[:]), sV)
        for s in (1, 2, 4, 8):
            n = TC2 - s
            Hr_rd = hr[:, :, :].rearrange("p kc (nq c) -> p kc nq c", c=TC2)[:, :, :, 0:n]
            Hi_rd = hi[:, :, :].rearrange("p kc (nq c) -> p kc nq c", c=TC2)[:, :, :, 0:n]
            Hr_wr = hr[:, :, :].rearrange("p kc (nq c) -> p kc nq c", c=TC2)[:, :, :, s:TC2]
            Hi_wr = hi[:, :, :].rearrange("p kc (nq c) -> p kc nq c", c=TC2)[:, :, :, s:TC2]
            dr_b = D1r[:, :, s].unsqueeze(2).unsqueeze(3).broadcast_to([128, 8, NQ, n])
            di_b = D1i[:, :, s].unsqueeze(2).unsqueeze(3).broadcast_to([128, 8, NQ, n])
            t1 = tm[0][:, :, :].rearrange("p kc (nq c) -> p kc nq c", c=TC2)[:, :, :, 0:n]
            t2 = tm[1][:, :, :].rearrange("p kc (nq c) -> p kc nq c", c=TC2)[:, :, :, 0:n]
            t3 = tm[2][:, :, :].rearrange("p kc (nq c) -> p kc nq c", c=TC2)[:, :, :, 0:n]
            t4 = tm[3][:, :, :].rearrange("p kc (nq c) -> p kc nq c", c=TC2)[:, :, :, 0:n]
            b.ch(v, v.tensor_tensor(t1, Hr_rd, dr_b, ALU.mult), sV)
            b.ch(v, v.tensor_tensor(t2, Hi_rd, di_b, ALU.mult), sV)
            b.ch(v, v.tensor_tensor(t3, Hr_rd, di_b, ALU.mult), sV)
            b.ch(v, v.tensor_tensor(t4, Hi_rd, dr_b, ALU.mult), sV)
            b.ch(v, v.tensor_sub(t1, t1, t2), sV)
            b.ch(v, v.tensor_add(Hr_wr, Hr_wr, t1), sV)
            b.ch(v, v.tensor_add(t3, t3, t4), sV)
            b.ch(v, v.tensor_add(Hi_wr, Hi_wr, t3), sV)

    csr = b.sb("lcsr", [128, 8, 16], F32); csi = b.sb("lcsi", [128, 8, 16], F32)
    cpr = b.sb("lcpr", [128, 8, 16], F32); cpi = b.sb("lcpi", [128, 8, 16], F32)
    @block.vector
    def _(v):
        b.ch(v, v.tensor_copy(csr[:], hr[:, :, TC2-1::TC2]), sV)
        b.ch(v, v.tensor_copy(csi[:], hi[:, :, TC2-1::TC2]), sV)
        for s in (1, 2, 4, 8):
            n = 16 - s
            dr_b = D2qr[:, :, s].unsqueeze(2).broadcast_to([128, 8, n])
            di_b = D2qi[:, :, s].unsqueeze(2).broadcast_to([128, 8, n])
            t1 = tm[0][:, :, 0:n]; t2 = tm[1][:, :, 0:n]
            t3 = tm[2][:, :, 0:n]; t4 = tm[3][:, :, 0:n]
            b.ch(v, v.tensor_tensor(t1, csr[:, :, 0:n], dr_b, ALU.mult), sV)
            b.ch(v, v.tensor_tensor(t2, csi[:, :, 0:n], di_b, ALU.mult), sV)
            b.ch(v, v.tensor_tensor(t3, csr[:, :, 0:n], di_b, ALU.mult), sV)
            b.ch(v, v.tensor_tensor(t4, csi[:, :, 0:n], dr_b, ALU.mult), sV)
            b.ch(v, v.tensor_sub(t1, t1, t2), sV)
            b.ch(v, v.tensor_add(csr[:, :, s:16], csr[:, :, s:16], t1), sV)
            b.ch(v, v.tensor_add(t3, t3, t4), sV)
            b.ch(v, v.tensor_add(csi[:, :, s:16], csi[:, :, s:16], t3), sV)

    cst2 = b.sb("lcst", [128, 16], F32)
    alg2 = b.sb("lalg", [128, 2, 16], F32)
    @block.vector
    def _(v):
        b.ch(v, v.tensor_copy(cst2[:, 0:8], csr[:, :, 15]), sV)
        b.inc(v.tensor_copy(cst2[:, 8:16], csi[:, :, 15]), sV)
    @block.sync
    def _(sy):
        b.wait(sy, sV)
        b.dma(sy, hbm["cc_lin"], cst2[:], sDso)
    @block.gpsimd
    def _(g):
        b.wait(g, sDso)
        b.inc(g.collective_compute("AllGather", ALU.bypass, replica_groups=[list(range(NCC))],
                                   ins=[hbm["cc_lin"]], outs=[hbm["cc_lout"]]), sCC)
    @block.vector
    def _(v):
        b.ch(v, v.memset(hin[:], 0.0), sV)
    for r in range(NCC):
        @block.gpsimd
        def _(g, r=r):
            if r == 0:
                b.wait(g, sCC)
            else:
                b.wait(g, sV)
            b.dma(g, alg2[:, r % 2, :], hbm["cc_lout"][r*128:(r+1)*128, :], sD)
        @block.vector
        def _(v, r=r):
            b.wait(v, sD)
            b.ch(v, v.scalar_tensor_tensor(hin[:], alg2[:, r % 2, :], m8_128[:, r:r+1],
                                           hin[:], ALU.mult, ALU.add), sV)

    @block.vector
    def _(v):
        b.wait(v, sV)
        b.ch(v, v.memset(cpr[:, :, 0:1], 0.0), sV)
        b.ch(v, v.memset(cpi[:, :, 0:1], 0.0), sV)
        b.ch(v, v.tensor_copy(cpr[:, :, 1:16], csr[:, :, 0:15]), sV)
        b.ch(v, v.tensor_copy(cpi[:, :, 1:16], csi[:, :, 0:15]), sV)
        hr_b = hin[:, 0:8].unsqueeze(2).broadcast_to([128, 8, 16])
        hi_b = hin[:, 8:16].unsqueeze(2).broadcast_to([128, 8, 16])
        t1 = tm[0][:, :, 0:16]; t2 = tm[1][:, :, 0:16]
        b.ch(v, v.tensor_tensor(t1, D2qr[:], hr_b, ALU.mult), sV)
        b.ch(v, v.tensor_tensor(t2, D2qi[:], hi_b, ALU.mult), sV)
        b.ch(v, v.tensor_sub(t1, t1, t2), sV)
        b.ch(v, v.tensor_add(cpr[:], cpr[:], t1), sV)
        b.ch(v, v.tensor_tensor(t1, D2qr[:], hi_b, ALU.mult), sV)
        b.ch(v, v.tensor_tensor(t2, D2qi[:], hr_b, ALU.mult), sV)
        b.ch(v, v.tensor_add(t1, t1, t2), sV)
        b.ch(v, v.tensor_add(cpi[:], cpi[:], t1), sV)

    @block.vector
    def _(v):
        for kc in range(8):
            hr4 = hr[:, kc, :].rearrange("p (nq c) -> p nq c", c=TC2)
            hi4 = hi[:, kc, :].rearrange("p (nq c) -> p nq c", c=TC2)
            cmr = tm[0][:, kc, :].rearrange("p (nq c) -> p nq c", c=TC2)
            cmi = tm[1][:, kc, :].rearrange("p (nq c) -> p nq c", c=TC2)
            t1 = tm[2][:, kc, :].rearrange("p (nq c) -> p nq c", c=TC2)
            t2 = tm[3][:, kc, :].rearrange("p (nq c) -> p nq c", c=TC2)
            b.ch(v, v.tensor_copy(cmr, cpr[:, kc, :].unsqueeze(2).broadcast_to([128, 16, 16])), sV)
            b.ch(v, v.tensor_copy(cmi, cpi[:, kc, :].unsqueeze(2).broadcast_to([128, 16, 16])), sV)
            d1r_b = D1r[:, kc, 1:17].unsqueeze(1).broadcast_to([128, 16, 16])
            d1i_b = D1i[:, kc, 1:17].unsqueeze(1).broadcast_to([128, 16, 16])
            b.ch(v, v.tensor_tensor(t1, cmr, d1r_b, ALU.mult), sV)
            b.ch(v, v.tensor_tensor(t2, cmi, d1i_b, ALU.mult), sV)
            b.ch(v, v.tensor_sub(t1, t1, t2), sV)
            b.ch(v, v.tensor_add(hr4, hr4, t1), sV)
            b.ch(v, v.tensor_tensor(t1, cmi, d1r_b, ALU.mult), sV)
            b.ch(v, v.tensor_tensor(t2, cmr, d1i_b, ALU.mult), sV)
            b.ch(v, v.tensor_add(t1, t1, t2), sV)
            b.ch(v, v.tensor_add(hi4, hi4, t1), sV)

    b.free("lSr"); b.free("lSi")
    hpr = b.sb("lhpr", [128, 8, NZL], F32); hpi = b.sb("lhpi", [128, 8, NZL], F32)
    @block.vector
    def _(v):
        b.ch(v, v.tensor_copy(hpr[:, :, 1:NZL], hr[:, :, 0:NZL-1]), sV)
        b.ch(v, v.tensor_copy(hpi[:, :, 1:NZL], hi[:, :, 0:NZL-1]), sV)
        b.ch(v, v.tensor_copy(hpr[:, :, 0], hin[:, 0:8]), sV)
        b.ch(v, v.tensor_copy(hpi[:, :, 0], hin[:, 8:16]), sV)
        for l in range(TC1):
            o3 = yw[:, :, :].rearrange("p kc (nz l8) -> p kc nz l8", l8=TC1)[:, :, :, l]
            pr_b = Prt[:, l, :].unsqueeze(2).broadcast_to([128, 8, NZL])
            pn_b = Pnt[:, l, :].unsqueeze(2).broadcast_to([128, 8, NZL])
            b.ch(v, v.tensor_tensor(tm[0][:], hpr[:], pr_b, ALU.mult), sV)
            b.ch(v, v.tensor_add(o3, o3, tm[0][:]), sV)
            b.ch(v, v.tensor_tensor(tm[0][:], hpi[:], pn_b, ALU.mult), sV)
            b.ch(v, v.tensor_add(o3, o3, tm[0][:]), sV)
        for kc in range(8):
            b.ch(v, v.scalar_tensor_tensor(yw[:, kc, :], u[:, kc, :], l1D[:, kc:kc+1],
                                           yw[:, kc, :], ALU.mult, ALU.add), sV)

    for nm in ("ltm0", "ltm1", "ltm2", "ltm3"):
        b.free(nm)
    x1c = [b.sb(f"lx1{i}", [128, T], F32) for i in range(2)]
    ldc = {}; stc = {}; addc = {}
    for kc in range(8):
        @block.sync
        def _(sy, kc=kc):
            if kc == 0:
                b.wait(sy, sV)
            if kc >= 1:
                sy.wait_ge(b.h(sDs), ldc[kc-1])
            if kc >= 2:
                b.wait(sy, sV, addc[kc-2])
            ldc[kc] = b.dma(sy, x1c[kc % 2][:], hbm["x1"][:, kc, :], sDs)
        @block.vector
        def _(v, kc=kc):
            b.wait(v, sDs, ldc[kc])
            addc[kc] = b.inc(v.tensor_add(yw[:, kc, :], yw[:, kc, :], x1c[kc % 2][:]), sV)
        @block.sync
        def _(sy, kc=kc):
            b.wait(sy, sV, addc[kc])
            if kc >= 1:
                sy.wait_ge(b.h(sDso), stc[kc-1])
            stc[kc] = b.dma(sy, hbm["x2"][:, kc, :], yw[:, kc, :], sDso)
    @block.sync
    def _(sy):
        b.wait(sy, sDso)
    b.barrier()
    for nm in ("lru_u", "lApr", "lApi", "lkr", "lWr", "lWi", "lPr", "lPn",
               "lD1r", "lD1i", "lD2qr", "lD2qi", "lta8", "ltb8",
               "lCBr", "lCBi", "lhin", "lyw", "lhr", "lhi",
               "lcsr", "lcsi", "lcpr", "lcpi",
               "lhpr", "lhpi", "lcst", "lalg", "lx10", "lx11"):
        b.free(nm)
    b.barrier()


def emit_moe(b, io, hbm, C, lns, out_x3):
    nc, block = b.nc, b.block
    sD, sDs, sT, sV, sS, sG, sCC = C["sems"]
    sDw = C["sDw"]; sDw2 = C["sDw2"]; sDso = C["sDso"]
    id_f32 = C["id_f32"]; gate_bias = C["gate_bias"]

    gsb = b.sb("mgw", [128, 8, 8], BF16)
    @block.gpsimd
    def _(g):
        b.dma(g, gsb[:], io["gateT"].rearrange("(k p) j -> p k j", p=128), sD)
        g.wait_ge(b.h(sD), b.val(sD))
        b.inc(g.nop(), sG)
    b.barrier()

    for tt in range(2):
        x_ln3 = b.sb("mxln", [128, 8, TT], BF16)
        emit_ln(b, hbm["x2"][:, :, tt*TT:(tt+1)*TT], x_ln3, TT, lns, f"lnmoe{tt}")

        lg = b.sb("mlg", [8, TT], F32)
        g_em = b.sb("mgem", [8, TT], F32)
        tok = b.sb("mtok", [128, 8], F32)
        s8 = b.sb("ms8", [128, 8], F32)
        bvt = b.sb("mbv", [128, 8], F32)
        msk = b.sb("mmsk", [128, 8], F32)
        b2t = b.sb("mb2", [128, 8], F32)
        rcol = b.sb("mrc", [128, 4], F32)
        with nc.psum_tensor(f"mpslg{tt}", [8, 1024], F32) as ps_lg, \
             nc.psum_tensor(f"mpst{tt}", [128, 8], F32) as ps_t, \
             nc.psum_tensor(f"mps3{tt}", [8, 128], F32) as ps3:
            @block.tensor
            def _(t):
                b.wait(t, sV)
                b.wait(t, sG)
                for th in range(2):
                    for k in range(8):
                        ii = t.matmul(ps_lg[0:8, th*512:(th+1)*512], gsb[:, k, :],
                                      x_ln3[:, k, th*512:(th+1)*512], start=(k == 0), stop=(k == 7))
                b.inc(ii, sT)
            @block.vector
            def _(v):
                b.wait(v, sT)
                b.inc(v.tensor_copy(lg[:], ps_lg[0:8, :]), sV)
            for tb in range(8):
                @block.tensor
                def _(t, tb=tb):
                    b.wait(t, sV)
                    b.inc(t.transpose(ps_t[:], lg[0:8, tb*128:(tb+1)*128], id_f32[0:8, 0:8]), sT)
                @block.vector
                def _(v, tb=tb):
                    b.wait(v, sT)
                    b.ch(v, v.tensor_copy(tok[:], ps_t[:]), sV)
                    b.ch(v, v.reduce_max(rcol[:, 0:1], tok[:], AXX), sV)
                    b.ch(v, v.tensor_scalar(rcol[:, 1:2], rcol[:, 0:1], -1.0, None, ALU.mult), sV)
                @block.scalar
                def _(s, tb=tb):
                    b.wait(s, sV)
                    b.inc(s.activation(tok[:], tok[:], AF.Exp, bias=rcol[:, 1:2], scale=1.0), sS)
                @block.vector
                def _(v, tb=tb):
                    b.wait(v, sS)
                    b.ch(v, v.reduce_sum(rcol[:, 2:3], tok[:], AXX), sV)
                    b.ch(v, v.reciprocal(rcol[:, 3:4], rcol[:, 2:3]), sV)
                    b.ch(v, v.tensor_scalar(s8[:], tok[:], rcol[:, 3:4], None, ALU.mult), sV)
                    b.ch(v, v.tensor_add(bvt[:], s8[:], gate_bias[:, 0:8]), sV)
                    b.ch(v, v.reduce_max(rcol[:, 0:1], bvt[:], AXX), sV)
                    b.ch(v, v.tensor_scalar(msk[:], bvt[:], rcol[:, 0:1], None, ALU.is_ge), sV)
                    b.ch(v, v.tensor_mul(b2t[:], bvt[:], msk[:]), sV)
                    b.ch(v, v.tensor_sub(b2t[:], bvt[:], b2t[:]), sV)
                    b.ch(v, v.tensor_scalar(msk[:], msk[:], -1e30, None, ALU.mult), sV)
                    b.ch(v, v.tensor_add(b2t[:], b2t[:], msk[:]), sV)
                    b.ch(v, v.reduce_max(rcol[:, 0:1], b2t[:], AXX), sV)
                    b.ch(v, v.tensor_scalar(msk[:], bvt[:], rcol[:, 0:1], None, ALU.is_ge), sV)
                    b.ch(v, v.tensor_mul(tok[:], s8[:], msk[:]), sV)
                @block.tensor
                def _(t, tb=tb):
                    b.wait(t, sV)
                    b.inc(t.transpose(ps3[:], tok[:], id_f32[:, :]), sT)
                @block.vector
                def _(v, tb=tb):
                    b.wait(v, sT)
                    b.ch(v, v.tensor_copy(g_em[0:8, tb*128:(tb+1)*128], ps3[0:8, :]), sV)
        @block.gpsimd
        def _(g):
            b.wait(g, sV)
            b.dma(g, hbm["gmoe"][0:8, tt*TT:(tt+1)*TT], g_em[:], sD)
            g.wait_ge(b.h(sD), b.val(sD))
            b.inc(g.nop(), sG)
        for nm in ("mlg", "mtok", "ms8", "mbv", "mmsk", "mb2", "mrc"):
            b.free(nm)

        hs = b.sb("mhs", [128, 16, TT], BF16)
        wA = [b.sb(f"mwa{i}", [128, 8, 128], BF16) for i in range(2)]
        wB = [b.sb(f"mwb{i}", [128, 8, 128], BF16) for i in range(2)]
        sa = b.sb("msa", [128, 2, TT], F32)
        y_acc = b.sb("myacc", [128, 8, TT], F32)
        wC = [b.sb(f"mwc{i}", [128, 16, 128], BF16) for i in range(2)]
        gbc = b.sb("mgbc", [128, 2, TT], F32)
        he = b.sb("mhe", [128, 8, TT], BF16)
        tb1 = b.sb("mtb1", [128, 2, TT], F32)
        wD = [b.sb(f"mwd{i}", [128, 8, 128], BF16) for i in range(2)]
        tA = {}; sA = {}; vA = {}; tB = {}; vB = {}
        t0_ = b.val(sT)
        with nc.psum_tensor(f"mpsA{tt}", [128, 2048], F32) as psA, \
             nc.psum_tensor(f"mpsB{tt}", [128, 2048], F32) as psB:
            for m in range(16):
                mh = m % 2
                @block.gpsimd
                def _(g, m=m, mh=mh):
                    g.wait_ge(b.h(sT), tA[m-2] if m >= 2 else t0_)
                    b.dma(g, wA[mh][:], io["sw1s"][m].rearrange("(k p) j -> p k j", p=128), sDw[mh])
                    b.dma(g, wB[mh][:], io["sw3s"][m].rearrange("(k p) j -> p k j", p=128), sDw[mh])
                @block.tensor
                def _(t, m=m, mh=mh):
                    b.wait(t, sDw[mh])
                    if m >= 2:
                        t.wait_ge(b.h(sS), sA[m-2])
                        t.wait_ge(b.h(sV), vA[m-2])
                    for th in range(2):
                        o = mh*1024 + th*512
                        for k in range(8):
                            t.matmul(psA[:, o:o+512], wA[mh][:, k, :],
                                     x_ln3[:, k, th*512:(th+1)*512], start=(k == 0), stop=(k == 7))
                    for th in range(2):
                        o = mh*1024 + th*512
                        for k in range(8):
                            ii = t.matmul(psB[:, o:o+512], wB[mh][:, k, :],
                                          x_ln3[:, k, th*512:(th+1)*512], start=(k == 0), stop=(k == 7))
                    tA[m] = b.inc(ii, sT)
                @block.scalar
                def _(s, m=m, mh=mh):
                    b.wait(s, sT)
                    if m >= 2:
                        s.wait_ge(b.h(sV), vA[m-2])
                    sA[m] = b.inc(s.activation(sa[:, mh, :], psA[:, mh*1024:(mh+1)*1024],
                                               AF.Silu, bias=0.0, scale=1.0), sS)
                @block.vector
                def _(v, m=m, mh=mh):
                    b.wait(v, sS)
                    vA[m] = b.inc(v.tensor_tensor(hs[:, m, :], psB[:, mh*1024:(mh+1)*1024],
                                                  sa[:, mh, :], ALU.mult), sV)
            t0b = b.val(sT)
            for m in range(8):
                mh = m % 2
                @block.gpsimd
                def _(g, m=m, mh=mh):
                    g.wait_ge(b.h(sT), tB[m-2] if m >= 2 else t0b)
                    b.dma(g, wC[mh][:], io["sw2s"][m].rearrange("(k p) j -> p k j", p=128), sDw2[mh])
                @block.tensor
                def _(t, m=m, mh=mh):
                    b.wait(t, sDw2[mh])
                    if m < 2:
                        b.wait(t, sV)
                        b.wait(t, sS)
                    if m >= 2:
                        t.wait_ge(b.h(sV), vB[m-2])
                    for th in range(2):
                        o = mh*1024 + th*512
                        for k in range(16):
                            ii = t.matmul(psA[:, o:o+512], wC[mh][:, k, :],
                                          hs[:, k, th*512:(th+1)*512], start=(k == 0), stop=(k == 15))
                    tB[m] = b.inc(ii, sT)
                @block.vector
                def _(v, m=m, mh=mh):
                    b.wait(v, sT)
                    vB[m] = b.inc(v.tensor_copy(y_acc[:, m, :], psA[:, mh*1024:(mh+1)*1024]), sV)
            for e in range(8):
                eh = e % 2
                @block.gpsimd
                def _(g, e=e, eh=eh):
                    if e >= 2:
                        g.wait_ge(b.h(sV), vA[16 + (e-2)*8 + 7])
                    else:
                        b.wait(g, sG)
                    b.dma(g, gbc[:, eh, :],
                          hbm["gmoe"][e:e+1, tt*TT:(tt+1)*TT].broadcast_to([128, TT]), sD)
                for m in range(8):
                    aidx = 16 + e*8 + m
                    mh = aidx % 2
                    @block.gpsimd
                    def _(g, e=e, m=m, mh=mh, aidx=aidx):
                        g.wait_ge(b.h(sT), tA[aidx-2])
                        b.dma(g, wA[mh][:], io["ew1s"][e][m].rearrange("(k p) j -> p k j", p=128), sDw[mh])
                        b.dma(g, wB[mh][:], io["ew3s"][e][m].rearrange("(k p) j -> p k j", p=128), sDw[mh])
                    @block.tensor
                    def _(t, e=e, m=m, mh=mh, aidx=aidx):
                        b.wait(t, sDw[mh])
                        if m < 2:
                            b.wait(t, sV)
                        t.wait_ge(b.h(sS), sA[aidx-2])
                        t.wait_ge(b.h(sV), vA[aidx-2])
                        for th in range(2):
                            o = mh*1024 + th*512
                            for k in range(8):
                                t.matmul(psA[:, o:o+512], wA[mh][:, k, :],
                                         x_ln3[:, k, th*512:(th+1)*512], start=(k == 0), stop=(k == 7))
                        for th in range(2):
                            o = mh*1024 + th*512
                            for k in range(8):
                                ii = t.matmul(psB[:, o:o+512], wB[mh][:, k, :],
                                              x_ln3[:, k, th*512:(th+1)*512], start=(k == 0), stop=(k == 7))
                        tA[aidx] = b.inc(ii, sT)
                    @block.scalar
                    def _(s, m=m, mh=mh, aidx=aidx):
                        b.wait(s, sT)
                        s.wait_ge(b.h(sV), vA[aidx-2])
                        sA[aidx] = b.inc(s.activation(sa[:, mh, :], psA[:, mh*1024:(mh+1)*1024],
                                                      AF.Silu, bias=0.0, scale=1.0), sS)
                    @block.vector
                    def _(v, e=e, m=m, mh=mh, eh=eh, aidx=aidx):
                        b.wait(v, sS)
                        if e >= 1 and m == 0:
                            b.wait(v, sD)
                        elif e == 0 and m == 0:
                            b.wait(v, sD)
                        b.ch(v, v.tensor_tensor(tb1[:, mh, :], psB[:, mh*1024:(mh+1)*1024],
                                                gbc[:, eh, :], ALU.mult), sV)
                        vA[aidx] = b.inc(v.tensor_tensor(he[:, m, :], tb1[:, mh, :],
                                                         sa[:, mh, :], ALU.mult), sV)
                for m2 in range(8):
                    bidx = 8 + e*8 + m2
                    mh = bidx % 2
                    @block.gpsimd
                    def _(g, e=e, m2=m2, mh=mh, bidx=bidx):
                        g.wait_ge(b.h(sT), tB[bidx-2])
                        b.dma(g, wD[mh][:], io["ew2s"][e][m2].rearrange("(k p) j -> p k j", p=128), sDw2[mh])
                    @block.tensor
                    def _(t, m2=m2, mh=mh, bidx=bidx):
                        b.wait(t, sDw2[mh])
                        if m2 < 2:
                            b.wait(t, sV)
                            b.wait(t, sS)
                        t.wait_ge(b.h(sV), vB[bidx-2])
                        for th in range(2):
                            o = mh*1024 + th*512
                            for k in range(8):
                                ii = t.matmul(psA[:, o:o+512], wD[mh][:, k, :],
                                              he[:, k, th*512:(th+1)*512], start=(k == 0), stop=(k == 7))
                        tB[bidx] = b.inc(ii, sT)
                    @block.vector
                    def _(v, m2=m2, mh=mh, bidx=bidx):
                        b.wait(v, sT)
                        b.ch(v, v.tensor_add(y_acc[:, m2, :], y_acc[:, m2, :],
                                             psA[:, mh*1024:(mh+1)*1024]), sV)
                        vB[bidx] = b.val(sV)

        x2c = [b.sb(f"mx2{i}", [128, TT], F32) for i in range(2)]
        ldc = {}; stc = {}; adc = {}
        for kc in range(8):
            @block.sync
            def _(sy, kc=kc):
                if kc == 0:
                    b.wait(sy, sV)
                if kc >= 1:
                    sy.wait_ge(b.h(sDs), ldc[kc-1])
                if kc >= 2:
                    b.wait(sy, sV, adc[kc-2])
                ldc[kc] = b.dma(sy, x2c[kc % 2][:, 0:TT], hbm["x2"][:, kc, tt*TT:(tt+1)*TT], sDs)
            @block.vector
            def _(v, kc=kc):
                b.wait(v, sDs, ldc[kc])
                adc[kc] = b.inc(v.tensor_add(y_acc[:, kc, :], y_acc[:, kc, :], x2c[kc % 2][:, 0:TT]), sV)
            @block.sync
            def _(sy, kc=kc):
                b.wait(sy, sV, adc[kc])
                if kc >= 1:
                    sy.wait_ge(b.h(sDso), stc[kc-1])
                stc[kc] = b.dma(sy, out_x3[:, kc, tt*TT:(tt+1)*TT], y_acc[:, kc, :], sDso)
        @block.sync
        def _(sy):
            b.wait(sy, sDso)
        b.barrier()
        for nm in ("mxln", "mgem", "mhs", "mwa0", "mwa1", "mwb0", "mwb1", "msa",
                   "myacc", "mwc0", "mwc1", "mgbc", "mhe", "mtb1", "mwd0", "mwd1",
                   "mx20", "mx21"):
            b.free(nm)
    b.free("mgw")
    b.barrier()


def build_program():
    nc = bass.Bass("TRN2", target_bir_lowering=False)

    def din(name, shape, dt=F32):
        return nc.dram_tensor(name, shape, dt, kind="ExternalInput").ap()

    io = {}
    io["u_fm"] = din("u_fm", [128, 8, HALO + T])
    io["conv_w"] = din("conv_w", [128, 17, 4])
    io["conv_b"] = din("conv_b", [128, 17])
    io["dt_bias"] = din("dt_bias", [32, 1])
    io["A_log"] = din("A_log", [32, 1])
    io["D0"] = din("D0", [128, 16])
    io["norm_w"] = din("norm_w", [128, 16])
    io["w1s"] = din("w1s", [32, 1024, 128])
    io["w2s"] = din("w2s", [32, 1024, 128])
    io["h1s"] = din("h1s", [32, 4096, 128])
    io["h2s"] = din("h2s", [32, 4096, 128])
    io["w3s"] = din("w3s", [8, 4096, 128])
    io["wins"] = din("wins", [33, 1024, 128])
    io["windt"] = din("windt", [1024, 32])
    io["wouts"] = din("wouts", [8, 2048, 128])
    io["s1"] = din("s1", [128, 32])
    io["s2"] = din("s2", [128, 32])
    io["sh1"] = din("sh1", [128, 32])
    io["sh2"] = din("sh2", [128, 32])
    io["s3"] = din("s3", [128, 8])
    io["gateT"] = din("gateT", [1024, 8])
    io["gate_bias"] = din("gate_bias", [128, 8])
    io["ew1s"] = din("ew1s", [8, 8, 1024, 128])
    io["ew3s"] = din("ew3s", [8, 8, 1024, 128])
    io["ew2s"] = din("ew2s", [8, 8, 1024, 128])
    io["sw1s"] = din("sw1s", [16, 1024, 128])
    io["sw3s"] = din("sw3s", [16, 1024, 128])
    io["sw2s"] = din("sw2s", [8, 2048, 128])
    io["l1A"] = din("l1A", [128, 16])
    io["l1B"] = din("l1B", [128, 16])
    io["l1C"] = din("l1C", [128, 16])
    io["l1D"] = din("l1D", [128, 8])
    io["m8_64"] = din("m8_64", [64, 8])
    io["shl"] = din("shl", [64, 256])
    io["m8_128"] = din("m8_128", [128, 8])

    out_x3 = nc.dram_tensor("x3", [128, 8, T], F32, kind="ExternalOutput").ap()
    dbg = nc.dram_tensor("dbgo", [128, 18600], F32, kind="ExternalOutput").ap() if DBG else None

    x1_hbm = nc.dram_tensor("x1_hbm", [128, 8, T], F32, kind="Internal").ap()
    x2_hbm = nc.dram_tensor("x2_hbm", [128, 8, T], F32, kind="Internal").ap()
    xcv_hbm = nc.dram_tensor("xcv_hbm", [128, 17, T], BF16, kind="Internal").ap()
    acsf_hbm = nc.dram_tensor("acsf_hbm", [1, NCH_T * NH * CH], F32, kind="Internal").ap()
    acsf2_hbm = nc.dram_tensor("acsf2_hbm", [1, NCH_T * NH * CH], BF16, kind="Internal").ap()
    flat_hbm = nc.dram_tensor("flat_hbm", [1, 16384], F32, kind="Internal").ap()
    cc_min = nc.dram_tensor("cc_min", [64, NH * NS], F32, kind="Internal").ap()
    cc_mout = nc.dram_tensor("cc_mout", [NCC * 64, NH * NS], F32, kind="Internal").ap()
    cc_lin = nc.dram_tensor("cc_lin", [128, 16], F32, kind="Internal").ap()
    cc_lout = nc.dram_tensor("cc_lout", [NCC * 128, 16], F32, kind="Internal").ap()
    gmoe_hbm = nc.dram_tensor("gmoe_hbm", [8, T], F32, kind="Internal").ap()

    with nc.Block() as block:
        b = B(nc, block)
        b.init_pool(53056)
        sD = b.sem("dma_g")
        sDs = b.sem("dma_s")
        sT = b.sem("s_t")
        sV = b.sem("s_v")
        sS = b.sem("s_s")
        sG = b.sem("s_g")
        sCC = b.sem("s_cc")
        sDw = [b.sem("dma_w0"), b.sem("dma_w1")]
        sDw2 = [b.sem("dma_w2a"), b.sem("dma_w2b")]
        sDso = b.sem("dma_so")

        id_bf = b.sb("id_bf", [128, 128], BF16)
        id_f32 = b.sb("id_f32", [128, 128], F32)
        ones_bf = b.sb("ones_bf", [128, 1], BF16)
        ones_f32r = b.sb("ones_f32r", [1, 128], F32)
        maskST = b.sb("maskST", [64, 128], F32)
        conv_w = b.sb("conv_wsb", [128, 17, 4], F32)
        conv_b = b.sb("conv_bsb", [128, 17], F32)
        dt_bias = b.sb("dt_biassb", [32, 1], F32)
        A_col = b.sb("A_col", [32, 1], F32)
        D0 = b.sb("D0sb", [128, 16], F32)
        norm_w = b.sb("norm_wsb", [128, 16], F32)
        m8_64 = b.sb("m8_64sb", [64, 8], F32)
        shl_sb = b.sb("shlsb", [64, 256], F32)
        m8_128 = b.sb("m8_128sb", [128, 8], F32)
        gate_bias = b.sb("gate_biassb", [128, 8], F32)
        l1A = b.sb("l1Asb", [128, 16], F32)
        l1B = b.sb("l1Bsb", [128, 16], F32)
        l1C = b.sb("l1Csb", [128, 16], F32)
        l1D = b.sb("l1Dsb", [128, 8], F32)

        @block.vector
        def _(v):
            v.memset(ones_bf[:], 1.0)
            v.memset(ones_f32r[:], 1.0)
            v.memset(id_bf[:], 1.0)
            v.memset(id_f32[:], 1.0)
            b.inc(v.memset(maskST[:], 1.0), sV)

        @block.gpsimd
        def _(g):
            for dst, src in ((conv_w, "conv_w"), (conv_b, "conv_b"), (dt_bias, "dt_bias"),
                             (A_col, "A_log"), (D0, "D0"), (norm_w, "norm_w"),
                             (m8_64, "m8_64"), (m8_128, "m8_128"), (gate_bias, "gate_bias"),
                             (l1A, "l1A"), (l1B, "l1B"), (l1C, "l1C"), (l1D, "l1D"),
                             (shl_sb, "shl")):
                b.dma(g, dst[:], io[src], sD)
            g.wait_ge(b.h(sD), b.val(sD))
            b.wait(g, sV)
            g.affine_select(id_bf[:], id_bf[:], pattern=[[-1, 128]], compare_op=ALU.is_equal,
                            fill=0.0, base=0, channel_multiplier=1)
            g.affine_select(id_f32[:], id_f32[:], pattern=[[-1, 128]], compare_op=ALU.is_equal,
                            fill=0.0, base=0, channel_multiplier=1)
            g.affine_select(maskST[:], maskST[:], pattern=[[0, 2], [1, 64]],
                            compare_op=ALU.is_ge, fill=0.0, base=0, channel_multiplier=-1)
            b.inc(g.engine_nop(), sG)


        @block.scalar
        def _(s):
            b.wait(s, sG)
            b.ch(s, s.activation(A_col[:], A_col[:], AF.Exp, bias=0.0, scale=1.0), sS)
            b.inc(s.mul(A_col[:], A_col[:], -1.0), sS)

        b.barrier()

        W0 = HALO + T
        lns = {
            "ones_bf": ones_bf, "ones_f32r": ones_f32r,
            "sems": (sD, sDs, sT, sV, sS),
        }

        x_ln = b.sb("x_ln", [128, 8, W0], BF16)
        emit_ln(b, io["u_fm"], x_ln, W0, lns, "ln0")
        b.barrier()

        if DBG == "ln0":
            @block.gpsimd
            def _(g):
                b.wait(g, sV)
                for kc in range(8):
                    b.dma(g, dbg[:, kc*2051:(kc+1)*2051], x_ln[:, kc, :], sD)
                g.wait_ge(b.h(sD), b.val(sD))
            b.stack.close()
            return nc

        consts = {
            "sems": (sD, sDs, sT, sV, sS, sG, sCC),
            "sDw": sDw, "sDw2": sDw2, "sDso": sDso,
            "ones_bf": ones_bf, "ones_f32r": ones_f32r, "id_bf": id_bf, "id_f32": id_f32,
            "maskST": maskST, "conv_w": conv_w, "conv_b": conv_b, "dt_bias": dt_bias,
            "A_col": A_col, "D0": D0, "norm_w": norm_w, "m8_64": m8_64, "m8_128": m8_128,
            "gate_bias": gate_bias, "l1A": l1A, "l1B": l1B, "l1C": l1C, "l1D": l1D,
            "shl": shl_sb,
        }
        hbm = {
            "x1": x1_hbm, "x2": x2_hbm, "xcv": xcv_hbm, "acsf": acsf_hbm,
            "acsf2": acsf2_hbm, "flat": flat_hbm, "cc_min": cc_min, "cc_mout": cc_mout,
            "cc_lin": cc_lin, "cc_lout": cc_lout, "gmoe": gmoe_hbm,
        }

        emit_mamba(b, io, x_ln, consts, hbm, dbg)

        lvl = 99 if not DBG else {"ffnin": 1, "lruin": 2, "moein": 3}.get(DBG, 0)

        def dump_x1():
            @block.gpsimd
            def _(g):
                b.dma(g, dbg[:, 0:16384], x1_hbm[:, :, :].rearrange("p a t -> p (a t)"), sD)
                g.wait_ge(b.h(sD), b.val(sD))
                b.inc(g.nop(), sG)
            b.barrier()

        def emit_out(src):
            @block.sync
            def _(sy):
                b.wait(sy, sDso)
                b.dma(sy, out_x3[:], src, sDs)
                sy.wait_ge(b.h(sDs), b.val(sDs))

        if lvl >= 1:
            if DBG == "ffnin":
                dump_x1()
            emit_ffn0(b, io, hbm, consts, lns)
        if lvl == 1:
            emit_out(x1_hbm)
        if lvl >= 2:
            if DBG == "lruin":
                dump_x1()
            emit_lru(b, io, hbm, consts, lns)
        if lvl == 2:
            emit_out(x2_hbm)
        if lvl >= 3:
            if DBG == "moein":
                @block.gpsimd
                def _(g):
                    b.dma(g, dbg[:, 0:16384], x2_hbm[:, :, :].rearrange("p a t -> p (a t)"), sD)
                    g.wait_ge(b.h(sD), b.val(sD))
                    b.inc(g.nop(), sG)
                b.barrier()
            emit_moe(b, io, hbm, consts, lns, out_x3)

        b.stack.close()
        b.pool_guard.__exit__(None, None, None)
    return nc


def _prep_inputs(inputs):
    f = np.float32
    u = inputs["L0_attn_u"]
    Win = inputs["L0_attn_W_in"]
    winT = np.ascontiguousarray(Win.T)
    wins = np.zeros((33, 1024, 128), f)
    for m in range(32):
        c0 = m * 128
        c1 = min(c0 + 128, 4128)
        if c0 < 4128:
            wins[m, :, 0:c1-c0] = winT[:, c0:c1]
    wins[32, :, 0:16] = winT[:, 4096:4112]
    wins[32, :, 32:48] = winT[:, 4112:4128]
    windt = np.ascontiguousarray(winT[:, 4128:4160])
    wout = inputs["L0_attn_W_out"]
    woutT = np.ascontiguousarray(wout.T)
    wouts = np.ascontiguousarray(woutT.reshape(2048, 8, 128).transpose(1, 0, 2))

    def strips(wT, nm):
        return np.ascontiguousarray(wT.reshape(wT.shape[0], nm, 128).transpose(1, 0, 2))

    w1T = np.ascontiguousarray(inputs["L0_ffn_w1_w"].T)
    w2T = np.ascontiguousarray(inputs["L0_ffn_w2_w"].T)
    h1T = np.ascontiguousarray(inputs["L0_ffn_h1_w"].T)
    h2T = np.ascontiguousarray(inputs["L0_ffn_h2_w"].T)
    w3T = np.ascontiguousarray(inputs["L0_ffn_w3_w"].T)

    conv_w = np.zeros((128, 17, 4), f)
    cw = inputs["L0_attn_conv_w"][:, 0, :]
    cwp = np.zeros((17 * 128, 4), f)
    cwp[:2048] = cw[:2048]
    cwp[2048:2064] = cw[2048:2064]
    cwp[2048+32:2048+48] = cw[2064:2080]
    conv_w = np.ascontiguousarray(cwp.reshape(17, 128, 4).transpose(1, 0, 2))
    cbp = np.zeros(17 * 128, f)
    cbp[:2048] = inputs["L0_attn_conv_b"][:2048]
    cbp[2048:2064] = inputs["L0_attn_conv_b"][2048:2064]
    cbp[2048+32:2048+48] = inputs["L0_attn_conv_b"][2064:2080]
    conv_b = np.ascontiguousarray(cbp.reshape(17, 128).T)

    shl = np.zeros((64, 256), f)
    for k in range(64):
        shl[k, k] = 1.0
        shl[k, 128 + 64 + k] = 1.0

    def pack_pf(vec, ncols):
        return np.ascontiguousarray(vec.reshape(ncols, 128).T)

    D0 = pack_pf(np.repeat(inputs["L0_attn_D"], 64), 16)
    norm_w = pack_pf(inputs["L0_attn_norm_w"], 16)
    l1A = np.concatenate([pack_pf(inputs["L1_attn_A_real"], 8),
                          pack_pf(inputs["L1_attn_A_imag"], 8)], axis=1)
    l1B = np.concatenate([pack_pf(inputs["L1_attn_B_real"], 8),
                          pack_pf(inputs["L1_attn_B_imag"], 8)], axis=1)
    l1C = np.concatenate([pack_pf(inputs["L1_attn_C_real"], 8),
                          pack_pf(inputs["L1_attn_C_imag"], 8)], axis=1)
    l1D = pack_pf(inputs["L1_attn_D"], 8)
    gateT = np.ascontiguousarray(inputs["L1_ffn_gate_w"].T)
    gate_bias = np.ascontiguousarray(np.broadcast_to(inputs["L1_ffn_gate_bias"], (128, 8)))
    ew1s = np.stack([strips(np.ascontiguousarray(inputs["L1_ffn_w1"][e].T), 8) for e in range(8)])
    ew3s = np.stack([strips(np.ascontiguousarray(inputs["L1_ffn_w3"][e].T), 8) for e in range(8)])
    ew2s = np.stack([strips(np.ascontiguousarray(inputs["L1_ffn_w2"][e].T), 8) for e in range(8)])
    sw1s = strips(np.ascontiguousarray(inputs["L1_ffn_sw1"].T), 16)
    sw3s = strips(np.ascontiguousarray(inputs["L1_ffn_sw3"].T), 16)
    sw2s = strips(np.ascontiguousarray(inputs["L1_ffn_sw2"].T), 8)

    common = dict(
        wins=wins, windt=windt, wouts=wouts, shl=shl,
        conv_w=conv_w, conv_b=conv_b,
        dt_bias=inputs["L0_attn_dt_bias"].reshape(32, 1).astype(f),
        A_log=inputs["L0_attn_A_log"].reshape(32, 1).astype(f),
        D0=D0, norm_w=norm_w,
        w1s=strips(w1T, 32), w2s=strips(w2T, 32),
        h1s=strips(h1T, 32), h2s=strips(h2T, 32), w3s=strips(w3T, 8),
        s1=pack_pf(inputs["L0_ffn_w1_s"], 32), s2=pack_pf(inputs["L0_ffn_w2_s"], 32),
        sh1=pack_pf(inputs["L0_ffn_h1_s"], 32), sh2=pack_pf(inputs["L0_ffn_h2_s"], 32),
        s3=pack_pf(inputs["L0_ffn_w3_s"], 8),
        gateT=gateT, gate_bias=gate_bias,
        ew1s=ew1s, ew3s=ew3s, ew2s=ew2s, sw1s=sw1s, sw3s=sw3s, sw2s=sw2s,
        l1A=l1A, l1B=l1B, l1C=l1C, l1D=l1D,
    )

    in_maps = []
    for c in range(8):
        bb, sh = c // 2, c % 2
        t0 = sh * T
        u_slice = u[bb, t0:t0 + T]
        halo = u[bb, t0 - HALO:t0] if sh == 1 else np.zeros((HALO, DM), f)
        u_ext = np.concatenate([halo, u_slice], 0)
        u_fm = np.ascontiguousarray(u_ext.T).reshape(8, 128, HALO + T).transpose(1, 0, 2)
        u_fm = np.ascontiguousarray(u_fm)
        m8v_64 = np.zeros((64, 8), f)
        m8v_128 = np.zeros((128, 8), f)
        if sh == 1:
            m8v_64[:, c - 1] = 1.0
            m8v_128[:, c - 1] = 1.0
        m = dict(common)
        m["u_fm"] = u_fm
        m["m8_64"] = m8v_64
        m["m8_128"] = m8v_128
        in_maps.append(m)
    return in_maps


def kernel(**inputs):
    in_maps = _prep_inputs(inputs)
    nc = build_program()
    res = run_bass_kernel_spmd(nc, in_maps, core_ids=list(range(8)))
    out = np.zeros((4, 4096, 1024), np.float32)
    for c in range(8):
        bb, sh = c // 2, c % 2
        x3 = np.asarray(res.results[c]["x3"])
        out[bb, sh*T:(sh+1)*T] = x3.transpose(1, 0, 2).reshape(1024, T).T
    return out


if __name__ == "__main__":
    nc = build_program()
    print("build ok")


# version 08
# speedup vs baseline: 1.1316x; 1.0152x over previous
import os
import sys
if '/opt/trn_rl_repo' not in sys.path:
    sys.path.insert(0, '/opt/trn_rl_repo')
import numpy as np
import concourse.bass as bass
import concourse.mybir as mybir
from concourse.bass_utils import run_bass_kernel_spmd
from contextlib import ExitStack

F32 = mybir.dt.float32
BF16 = mybir.dt.bfloat16
AF = mybir.ActivationFunctionType
ALU = mybir.AluOpType
AXX = mybir.AxisListType.X

T = 2048
TT = 1024
HALO = 3
DM = 1024
DI = 2048
NH = 32
HP = 64
NS = 16
CH = 64
NCH_T = TT // CH
CONVD = 2080
DH = 4096
TC1 = 8
TC2 = 16
NZ = T // TC1
NQ = NZ // TC2

DBG = ""
NCC = 8

class B:
    def __init__(self, nc, block):
        self.nc = nc
        self.block = block
        self.semc = {}
        self.stack = ExitStack()
        self._sb = {}
        self.engines = ["tensor", "vector", "scalar", "gpsimd", "sync"]
        for e in self.engines:
            self.sem("prog_" + e)

    def sem(self, name):
        h = self.stack.enter_context(self.nc.semaphore(name))
        self.semc[name] = [h, 0]
        return name

    def inc(self, inst, name, by=1):
        self.semc[name][0] and inst.then_inc(self.semc[name][0], by)
        self.semc[name][1] += by
        return self.semc[name][1]

    def bump(self, name, by):
        self.semc[name][1] += by
        return self.semc[name][1]

    def val(self, name):
        return self.semc[name][1]

    def h(self, name):
        return self.semc[name][0]

    def wait(self, eng, name, value=None):
        eng.wait_ge(self.h(name), self.val(name) if value is None else value)

    def dma(self, q, out, in_, sem):
        q.dma_start(out=out, in_=in_).then_inc(self.h(sem), 16)
        return self.bump(sem, 16)

    def init_pool(self, cols):
        self.pool_guard = self.nc.sbuf_tensor("pool", [128, cols], F32)
        h = self.pool_guard.__enter__()
        self.pf = h.ap()
        self.pb = self.pf.bitcast(BF16)
        self.free_list = [(0, cols)]
        self.allocs = {}

    def sb(self, name, shape, dtype, top=False):
        assert name not in self.allocs, name
        n_inner = 1
        for s in shape[1:]:
            n_inner *= s
        if dtype == F32:
            units = n_inner
        else:
            units = (n_inner + 1) // 2
        units = (units + 3) & ~3
        order = reversed(list(enumerate(self.free_list))) if top else enumerate(self.free_list)
        for i, (off, size) in order:
            if size >= units:
                if top:
                    off = off + size - units
                    if size == units:
                        self.free_list.pop(i)
                    else:
                        self.free_list[i] = (self.free_list[i][0], size - units)
                elif size == units:
                    self.free_list.pop(i)
                else:
                    self.free_list[i] = (off + units, size - units)
                self.allocs[name] = (off, units)
                base = self.pf if dtype == F32 else self.pb
                scale = 1 if dtype == F32 else 2
                v = base[0:shape[0], off*scale: off*scale + n_inner]
                if len(shape) > 2:
                    pat = "p (" + " ".join(f"d{j}" for j in range(len(shape)-1)) + ") -> p " +                           " ".join(f"d{j}" for j in range(len(shape)-1))
                    kw = {f"d{j}": shape[j+1] for j in range(len(shape)-2)}
                    v = v.rearrange(pat, **kw)
                return v
        raise MemoryError(f"pool OOM for {name}: need {units}, free={self.free_list}, allocs={ {k: v for k, v in sorted(self.allocs.items(), key=lambda x: x[1][0])} }")

    def free(self, name):
        off, units = self.allocs.pop(name)
        self.free_list.append((off, units))
        self.free_list.sort()
        merged = []
        for o, s in self.free_list:
            if merged and merged[-1][0] + merged[-1][1] == o:
                merged[-1] = (merged[-1][0], merged[-1][1] + s)
            else:
                merged.append((o, s))
        self.free_list = [tuple(x) for x in merged]

    def gsync(self, g, sem):
        self.inc(g.nop(), sem)
        g.wait_ge(self.h(sem), self.val(sem))

    def ch(self, E, inst, sem):
        self.inc(inst, sem)
        E.wait_ge(self.h(sem), self.val(sem))

    def barrier(self):
        targets = {}
        for e in self.engines:
            @getattr(self.block, e)
            def _(E, e=e):
                self.inc(E.drain(), "prog_" + e)
            targets[e] = self.val("prog_" + e)
        for e in self.engines:
            @getattr(self.block, e)
            def _(E, e=e):
                for o in self.engines:
                    if o != e:
                        E.wait_ge(self.h("prog_" + o), targets[o])


def emit_ln(b, src_hbm, dst_bf, width, scratch, tag, u_buf=None):
    scratch = dict(scratch)
    scratch["u"] = u_buf if u_buf is not None else b.sb("lnu_" + tag, [128, 8, width], F32)
    scratch["ubf"] = b.sb("lnubf_" + tag, [128, 8, width], BF16)
    scratch["sqbf"] = b.sb("lnsq_" + tag, [128, 8, width], BF16)
    scratch["m_row"] = b.sb("lnmr_" + tag, [1, width], F32)
    scratch["r_row"] = b.sb("lnrr_" + tag, [1, width], F32)
    scratch["t1_row"] = b.sb("lnt1_" + tag, [1, width], F32)
    scratch["mb"] = b.sb("lnmb_" + tag, [128, 512], F32)
    scratch["rb"] = b.sb("lnrb_" + tag, [128, 512], F32)
    scratch["tw"] = b.sb("lntw_" + tag, [128, 512], F32)
    """LN over d=1024 of a feature-major fp32 HBM tensor src_hbm [128,8,width].
    Writes dst_bf [128,8,width] bf16 = (x - m) * rsqrt(var+eps).
    scratch: dict with persistent tiles (created by caller once):
      u (f32 [128,8,width_max]), ubf, sqbf (bf16 same), rows (f32 [1,width_max*?]),
      mb, rb (f32 [128, 512])
    Loads src from HBM (fp32 via sync + bf16 cast via gpsimd).
    """
    nc, block = b.nc, b.block
    u = scratch["u"]; ubf = scratch["ubf"]; sqbf = scratch["sqbf"]
    m_row = scratch["m_row"]; r_row = scratch["r_row"]
    t1_row = scratch["t1_row"]
    ones_bf = scratch["ones_bf"]; ones_f32r = scratch["ones_f32r"]
    mb = scratch["mb"]; rb = scratch["rb"]
    sD, sDs, sT, sV, sS = scratch["sems"]

    @block.sync
    def _(sy):
        b.dma(sy, u[:, :, 0:width], src_hbm, sDs)
    @block.gpsimd
    def _(g):
        b.dma(g, ubf[:, :, 0:width], src_hbm, sD)

    @block.scalar
    def _(s):
        b.wait(s, sDs)
        for kc in range(8):
            ii = s.activation(sqbf[:, kc, 0:width], u[:, kc, 0:width], AF.Square, bias=0.0, scale=1.0)
        b.inc(ii, sS)

    wins = [(o, min(512, width - o)) for o in range(0, width, 512)]
    with nc.psum_tensor("psst_" + tag, [1, 2560], F32) as ps_stat, \
         nc.psum_tensor("psbc_" + tag, [128, 1024], F32) as ps_bc:
        for ridx, srcb in ((0, ubf), (1, sqbf)):
            @block.tensor
            def _(t, ridx=ridx, srcb=srcb):
                if ridx == 0:
                    b.wait(t, sD)
                    b.wait(t, sS)
                else:
                    b.wait(t, sV)
                for (o, n) in wins:
                    for kc in range(8):
                        ii = t.matmul(ps_stat[0:1, o:o+n], ones_bf[:, 0:1],
                                      srcb[:, kc, o:o+n], start=(kc == 0), stop=(kc == 7))
                b.inc(ii, sT)

            @block.vector
            def _(v, ridx=ridx):
                b.wait(v, sT)
                b.inc(v.tensor_copy((m_row if ridx == 0 else r_row)[0:1, 0:width], ps_stat[0:1, 0:width]), sV)

        @block.vector
        def _(v):
            b.wait(v, sV)
            b.ch(v, v.tensor_scalar_mul(m_row[0:1, 0:width], m_row[0:1, 0:width], 1.0 / DM), sV)
            b.ch(v, v.tensor_scalar_mul(r_row[0:1, 0:width], r_row[0:1, 0:width], 1.0 / DM), sV)
            b.ch(v, v.tensor_mul(t1_row[0:1, 0:width], m_row[0:1, 0:width], m_row[0:1, 0:width]), sV)
            b.ch(v, v.tensor_sub(r_row[0:1, 0:width], r_row[0:1, 0:width], t1_row[0:1, 0:width]), sV)
            b.inc(v.tensor_scalar_add(r_row[0:1, 0:width], r_row[0:1, 0:width], 1e-5), sV)
        @block.scalar
        def _(s):
            b.wait(s, sV)
            b.inc(s.activation(r_row[0:1, 0:width], r_row[0:1, 0:width], AF.Sqrt, bias=0.0, scale=1.0), sS)
        @block.vector
        def _(v):
            b.wait(v, sS)
            b.inc(v.reciprocal(r_row[0:1, 0:width], r_row[0:1, 0:width]), sV)

        for (o, n) in wins:
            @block.tensor
            def _(t, o=o, n=n):
                b.wait(t, sV)
                if o > 0:
                    b.wait(t, sV)
                t.matmul(ps_bc[:, 0:n], ones_f32r[0:1, :], m_row[0:1, o:o+n], start=True, stop=True)
                b.inc(t.matmul(ps_bc[:, 512:512+n], ones_f32r[0:1, :], r_row[0:1, o:o+n], start=True, stop=True), sT)
            @block.vector
            def _(v, o=o, n=n):
                b.wait(v, sT)
                b.ch(v, v.tensor_copy(mb[:, 0:n], ps_bc[:, 0:n]), sV)
                b.ch(v, v.tensor_copy(rb[:, 0:n], ps_bc[:, 512:512+n]), sV)
                for kc in range(8):
                    b.ch(v, v.tensor_sub(scratch["tw"][:, 0:n], u[:, kc, o:o+n], mb[:, 0:n]), sV)
                    ii = v.tensor_mul(dst_bf[:, kc, o:o+n], scratch["tw"][:, 0:n], rb[:, 0:n])
                    if kc < 7:
                        b.ch(v, ii, sV)
                b.inc(ii, sV)
    b.barrier()
    for nm in ("lnubf_", "lnsq_", "lnmr_", "lnrr_", "lnt1_", "lnmb_", "lnrb_", "lntw_"):
        b.free(nm + tag)
    if u_buf is None:
        b.free("lnu_" + tag)


def emit_mamba(b, io, x_ln, C, hbm, dbg):
    nc, block = b.nc, b.block
    sD, sDs, sT, sV, sS, sG, sCC = C["sems"]
    sDw = C["sDw"]; sDso = C["sDso"]
    ones_bf = C["ones_bf"]; ones_f32r = C["ones_f32r"]; id_bf = C["id_bf"]
    id_f32 = C["id_f32"]; maskST = C["maskST"]; conv_w = C["conv_w"]
    conv_b = C["conv_b"]; dt_bias = C["dt_bias"]; A_col = C["A_col"]
    D0 = C["D0"]; norm_w = C["norm_w"]; m8_64 = C["m8_64"]; shl_sb = C["shl"]

    acs2 = b.sb("acs2", [32, 2, TT], F32)
    acsT = b.sb("acsT", [128, 16, 32], F32)
    dtT = b.sb("dtT", [128, 16, 32], F32)
    dexT = b.sb("dexT", [128, 16, 32], F32)
    asum = b.sb("asum", [32, 32], F32)
    state_in = b.sb("state_in", [64, 512], F32)
    ST3 = [b.sb("ST3a", [64, 8192], F32, top=True), b.sb("ST3b", [64, 8192], F32, top=True)]
    adt_gate = b.sb("adt_gate", [32, TT], F32)

    @block.vector
    def _(v):
        b.ch(v, v.memset(adt_gate[:], 1.0), sV)
        for c in range(NCH_T):
            ii = v.memset(adt_gate[:, c * CH:c * CH + 1], 0.0)
        b.inc(ii, sV)

    for tt in range(2):
        o_t = HALO + tt * TT
        xcv = b.sb("xcv", [128, 17, TT], BF16)
        xbc = b.sb("xbc", [128, 17, HALO + TT], BF16)
        wb = [b.sb(f"wbA{i}", [128, 8, 128], BF16) for i in range(2)]
        dtraw = b.sb("dtraw", [32, TT], F32)
        conv_acc = b.sb("conv_acc", [128, TT], F32)

        tcounts = {}
        loop_t0 = b.val(sT)
        with nc.psum_tensor(f"ps_xbc{tt}", [128, 1536], F32) as ps_xbc, \
             nc.psum_tensor(f"ps_dt{tt}", [32, 1024], F32) as ps_dt:
            for si, m in enumerate(range(16, 33)):
                buf = wb[si % 2]
                @block.gpsimd
                def _(g, m=m, buf=buf, si=si):
                    g.wait_ge(b.h(sT), tcounts[si - 2] if si >= 2 else loop_t0)
                    if si < 2:
                        b.wait(g, sV)
                    b.dma(g, buf[:], io["wins"][m].rearrange("(k p) j -> p k j", p=128), sDw[si % 2])
                @block.tensor
                def _(t, buf=buf, si=si):
                    b.wait(t, sDw[si % 2])
                    if si >= 1:
                        b.wait(t, sV)
                    for k in range(8):
                        t.matmul(ps_xbc[:, 0:HALO], buf[:, k, :], x_ln[:, k, o_t-HALO:o_t],
                                 start=(k == 0), stop=(k == 7))
                        t.matmul(ps_xbc[:, 512:1024], buf[:, k, :], x_ln[:, k, o_t:o_t+512],
                                 start=(k == 0), stop=(k == 7))
                        ii = t.matmul(ps_xbc[:, 1024:1536], buf[:, k, :], x_ln[:, k, o_t+512:o_t+1024],
                                      start=(k == 0), stop=(k == 7))
                    tcounts[si] = b.inc(ii, sT)
                @block.vector
                def _(v, m=m):
                    b.wait(v, sT)
                    cc = m - 16
                    v.tensor_copy(xbc[:, cc, 0:HALO], ps_xbc[:, 0:HALO])
                    b.inc(v.tensor_copy(xbc[:, cc, HALO:HALO+TT], ps_xbc[:, 512:1536]), sV)
            @block.gpsimd
            def _(g):
                g.wait_ge(b.h(sT), tcounts[16])
                b.dma(g, wb[1][:, :, 0:32], io["windt"].rearrange("(k p) j -> p k j", p=128), sDw[0])
            @block.tensor
            def _(t):
                b.wait(t, sDw[0])
                b.wait(t, sV)
                for k in range(8):
                    t.matmul(ps_dt[:, 0:512], wb[1][:, k, 0:32], x_ln[:, k, o_t:o_t+512],
                             start=(k == 0), stop=(k == 7))
                    ii = t.matmul(ps_dt[:, 512:1024], wb[1][:, k, 0:32], x_ln[:, k, o_t+512:o_t+1024],
                                  start=(k == 0), stop=(k == 7))
                b.inc(ii, sT)
            @block.vector
            def _(v):
                b.wait(v, sT)
                b.inc(v.tensor_copy(dtraw[:], ps_dt[:]), sV)
        b.barrier()
        b.free("wbA0"); b.free("wbA1")

        for cc in range(17):
            @block.vector
            def _(v, cc=cc):
                if cc == 0:
                    b.wait(v, sV)
                else:
                    b.wait(v, sS)
                b.ch(v, v.tensor_scalar(conv_acc[:], xbc[:, cc, 0:TT], conv_w[:, cc, 0:1], None, ALU.mult), sV)
                for j in range(1, 4):
                    b.ch(v, v.scalar_tensor_tensor(conv_acc[:], xbc[:, cc, j:j+TT], conv_w[:, cc, j:j+1],
                                                   conv_acc[:], ALU.mult, ALU.add), sV)
            @block.scalar
            def _(s, cc=cc):
                b.wait(s, sV)
                b.inc(s.activation(xcv[:, cc, :], conv_acc[:], AF.Silu,
                                   bias=conv_b[:, cc:cc+1], scale=1.0), sS)
        @block.gpsimd
        def _(g):
            b.wait(g, sS)
            b.dma(g, hbm["xcv"][:, :, tt*TT:(tt+1)*TT], xcv[:], sD)
            g.wait_ge(b.h(sD), b.val(sD))
        b.barrier()
        b.free("xbc"); b.free("conv_acc")
        x_dt_T = b.sb("x_dt_T", [128, 8, 2048], BF16)
        dtv = b.sb("dtv", [32, TT], F32)
        rw1 = b.sb("rw1", [32, TT], F32)
        dex_fm = b.sb("dex_fm", [32, TT], F32)

        @block.vector
        def _(v):
            b.wait(v, sV)
            b.inc(v.tensor_scalar(dtraw[:], dtraw[:], dt_bias[:, 0:1], None, ALU.add), sV)
        @block.scalar
        def _(s):
            b.wait(s, sV)
            b.ch(s, s.activation(rw1[:], dtraw[:], AF.Abs, bias=0.0, scale=1.0), sS)
            b.inc(s.activation(rw1[:], rw1[:], AF.Exp, bias=0.0, scale=-1.0), sS)
        @block.vector
        def _(v):
            b.wait(v, sS)
            b.inc(v.tensor_scalar_add(rw1[:], rw1[:], 1.0), sV)
        @block.scalar
        def _(s):
            b.wait(s, sV)
            b.inc(s.activation(rw1[:], rw1[:], AF.Ln, bias=0.0, scale=1.0), sS)
        @block.vector
        def _(v):
            b.wait(v, sS)
            b.ch(v, v.tensor_scalar_max(dtv[:], dtraw[:], 0.0), sV)
            b.ch(v, v.tensor_add(dtv[:], dtv[:], rw1[:]), sV)
            b.ch(v, v.tensor_scalar(rw1[:], dtv[:], A_col[:, 0:1], None, ALU.mult), sV)
            b.ch(v, v.tensor_tensor_scan(acs2[:, tt, :], adt_gate[:], rw1[:], 0.0,
                                         ALU.mult, ALU.add), sV)
            b.ch(v, v.tensor_copy(asum[:, tt*16:(tt+1)*16], acs2[:, tt, CH-1:TT:CH]), sV)
            for c in range(NCH_T):
                ii = v.tensor_scalar(dex_fm[:, c*CH:(c+1)*CH], acs2[:, tt, c*CH:(c+1)*CH],
                                     acs2[:, tt, (c+1)*CH-1:(c+1)*CH], None, ALU.subtract)
            b.inc(ii, sV)
        @block.scalar
        def _(s):
            b.wait(s, sV)
            b.inc(s.activation(dex_fm[:], dex_fm[:], AF.Exp, bias=0.0, scale=-1.0), sS)

        b.barrier()
        for nm in ("dtraw", "rw1"):
            b.free(nm)

        with nc.psum_tensor(f"ps_trf{tt}", [128, 512], F32) as ps_trf, \
             nc.psum_tensor(f"ps_trb{tt}", [128, 512], BF16) as ps_trb:
            for tb in range(8):
                @block.tensor
                def _(t, tb=tb):
                    b.wait(t, sS)
                    b.wait(t, sV)
                    t.transpose(ps_trf[:, 0:32], acs2[:, tt, tb*128:(tb+1)*128], id_f32[0:32, 0:32])
                    t.transpose(ps_trf[:, 32:64], dtv[:, tb*128:(tb+1)*128], id_f32[0:32, 0:32])
                    b.inc(t.transpose(ps_trf[:, 64:96], dex_fm[:, tb*128:(tb+1)*128], id_f32[0:32, 0:32]), sT)
                @block.vector
                def _(v, tb=tb):
                    b.wait(v, sT)
                    v.tensor_copy(acsT[:, tt*8+tb, :], ps_trf[:, 0:32])
                    v.tensor_copy(dtT[:, tt*8+tb, :], ps_trf[:, 32:64])
                    b.inc(v.tensor_copy(dexT[:, tt*8+tb, :], ps_trf[:, 64:96]), sV)
            b.barrier()
            b.free("dtv"); b.free("dex_fm")
            B_T = b.sb("B_T", [128, 8, 16], BF16)
            Bd_T = b.sb("Bd_T", [128, 8, 512], BF16)
            for tb in range(8):
                for kg in range(4):
                    @block.tensor
                    def _(t, tb=tb, kg=kg):
                        b.wait(t, sV)
                        for j in range(4):
                            ii = t.transpose(ps_trb[:, j*128:(j+1)*128],
                                             xcv[:, kg*4+j, tb*128:(tb+1)*128], id_bf[:, :])
                        b.inc(ii, sT)
                    @block.vector
                    def _(v, tb=tb, kg=kg):
                        b.wait(v, sT)
                        o4 = x_dt_T[:, tb, kg*512:(kg+1)*512].rearrange("p (a h2 c) -> p a h2 c", a=4, h2=2)
                        i4 = ps_trb[:].rearrange("p (a h2 c) -> p a h2 c", a=4, h2=2)
                        d4 = dtT[:, tt*8+tb, kg*8:kg*8+8].rearrange("p (a h2) -> p a h2", a=4) \
                            .unsqueeze(3).broadcast_to([128, 4, 2, 64])
                        b.inc(v.tensor_tensor(o4, i4, d4, ALU.mult), sV)
            for tb in range(8):
                @block.tensor
                def _(t, tb=tb):
                    b.wait(t, sV)
                    b.inc(t.transpose(ps_trb[:, 0:16], xcv[0:16, 16, tb*128:(tb+1)*128],
                                      id_bf[0:16, 0:16]), sT)
                @block.vector
                def _(v, tb=tb):
                    b.wait(v, sT)
                    b.inc(v.tensor_copy(B_T[:, tb, :], ps_trb[:, 0:16]), sV)
            for tb in range(8):
                @block.vector
                def _(v, tb=tb):
                    if tb == 0:
                        b.wait(v, sV)
                    o3 = Bd_T[:, tb, :].rearrange("p (h n) -> p h n", h=32)
                    i0 = B_T[:, tb, :].unsqueeze(1).broadcast_to([128, 32, 16])
                    i1 = dexT[:, tt*8+tb, :].unsqueeze(2).broadcast_to([128, 32, 16])
                    ii = v.tensor_tensor(o3, i0, i1, ALU.mult)
                    if tb == 7:
                        b.inc(ii, sV)

        b.barrier()
        with nc.psum_tensor(f"ps_st{tt}", [128, 4096], F32) as ps_st:
            for half in range(2):
                @block.tensor
                def _(t, half=half):
                    b.wait(t, sV)
                    if half == 1:
                        b.wait(t, sV)
                    for cl in range(8):
                        c_local = half*8 + cl
                        rb_, tb = (c_local % 2)*64, c_local // 2
                        for h in range(NH):
                            ii = t.matmul(ps_st[0:64, cl*512 + h*16: cl*512 + (h+1)*16],
                                          x_dt_T[rb_:rb_+64, tb, h*64:(h+1)*64],
                                          Bd_T[rb_:rb_+64, tb, h*16:(h+1)*16],
                                          start=True, stop=True)
                    b.inc(ii, sT)
                @block.vector
                def _(v, half=half):
                    b.wait(v, sT)
                    for cl in range(8):
                        gc = tt*16 + half*8 + cl
                        for hh in range(2):
                            dst = ST3[hh][:, :].rearrange("p (h n c) -> p h n c", h=16, n=16)[:, :, :, gc]
                            src = ps_st[0:64, cl*512+hh*256:cl*512+(hh+1)*256].rearrange("p (h n) -> p h n", h=16)
                            ii = v.tensor_copy(dst, src)
                    b.inc(ii, sV)

        for nm in ("xcv", "x_dt_T", "B_T", "Bd_T"):
            b.free(nm)
        b.barrier()

    if dbg is not None and DBG == "st3":
        @block.gpsimd
        def _(g):
            b.dma(g, dbg[0:64, 0:8192], ST3[0][:], sD)
            b.dma(g, dbg[0:64, 8192:16384], ST3[1][:], sD)
            g.wait_ge(b.h(sD), b.val(sD))
        b.barrier()
        return

    eas = b.sb("eas", [32, 32], F32)
    dprA = b.sb("dprA", [32, 32], F32)
    scanpat = b.sb("scanpat", [64, 1024], F32)
    dpref = b.sb("dpref", [64, 1024], F32)
    spfull = [b.sb("spfulla", [64, 8192], F32, top=True), b.sb("spfullb", [64, 8192], F32, top=True)]
    SC = ST3

    @block.scalar
    def _(s):
        b.wait(s, sV)
        b.inc(s.activation(eas[:], asum[:], AF.Exp, bias=0.0, scale=1.0), sS)
    @block.vector
    def _(v):
        b.wait(v, sV)
        b.ch(v, v.memset(dprA[:, 0:1], 0.0), sV)
        b.ch(v, v.tensor_copy(dprA[:, 1:32], asum[:, 0:31]), sV)
        b.ch(v, v.tensor_tensor_scan(dprA[:], adt_gate[:, 1:33], dprA[:], 0.0, ALU.mult, ALU.add), sV)
    @block.scalar
    def _(s):
        b.wait(s, sV)
        b.inc(s.activation(dprA[:], dprA[:], AF.Exp, bias=0.0, scale=1.0), sS)
    @block.gpsimd
    def _(g):
        b.wait(g, sS)
        b.dma(g, hbm["flat"][0:1, 0:1024], eas[:], sD)
        b.dma(g, hbm["flat"][0:1, 1024:2048], dprA[:], sD)
        g.wait_ge(b.h(sD), b.val(sD))
        b.dma(g, scanpat[0:1, :], hbm["flat"][0:1, 0:1024], sD)
        b.dma(g, dpref[0:1, :], hbm["flat"][0:1, 1024:2048], sD)
    with nc.psum_tensor("ps_bcg", [128, 1024], F32) as ps_bcg:
        @block.tensor
        def _(t):
            b.wait(t, sD)
            t.matmul(ps_bcg[0:64, 0:512], ones_f32r[0:1, 0:64], scanpat[0:1, 0:512], start=True, stop=True)
            b.inc(t.matmul(ps_bcg[0:64, 512:1024], ones_f32r[0:1, 0:64], scanpat[0:1, 512:1024], start=True, stop=True), sT)
        @block.vector
        def _(v):
            b.wait(v, sT)
            b.inc(v.tensor_copy(scanpat[:, :], ps_bcg[0:64, :]), sV)
        @block.tensor
        def _(t):
            b.wait(t, sV)
            t.matmul(ps_bcg[0:64, 0:512], ones_f32r[0:1, 0:64], dpref[0:1, 0:512], start=True, stop=True)
            b.inc(t.matmul(ps_bcg[0:64, 512:1024], ones_f32r[0:1, 0:64], dpref[0:1, 512:1024], start=True, stop=True), sT)
        @block.vector
        def _(v):
            b.wait(v, sT)
            b.ch(v, v.tensor_copy(dpref[:, :], ps_bcg[0:64, :]), sV)
            zc = scanpat[:, :].rearrange("p (h c) -> p h c", h=32)[:, :, 0:1]
            b.ch(v, v.memset(zc, 0.0), sV)
            for hh in range(2):
                o4 = spfull[hh][:, :].rearrange("p (h n c) -> p h n c", h=16, n=16)
                i4 = scanpat[:, :].rearrange("p (h c) -> p h c", h=32)[:, hh*16:(hh+1)*16, :] \
                    .unsqueeze(2).broadcast_to([64, 16, 16, 32])
                b.ch(v, v.tensor_copy(o4, i4), sV)
                ii = v.tensor_tensor_scan(SC[hh][:], spfull[hh][:], ST3[hh][:], 0.0, ALU.mult, ALU.add)
                b.ch(v, ii, sV)

    cstage = b.sb("cstage", [64, 512], F32)
    allg = b.sb("allg", [64, 2, 512], F32)
    @block.vector
    def _(v):
        b.wait(v, sV)
        for hh in range(2):
            ii = v.tensor_copy(cstage[:, hh*256:(hh+1)*256].rearrange("p (h n) -> p h n", h=16),
                               SC[hh][:, :].rearrange("p (h n c) -> p h n c", h=16, n=16)[:, :, :, 31])
        b.inc(ii, sV)
    @block.sync
    def _(sy):
        b.wait(sy, sV)
        b.dma(sy, hbm["cc_min"], cstage[:], sDso)
    @block.gpsimd
    def _(g):
        b.wait(g, sDso)
        b.inc(g.collective_compute("AllGather", ALU.bypass, replica_groups=[list(range(NCC))],
                                   ins=[hbm["cc_min"]], outs=[hbm["cc_mout"]]), sCC)
    @block.vector
    def _(v):
        b.ch(v, v.memset(state_in[:], 0.0), sV)
    for r in range(NCC):
        @block.gpsimd
        def _(g, r=r):
            if r == 0:
                b.wait(g, sCC)
            else:
                b.wait(g, sV)
            b.dma(g, allg[:, r % 2, :], hbm["cc_mout"][r*64:(r+1)*64, :], sD)
        @block.vector
        def _(v, r=r):
            b.wait(v, sD)
            b.ch(v, v.scalar_tensor_tensor(state_in[:], allg[:, r % 2, :], m8_64[:, r:r+1],
                                           state_in[:], ALU.mult, ALU.add), sV)

    SCP = spfull
    SCP_bf = [None, None]
    for hh in range(2):
        @block.vector
        def _(v, hh=hh):
            b.wait(v, sV)
            v4 = SCP[hh][:, :].rearrange("p (h n c) -> p h n c", h=16, n=16)
            s4 = SC[hh][:, :].rearrange("p (h n c) -> p h n c", h=16, n=16)
            b.ch(v, v.memset(v4[:, :, :, 0:1], 0.0), sV)
            b.ch(v, v.tensor_copy(v4[:, :, :, 1:32], s4[:, :, :, 0:31]), sV)
            i0 = state_in[:, hh*256:(hh+1)*256].rearrange("p (h n) -> p h n", h=16) \
                .unsqueeze(3).broadcast_to([64, 16, 16, 32])
            i1 = dpref[:, :].rearrange("p (h c) -> p h c", h=32)[:, hh*16:(hh+1)*16, :] \
                .unsqueeze(2).broadcast_to([64, 16, 16, 32])
            b.ch(v, v.tensor_tensor(s4, i0, i1, ALU.mult), sV)
            b.ch(v, v.tensor_add(SCP[hh][:], SCP[hh][:], SC[hh][:]), sV)
        b.free("ST3a" if hh == 0 else "ST3b")
        SCP_bf[hh] = b.sb("SCP_bfa" if hh == 0 else "SCP_bfb", [64, 8192], BF16, top=True)
        @block.vector
        def _(v, hh=hh):
            b.ch(v, v.tensor_copy(SCP_bf[hh][:], SCP[hh][:]), sV)

    if dbg is not None and DBG == "scp":
        @block.gpsimd
        def _(g):
            b.wait(g, sV)
            b.dma(g, dbg[0:64, 0:8192], SCP[0][:], sD)
            b.dma(g, dbg[0:64, 8192:16384], SCP[1][:], sD)
            g.wait_ge(b.h(sD), b.val(sD))
        b.barrier()
        return

    for nm in ("spfulla", "spfullb", "scanpat", "dpref", "cstage", "allg", "eas", "dprA", "adt_gate"):
        b.free(nm)
    b.barrier()

    z_both = b.sb("z_both", [128, 32, TT], BF16)
    for tz in range(2):
        wb = [b.sb(f"wbZ{i}", [128, 8, 128], BF16) for i in range(2)]
        tcounts = {}
        loop_t0 = b.val(sT)
        with nc.psum_tensor(f"ps_zz{tz}", [128, 1024], F32) as ps_z:
            for m in range(16):
                buf = wb[m % 2]
                @block.gpsimd
                def _(g, m=m, buf=buf, tz=tz):
                    g.wait_ge(b.h(sT), tcounts[m - 2] if m >= 2 else loop_t0)
                    if m < 2:
                        b.wait(g, sV)
                    b.dma(g, buf[:], io["wins"][m].rearrange("(k p) j -> p k j", p=128), sDw[m % 2])
                @block.tensor
                def _(t, m=m, buf=buf, tz=tz):
                    b.wait(t, sDw[m % 2])
                    b.wait(t, sS)
                    o_t = HALO + tz * TT
                    for k in range(8):
                        t.matmul(ps_z[:, 0:512], buf[:, k, :], x_ln[:, k, o_t:o_t+512],
                                 start=(k == 0), stop=(k == 7))
                        ii = t.matmul(ps_z[:, 512:1024], buf[:, k, :], x_ln[:, k, o_t+512:o_t+1024],
                                      start=(k == 0), stop=(k == 7))
                    tcounts[m] = b.inc(ii, sT)
                @block.scalar
                def _(s, m=m, tz=tz):
                    b.wait(s, sT)
                    b.inc(s.activation(z_both[:, tz*16 + m, :], ps_z[:], AF.Silu, bias=0.0, scale=1.0), sS)
        b.free("wbZ0"); b.free("wbZ1")
    b.barrier()
    b.free("x_ln")

    for tt in range(2):
        y_bf = b.sb("y_bf", [128, 16, TT], BF16)
        wb = [b.sb(f"wbB{i}", [128, 8, 128], BF16) for i in range(2)]
        xp = b.sb("xp", [128, 17, 128], BF16)
        xdtp = b.sb("xdtp", [128, 4096], BF16)
        c0tmp = b.sb("c0tmp", [128, 16], BF16)
        C0p = b.sb("C0p", [16, 128], BF16)
        R1E = b.sb("R1E", [128, 2048], F32)
        Gt = b.sb("Gt", [128, 2048], BF16)
        ssm = b.sb("ssm", [128, 64], F32)
        smt = b.sb("smt", [64, 128], F32)
        Cdp = b.sb("Cdp", [16, 4096], BF16)
        ebcp = b.sb("ebcp", [16, 4096], BF16)
        stTp = b.sb("stTp", [16, 4096], BF16)

        @block.vector
        def _(v):
            b.ch(v, v.memset(xdtp[:], 0.0), sV)
            b.ch(v, v.memset(stTp[:], 0.0), sV)

        eacs = b.sb("eacs", [32, TT], BF16)
        @block.scalar
        def _(s):
            b.wait(s, sV)
            b.inc(s.activation(eacs[:], acs2[:, tt, :], AF.Exp, bias=0.0, scale=1.0), sS)
        @block.gpsimd
        def _(g):
            b.wait(g, sS)
            b.dma(g, hbm["acsf"][0:1, :].rearrange("a (c h l) -> (a h) c l", c=16, h=32),
                  acs2[:, tt, :].rearrange("p (c l) -> p c l", c=16), sD)
            b.dma(g, hbm["acsf2"][0:1, :].rearrange("a (c h l) -> (a h) c l", c=16, h=32),
                  eacs[:, :].rearrange("p (c l) -> p c l", c=16), sD)
            g.wait_ge(b.h(sD), b.val(sD))
            b.inc(g.nop(), sG)
        b.free("eacs")

        trc = {}; c0c1 = {}; c0c2 = {}; smc = {}; ssmc = {}; cdc = {}
        minc = {}; gcc = {}; stwin = {}; stlast = {}; yec = {}
        with nc.psum_tensor(f"ps_sc{tt}", [64, 128], F32) as ps_sc, \
             nc.psum_tensor(f"ps_trp{tt}", [128, 512], BF16) as ps_trp, \
             nc.psum_tensor(f"ps_y{tt}", [128, 2048], F32) as ps_y, \
             nc.psum_tensor(f"ps_sm{tt}", [128, 64], F32) as ps_sm, \
             nc.psum_tensor(f"ps_sT{tt}", [16, 512], BF16) as ps_sT:
            for cp in range(8):
                @block.gpsimd
                def _(g, cp=cp):
                    b.wait(g, sV)
                    b.wait(g, sG)
                    b.wait(g, sD)
                    b.dma(g, xp[:], hbm["xcv"][:, :, tt*TT + cp*128: tt*TT + (cp+1)*128], sD)
                    b.dma(g, ebcp[:, 0:2048], hbm["acsf2"][0:1, (cp*2)*2048:(cp*2+1)*2048].broadcast_to([16, 2048]), sD)
                    b.dma(g, ebcp[:, 2048:4096], hbm["acsf2"][0:1, (cp*2+1)*2048:(cp*2+2)*2048].broadcast_to([16, 2048]), sD)
                    b.dma(g, R1E[0:64, :], hbm["acsf"][0:1, (cp*2)*2048:(cp*2+1)*2048].broadcast_to([64, 2048]), sD)
                    b.dma(g, R1E[64:128, :], hbm["acsf"][0:1, (cp*2+1)*2048:(cp*2+2)*2048].broadcast_to([64, 2048]), sD)
                for kg in range(4):
                    @block.tensor
                    def _(t, cp=cp, kg=kg):
                        b.wait(t, sD)
                        if kg >= 1:
                            t.wait_ge(b.h(sV), trc[(cp, kg-1)])
                        elif cp >= 1:
                            t.wait_ge(b.h(sV), c0c2[cp-1])
                        for j in range(4):
                            ii = t.transpose(ps_trp[:, j*128:(j+1)*128],
                                             xp[:, kg*4+j, :], id_bf[:, :])
                        b.inc(ii, sT)
                    @block.vector
                    def _(v, cp=cp, kg=kg):
                        b.wait(v, sT)
                        base = kg * 1024
                        for h2 in range(2):
                            o3 = xdtp[:, base:base+1024].rearrange("p (a q) -> p a q", a=4)[:, :, h2*192:h2*192+64]
                            i3 = ps_trp[:, :].rearrange("p (a q) -> p a q", a=4)[:, :, h2*64:(h2+1)*64]
                            d3 = dtT[:, tt*8+cp, kg*8 + h2: kg*8 + 8: 2].unsqueeze(2).broadcast_to([128, 4, 64])
                            ii = v.tensor_tensor(o3, i3, d3, ALU.mult)
                        trc[(cp, kg)] = b.inc(ii, sV)
                @block.tensor
                def _(t, cp=cp):
                    t.wait_ge(b.h(sV), trc[(cp, 3)])
                    b.inc(t.transpose(ps_trp[:, 0:16], xp[32:48, 16, :], id_bf[32:48, 32:48]), sT)
                @block.vector
                def _(v, cp=cp):
                    b.wait(v, sT)
                    c0c1[cp] = b.inc(v.tensor_copy(c0tmp[:], ps_trp[:, 0:16]), sV)
                @block.tensor
                def _(t, cp=cp):
                    t.wait_ge(b.h(sV), c0c1[cp])
                    b.inc(t.transpose(ps_trp[0:16, 128:256], c0tmp[:], id_bf[:, :]), sT)
                @block.vector
                def _(v, cp=cp):
                    b.wait(v, sT)
                    c0c2[cp] = b.inc(v.tensor_copy(C0p[:], ps_trp[0:16, 128:256]), sV)
                @block.tensor
                def _(t, cp=cp):
                    t.wait_ge(b.h(sV), max(c0c2[cp], smc[cp-1]) if cp >= 1 else c0c2[cp])
                    for ci in range(2):
                        ii = t.matmul(ps_sc[0:64, ci*64:(ci+1)*64],
                                      xp[0:16, 16, ci*64:(ci+1)*64],
                                      C0p[0:16, ci*64:(ci+1)*64],
                                      start=True, stop=True)
                    b.inc(ii, sT)
                @block.vector
                def _(v, cp=cp):
                    b.wait(v, sT)
                    smc[cp] = b.inc(v.tensor_tensor(smt[:], ps_sc[0:64, :], maskST[:], ALU.mult), sV)
                @block.tensor
                def _(t, cp=cp):
                    t.wait_ge(b.h(sV), max(smc[cp], ssmc[cp-1]) if cp >= 1 else smc[cp])
                    t.matmul(ps_sm[:, 0:64], shl_sb[0:64, 0:128], smt[0:64, 0:64],
                             start=True, stop=False)
                    b.inc(t.matmul(ps_sm[:, 0:64], shl_sb[0:64, 128:256], smt[0:64, 64:128],
                                   start=False, stop=True), sT)
                @block.vector
                def _(v, cp=cp):
                    b.wait(v, sT)
                    b.ch(v, v.tensor_copy(ssm[:], ps_sm[:, 0:64]), sV)
                    ssmc[cp] = b.val(sV)
                    b.wait(v, sD)
                    o3 = Cdp[:, :].rearrange("p (ci h l) -> p ci h l", ci=2, h=32)
                    i0 = C0p[:, :].rearrange("p (ci l) -> p ci l", ci=2).unsqueeze(2).broadcast_to([16, 2, 32, 64])
                    i1 = ebcp[:, :].rearrange("p (ci h l) -> p ci h l", ci=2, h=32)
                    cdc[cp] = b.inc(v.tensor_tensor(o3, i0, i1, ALU.mult), sV)
                @block.vector
                def _(v, cp=cp):
                    b.wait(v, sD)
                    i1 = acsT[:, tt*8+cp, :].unsqueeze(2).broadcast_to([128, 32, 64])
                    b.ch(v, v.tensor_tensor(R1E[:, :].rearrange("p (h l) -> p h l", h=32),
                                            R1E[:, :].rearrange("p (h l) -> p h l", h=32), i1,
                                            ALU.subtract), sV)
                    minc[cp] = b.inc(v.tensor_scalar_min(R1E[:], R1E[:], 0.0), sV)
                @block.scalar
                def _(s, cp=cp):
                    s.wait_ge(b.h(sV), minc[cp])
                    b.inc(s.activation(R1E[:], R1E[:], AF.Exp, bias=0.0, scale=1.0), sS)
                @block.vector
                def _(v, cp=cp):
                    b.wait(v, sS)
                    i1 = ssm[:, :].unsqueeze(1).broadcast_to([128, 32, 64])
                    gcc[cp] = b.inc(v.tensor_tensor(Gt[:, :].rearrange("p (h l) -> p h l", h=32),
                                                    R1E[:, :].rearrange("p (h l) -> p h l", h=32), i1,
                                                    ALU.mult), sV)
                for ci in range(2):
                    gc = tt*16 + cp*2 + ci
                    for hg in range(4):
                        @block.tensor
                        def _(t, gc=gc, hg=hg):
                            if 0 in stwin:
                                t.wait_ge(b.h(sV), stwin[0])
                            for hh in range(8):
                                h = hg*8 + hh
                                src = SCP_bf[h // 16][:, :].rearrange("p (h n c) -> p h n c", h=16, n=16)[:, h % 16, :, gc]
                                ii = t.transpose(ps_sT[0:16, hh*64:(hh+1)*64],
                                                 src, id_bf[0:64, 0:64])
                            b.inc(ii, sT)
                        @block.vector
                        def _(v, hg=hg):
                            b.wait(v, sT)
                            for h2 in range(2):
                                dsl = stTp[0:16, hg*1024:(hg+1)*1024] \
                                    .rearrange("p (hp q) -> p hp q", hp=4)[:, :, h2*192:h2*192+64]
                                ssl = ps_sT[0:16, :].rearrange("p (hp q) -> p hp q", hp=4)[:, :, h2*64:(h2+1)*64]
                                ii = v.tensor_copy(dsl, ssl)
                            stwin[0] = b.inc(ii, sV)
                    @block.tensor
                    def _(t, cp=cp, ci=ci):
                        t.wait_ge(b.h(sV), max(gcc[cp], cdc[cp], trc[(cp, 3)], stwin[0]))
                        for hp in range(16):
                            co = ci*1024 + hp*64
                            for h2 in range(2):
                                h = 2*hp + h2
                                t.matmul(ps_y[:, co:co+64],
                                         xdtp[ci*64:(ci+1)*64, hp*256 + h2*128: hp*256 + (h2+1)*128],
                                         Gt[ci*64:(ci+1)*64, h*64:(h+1)*64],
                                         start=(h2 == 0), stop=False)
                                ii = t.matmul(ps_y[:, co:co+64],
                                              stTp[0:16, hp*256 + h2*128: hp*256 + (h2+1)*128],
                                              Cdp[0:16, ci*2048 + h*64: ci*2048 + (h+1)*64],
                                              start=False, stop=(h2 == 1))
                        b.inc(ii, sT)
                @block.vector
                def _(v, cp=cp):
                    b.wait(v, sT)
                    y4 = y_bf[:, :, cp*128:(cp+1)*128].rearrange("p q (ci l) -> p q ci l", ci=2)
                    x4 = xp[:, 0:16, :].rearrange("p q (ci l) -> p q ci l", ci=2)
                    z4 = z_both[:, tt*16:tt*16+16, cp*128:(cp+1)*128].rearrange("p q (ci l) -> p q ci l", ci=2)
                    p4 = ps_y[:, :].rearrange("p (ci q l) -> p ci q l", ci=2, q=16).transpose([0, 2, 1, 3])
                    d4 = D0[:, 0:16].unsqueeze(2).unsqueeze(3).broadcast_to([128, 16, 2, 64])
                    t4 = R1E[:, :].rearrange("p (q ci l) -> p q ci l", q=16, ci=2)
                    b.ch(v, v.tensor_tensor(t4, x4, d4, ALU.mult), sV)
                    b.ch(v, v.tensor_add(t4, t4, p4), sV)
                    b.ch(v, v.tensor_tensor(y4, t4, z4, ALU.mult), sV)
                    yec[cp] = b.val(sV)

        if dbg is not None and DBG == "pair0" and tt == 0:
            @block.gpsimd
            def _(g):
                b.wait(g, sV)
                b.dma(g, dbg[:, 0:2048], R1E[:], sD)
                b.dma(g, dbg[:, 2048:4096], Gt[:], sD)
                b.dma(g, dbg[0:16, 4096:8192], Cdp[:], sD)
                b.dma(g, dbg[0:16, 8192:12288], stTp[:], sD)
                b.dma(g, dbg[:, 12288:14336], xdtp[:], sD)
                b.dma(g, dbg[:, 14336:14400], ssm[:], sD)
                b.dma(g, dbg[0:16, 14400:14528], C0p[:], sD)
                for hp in range(4):
                    b.dma(g, dbg[:, 14528 + hp*128: 14528 + (hp+1)*128], y_bf[:, hp, 0:128], sD)
                b.dma(g, dbg[:, 15040: 15040 + 17*128], xp[:], sD)
                b.dma(g, dbg[0:32, 17216:17216+1024], acs2[:, 0, :], sD)
                g.wait_ge(b.h(sD), b.val(sD))
            b.barrier()
            return

        if dbg is not None and DBG == "y" and tt == 0:
            @block.gpsimd
            def _(g):
                b.wait(g, sV)
                for kc in range(16):
                    b.dma(g, dbg[:, kc*1024:(kc+1)*1024], y_bf[:, kc, :], sD)
                g.wait_ge(b.h(sD), b.val(sD))
            b.barrier()
            return

        for nm in ("xp", "xdtp", "c0tmp", "C0p", "R1E", "Gt", "ssm", "smt",
                   "Cdp", "ebcp", "stTp"):
            b.free(nm)
        sqb = b.sb("sqbB", [128, 16, TT], BF16)
        rrow = b.sb("rrowB", [1, TT], F32)
        rmsb = b.sb("rmsb", [128, TT], F32)
        @block.scalar
        def _(s):
            b.wait(s, sV)
            for kc in range(16):
                ii = s.activation(sqb[:, kc, :], y_bf[:, kc, :], AF.Square, bias=0.0, scale=1.0)
            b.inc(ii, sS)
        with nc.psum_tensor(f"ps_r{tt}", [1, 1024], F32) as ps_r, \
             nc.psum_tensor(f"ps_rb{tt}", [128, 1024], F32) as ps_rb:
            @block.tensor
            def _(t):
                b.wait(t, sS)
                for (o, n) in ((0, 512), (512, 512)):
                    for kc in range(16):
                        ii = t.matmul(ps_r[0:1, o:o+n], ones_bf[:, 0:1], sqb[:, kc, o:o+n],
                                      start=(kc == 0), stop=(kc == 15))
                b.inc(ii, sT)
            @block.vector
            def _(v):
                b.wait(v, sT)
                b.ch(v, v.tensor_copy(rrow[:], ps_r[:]), sV)
                b.ch(v, v.tensor_scalar_mul(rrow[:], rrow[:], 1.0 / DI), sV)
                b.inc(v.tensor_scalar_add(rrow[:], rrow[:], 1e-5), sV)
            @block.scalar
            def _(s):
                b.wait(s, sV)
                b.inc(s.activation(rrow[:], rrow[:], AF.Sqrt, bias=0.0, scale=1.0), sS)
            @block.vector
            def _(v):
                b.wait(v, sS)
                b.inc(v.reciprocal(rrow[:], rrow[:]), sV)
            @block.tensor
            def _(t):
                b.wait(t, sV)
                t.matmul(ps_rb[:, 0:512], ones_f32r[0:1, :], rrow[0:1, 0:512], start=True, stop=True)
                b.inc(t.matmul(ps_rb[:, 512:1024], ones_f32r[0:1, :], rrow[0:1, 512:1024], start=True, stop=True), sT)
            @block.vector
            def _(v):
                b.wait(v, sT)
                b.ch(v, v.tensor_copy(rmsb[:], ps_rb[:]), sV)
                for kc in range(16):
                    b.ch(v, v.scalar_tensor_tensor(y_bf[:, kc, :], y_bf[:, kc, :],
                                                   norm_w[:, kc:kc+1], rmsb[:], ALU.mult, ALU.mult), sV)

        for nm in ("sqbB", "rrowB", "rmsb"):
            b.free(nm)
        u_t = b.sb("u_tB", [128, 8, TT], F32)
        x1sb = b.sb("x1sb", [128, TT], F32)
        wo = [b.sb(f"woB{i}", [128, 16, 128], BF16) for i in range(2)]
        @block.sync
        def _(sy):
            b.wait(sy, sV)
            b.wait(sy, sS)
            b.dma(sy, u_t[:], io["u_fm"][:, :, HALO + tt*TT: HALO + (tt+1)*TT], sDs)
        tcounts = {}
        loop_t0 = b.val(sT)
        with nc.psum_tensor(f"ps_wo{tt}", [128, 1024], F32) as ps_wo:
            for m in range(8):
                buf = wo[m % 2]
                @block.gpsimd
                def _(g, m=m, buf=buf):
                    g.wait_ge(b.h(sT), tcounts[m - 2] if m >= 2 else loop_t0)
                    if m < 2:
                        b.wait(g, sV)
                    b.dma(g, buf[:], io["wouts"][m].rearrange("(k p) j -> p k j", p=128), sDw[m % 2])
                @block.tensor
                def _(t, m=m, buf=buf):
                    b.wait(t, sDw[m % 2])
                    if m >= 1:
                        b.wait(t, sV)
                    for k in range(16):
                        t.matmul(ps_wo[:, 0:512], buf[:, k, :], y_bf[:, k, 0:512],
                                 start=(k == 0), stop=(k == 15))
                        ii = t.matmul(ps_wo[:, 512:1024], buf[:, k, :], y_bf[:, k, 512:1024],
                                      start=(k == 0), stop=(k == 15))
                    tcounts[m] = b.inc(ii, sT)
                @block.vector
                def _(v, m=m):
                    b.wait(v, sT)
                    b.wait(v, sDs)
                    b.wait(v, sDso)
                    b.inc(v.tensor_add(x1sb[:], ps_wo[:], u_t[:, m, :]), sV)
                @block.sync
                def _(sy, m=m):
                    b.wait(sy, sV)
                    b.dma(sy, hbm["x1"][:, m, tt*TT:(tt+1)*TT], x1sb[:], sDso)

        @block.sync
        def _(sy):
            b.wait(sy, sDso)
        for nm in ("y_bf", "wbB0", "wbB1", "u_tB", "x1sb", "woB0", "woB1"):
            b.free(nm)
        b.barrier()

    for nm in ("acs2", "acsT", "dtT", "dexT", "asum", "state_in", "SCP_bfa", "SCP_bfb", "z_both"):
        b.free(nm)
    b.barrier()

def emit_ffn0(b, io, hbm, C, lns):
    nc, block = b.nc, b.block
    sD, sDs, sT, sV, sS, sG, sCC = C["sems"]
    sDw = C["sDw"]; sDso = C["sDso"]

    s1c = b.sb("s1c", [128, 32], F32)
    s2c = b.sb("s2c", [128, 32], F32)
    sh1c = b.sb("sh1c", [128, 32], F32)
    sh2c = b.sb("sh2c", [128, 32], F32)
    s3c = b.sb("s3c", [128, 8], F32)

    @block.gpsimd
    def _(g):
        for dst, src in ((s1c, "s1"), (s2c, "s2"), (sh1c, "sh1"), (sh2c, "sh2"), (s3c, "s3")):
            b.dma(g, dst[:], io[src], sD)
        g.wait_ge(b.h(sD), b.val(sD))
        b.inc(g.nop(), sG)
    b.barrier()

    for tt in range(2):
        x_ln2 = b.sb("f0xln", [128, 8, TT], BF16)
        emit_ln(b, hbm["x1"][:, :, tt*TT:(tt+1)*TT], x_ln2, TT, lns, f"lf0{tt}")

        hsb = b.sb("f0h", [128, 32, TT], BF16)
        wA = [b.sb(f"f0wa{i}", [128, 8, 128], BF16) for i in range(2)]
        wB = [b.sb(f"f0wb{i}", [128, 8, 128], BF16) for i in range(2)]
        sa_t = b.sb("f0sa", [128, 2, TT], F32)
        tcounts = {}; vcounts = {}; scounts = {}
        t0_ = b.val(sT)
        with nc.psum_tensor(f"f0psA{tt}", [128, 2048], F32) as psA, \
             nc.psum_tensor(f"f0psB{tt}", [128, 2048], F32) as psB:
            for m in range(32):
                mh = m % 2
                @block.gpsimd
                def _(g, m=m, mh=mh):
                    g.wait_ge(b.h(sT), tcounts[m-2] if m >= 2 else t0_)
                    b.dma(g, wA[mh][:], io["w1s"][m].rearrange("(k p) j -> p k j", p=128), sDw[mh])
                    b.dma(g, wB[mh][:], io["w2s"][m].rearrange("(k p) j -> p k j", p=128), sDw[mh])
                @block.tensor
                def _(t, m=m, mh=mh):
                    b.wait(t, sDw[mh])
                    if m >= 2:
                        t.wait_ge(b.h(sS), scounts[m-2])
                        t.wait_ge(b.h(sV), vcounts[m-2])
                    for th in range(2):
                        o = mh*1024 + th*512
                        for k in range(8):
                            t.matmul(psA[:, o:o+512], wA[mh][:, k, :],
                                     x_ln2[:, k, th*512:(th+1)*512], start=(k == 0), stop=(k == 7))
                    for th in range(2):
                        o = mh*1024 + th*512
                        for k in range(8):
                            ii = t.matmul(psB[:, o:o+512], wB[mh][:, k, :],
                                          x_ln2[:, k, th*512:(th+1)*512], start=(k == 0), stop=(k == 7))
                    tcounts[m] = b.inc(ii, sT)
                @block.scalar
                def _(s, m=m, mh=mh):
                    b.wait(s, sT)
                    if m >= 2:
                        s.wait_ge(b.h(sV), vcounts[m-2])
                    scounts[m] = b.inc(s.activation(sa_t[:, mh, :], psA[:, mh*1024:(mh+1)*1024],
                                                    AF.Silu, bias=0.0, scale=s1c[:, m:m+1]), sS)
                @block.vector
                def _(v, m=m, mh=mh):
                    b.wait(v, sS)
                    vcounts[m] = b.inc(v.scalar_tensor_tensor(hsb[:, m, :], psB[:, mh*1024:(mh+1)*1024],
                                                              s2c[:, m:m+1], sa_t[:, mh, :],
                                                              ALU.mult, ALU.mult), sV)
        b.barrier()
        for nm in ("f0xln", "f0wa0", "f0wa1", "f0wb0", "f0wb1", "f0sa"):
            b.free(nm)

        prev = hsb
        for (ph, wkey, sc) in (("B", "h1s", sh1c), ("C", "h2s", sh2c)):
            gout = b.sb(f"f0g{ph}", [128, 32, TT], BF16)
            wH = [b.sb(f"f0wh{ph}{i}", [128, 32, 128], BF16) for i in range(2)]
            tcounts = {}; scounts = {}
            t0_ = b.val(sT)
            with nc.psum_tensor(f"f0psg{ph}{tt}", [128, 2048], F32) as psG:
                for m in range(32):
                    mh = m % 2
                    @block.gpsimd
                    def _(g, m=m, mh=mh, wkey=wkey):
                        g.wait_ge(b.h(sT), tcounts[m-2] if m >= 2 else t0_)
                        b.dma(g, wH[mh][:], io[wkey][m].rearrange("(k p) j -> p k j", p=128), sDw[mh])
                    @block.tensor
                    def _(t, m=m, mh=mh, prev=prev):
                        b.wait(t, sDw[mh])
                        if m >= 2:
                            t.wait_ge(b.h(sS), scounts[m-2])
                        for th in range(2):
                            o = mh*1024 + th*512
                            for k in range(32):
                                ii = t.matmul(psG[:, o:o+512], wH[mh][:, k, :],
                                              prev[:, k, th*512:(th+1)*512], start=(k == 0), stop=(k == 31))
                        tcounts[m] = b.inc(ii, sT)
                    @block.scalar
                    def _(s, m=m, mh=mh, sc=sc, gout=gout):
                        b.wait(s, sT)
                        scounts[m] = b.inc(s.activation(gout[:, m, :], psG[:, mh*1024:(mh+1)*1024],
                                                        AF.Gelu, bias=0.0, scale=sc[:, m:m+1]), sS)
            b.barrier()
            b.free("f0h" if ph == "B" else "f0gB")
            b.free(f"f0wh{ph}0"); b.free(f"f0wh{ph}1")
            prev = gout

        g2 = prev
        x1t = b.sb("f0x1t", [128, 8, TT], F32)
        x1o = [b.sb(f"f0x1o{i}", [128, TT], F32) for i in range(2)]
        wO = [b.sb(f"f0wo{i}", [128, 32, 128], BF16) for i in range(2)]
        @block.sync
        def _(sy):
            b.dma(sy, x1t[:], hbm["x1"][:, :, tt*TT:(tt+1)*TT], sDs)
        tcounts = {}; vcounts = {}; socounts = {}
        t0_ = b.val(sT)
        with nc.psum_tensor(f"f0psO{tt}", [128, 2048], F32) as psO:
            for m in range(8):
                mh = m % 2
                @block.gpsimd
                def _(g, m=m, mh=mh):
                    g.wait_ge(b.h(sT), tcounts[m-2] if m >= 2 else t0_)
                    b.dma(g, wO[mh][:], io["w3s"][m].rearrange("(k p) j -> p k j", p=128), sDw[mh])
                @block.tensor
                def _(t, m=m, mh=mh):
                    b.wait(t, sDw[mh])
                    if m >= 2:
                        t.wait_ge(b.h(sV), vcounts[m-2])
                    for th in range(2):
                        o = mh*1024 + th*512
                        for k in range(32):
                            ii = t.matmul(psO[:, o:o+512], wO[mh][:, k, :],
                                          g2[:, k, th*512:(th+1)*512], start=(k == 0), stop=(k == 31))
                    tcounts[m] = b.inc(ii, sT)
                @block.vector
                def _(v, m=m, mh=mh):
                    b.wait(v, sT)
                    b.wait(v, sDs)
                    if m >= 2:
                        b.wait(v, sDso, socounts[m-2])
                    vcounts[m] = b.inc(v.scalar_tensor_tensor(x1o[mh][:], psO[:, mh*1024:(mh+1)*1024],
                                                              s3c[:, m:m+1], x1t[:, m, :],
                                                              ALU.mult, ALU.add), sV)
                @block.sync
                def _(sy, m=m, mh=mh):
                    b.wait(sy, sV)
                    if m >= 1:
                        sy.wait_ge(b.h(sDso), socounts[m-1])
                    socounts[m] = b.dma(sy, hbm["x1"][:, m, tt*TT:(tt+1)*TT], x1o[mh][:], sDso)
        @block.sync
        def _(sy):
            b.wait(sy, sDso)
        b.barrier()
        for nm in ("f0gC", "f0x1t", "f0x1o0", "f0x1o1", "f0wo0", "f0wo1"):
            b.free(nm)
    for nm in ("s1c", "s2c", "sh1c", "sh2c", "s3c"):
        b.free(nm)
    b.barrier()


def emit_lru(b, io, hbm, C, lns):
    nc, block = b.nc, b.block
    sD, sDs, sT, sV, sS, sG, sCC = C["sems"]
    sDso = C["sDso"]
    l1A = C["l1A"]; l1B = C["l1B"]; l1C = C["l1C"]; l1D = C["l1D"]
    m8_128 = C["m8_128"]

    NZL = T // TC1
    u = b.sb("lru_u", [128, 8, T], F32)
    emit_ln(b, hbm["x1"], u, T, lns, "lnlru", u_buf=u)

    Apr = b.sb("lApr", [128, 18, 8], F32); Api = b.sb("lApi", [128, 18, 8], F32)
    krt = b.sb("lkr", [128, 8, 8], F32)
    Wr2 = b.sb("lWr", [128, 8, 8], F32); Wi2 = b.sb("lWi", [128, 8, 8], F32)
    Prt = b.sb("lPr", [128, 8, 8], F32); Pnt = b.sb("lPn", [128, 8, 8], F32)
    D1r = b.sb("lD1r", [128, 8, 17], F32); D1i = b.sb("lD1i", [128, 8, 17], F32)
    D2qr = b.sb("lD2qr", [128, 8, 16], F32); D2qi = b.sb("lD2qi", [128, 8, 16], F32)
    ta8 = b.sb("lta8", [128, 8, 8], F32); tb8 = b.sb("ltb8", [128, 8, 8], F32)
    CBr = b.sb("lCBr", [128, 8], F32); CBi = b.sb("lCBi", [128, 8], F32)
    hin = b.sb("lhin", [128, 16], F32)
    Ar = l1A[:, 0:8]; Ai = l1A[:, 8:16]
    Br = l1B[:, 0:8]; Bi = l1B[:, 8:16]
    Cr = l1C[:, 0:8]; Ci = l1C[:, 8:16]

    def cmulb(v, outr, outi, ar, ai, br, bi, ta, tb):
        b.ch(v, v.tensor_tensor(ta, ar, br, ALU.mult), sV)
        b.ch(v, v.tensor_tensor(tb, ai, bi, ALU.mult), sV)
        b.ch(v, v.tensor_sub(outr, ta, tb), sV)
        b.ch(v, v.tensor_tensor(ta, ar, bi, ALU.mult), sV)
        b.ch(v, v.tensor_tensor(tb, ai, br, ALU.mult), sV)
        b.ch(v, v.tensor_add(outi, ta, tb), sV)

    @block.vector
    def _(v):
        b.ch(v, v.memset(Apr[:, 0, :], 1.0), sV)
        b.ch(v, v.memset(Api[:, 0, :], 0.0), sV)
        b.ch(v, v.tensor_copy(Apr[:, 1, :], Ar), sV)
        b.ch(v, v.tensor_copy(Api[:, 1, :], Ai), sV)
        for s in (1, 2, 4, 8, 16):
            n = min(s, 17 - s)
            if n <= 0:
                continue
            br_ = Apr[:, s, :].unsqueeze(1).broadcast_to([128, n, 8])
            bi_ = Api[:, s, :].unsqueeze(1).broadcast_to([128, n, 8])
            cmulb(v, Apr[:, s+1:s+1+n, :], Api[:, s+1:s+1+n, :],
                  Apr[:, 1:1+n, :], Api[:, 1:1+n, :], br_, bi_,
                  ta8[:, 0:n, :], tb8[:, 0:n, :])
        cmulb(v, CBr[:], CBi[:], Cr, Ci, Br, Bi, ta8[:, 0, :], tb8[:, 0, :])
        cbr_b = CBr[:, :].unsqueeze(1).broadcast_to([128, 8, 8])
        cbi_b = CBi[:, :].unsqueeze(1).broadcast_to([128, 8, 8])
        b.ch(v, v.tensor_tensor(krt[:], cbr_b, Apr[:, 0:8, :], ALU.mult), sV)
        b.ch(v, v.tensor_tensor(ta8[:], cbi_b, Api[:, 0:8, :], ALU.mult), sV)
        b.ch(v, v.tensor_sub(krt[:], krt[:], ta8[:]), sV)
        br_b = Br.unsqueeze(1).broadcast_to([128, 8, 8])
        bi_b = Bi.unsqueeze(1).broadcast_to([128, 8, 8])
        cmulb(v, Wr2[:], Wi2[:], Apr[:, 0:8, :], Api[:, 0:8, :], br_b, bi_b, ta8[:], tb8[:])
        cr_b = Cr.unsqueeze(1).broadcast_to([128, 8, 8])
        ci_b = Ci.unsqueeze(1).broadcast_to([128, 8, 8])
        cmulb(v, Prt[:], Pnt[:], cr_b, ci_b, Apr[:, 1:9, :], Api[:, 1:9, :], ta8[:], tb8[:])
        b.ch(v, v.tensor_scalar(Pnt[:], Pnt[:], -1.0, None, ALU.mult), sV)
        b.ch(v, v.memset(D1r[:, :, 0], 1.0), sV)
        b.ch(v, v.memset(D1i[:, :, 0], 0.0), sV)
        b.ch(v, v.tensor_copy(D1r[:, :, 1], Apr[:, 8, :]), sV)
        b.ch(v, v.tensor_copy(D1i[:, :, 1], Api[:, 8, :]), sV)
        for s in (1, 2, 4, 8):
            n = min(s, 16 - s)
            if n <= 0:
                continue
            br_ = D1r[:, :, s].unsqueeze(2).broadcast_to([128, 8, n])
            bi_ = D1i[:, :, s].unsqueeze(2).broadcast_to([128, 8, n])
            cmulb(v, D1r[:, :, s+1:s+1+n], D1i[:, :, s+1:s+1+n],
                  D1r[:, :, 1:1+n], D1i[:, :, 1:1+n], br_, bi_,
                  ta8[:, :, 0:n], tb8[:, :, 0:n])
        b.ch(v, v.memset(D2qr[:, :, 0], 1.0), sV)
        b.ch(v, v.memset(D2qi[:, :, 0], 0.0), sV)
        b.ch(v, v.tensor_copy(D2qr[:, :, 1], D1r[:, :, 16]), sV)
        b.ch(v, v.tensor_copy(D2qi[:, :, 1], D1i[:, :, 16]), sV)
        for s in (1, 2, 4, 8):
            n = min(s, 15 - s)
            if n <= 0:
                continue
            br_ = D2qr[:, :, s].unsqueeze(2).broadcast_to([128, 8, n])
            bi_ = D2qi[:, :, s].unsqueeze(2).broadcast_to([128, 8, n])
            cmulb(v, D2qr[:, :, s+1:s+1+n], D2qi[:, :, s+1:s+1+n],
                  D2qr[:, :, 1:1+n], D2qi[:, :, 1:1+n], br_, bi_,
                  ta8[:, :, 0:n], tb8[:, :, 0:n])

    yw = b.sb("lyw", [128, 8, T], F32)
    @block.vector
    def _(v):
        for kc in range(8):
            b.ch(v, v.tensor_scalar(yw[:, kc, :], u[:, kc, :], krt[:, 0, kc:kc+1], None, ALU.mult), sV)
        for l in range(1, TC1):
            for kc in range(8):
                o3 = yw[:, kc, :].rearrange("p (nz l8) -> p nz l8", l8=TC1)[:, :, l:TC1]
                i3 = u[:, kc, :].rearrange("p (nz l8) -> p nz l8", l8=TC1)[:, :, 0:TC1-l]
                b.ch(v, v.scalar_tensor_tensor(o3, i3, krt[:, l, kc:kc+1], o3, ALU.mult, ALU.add), sV)

    Sr = b.sb("lSr", [128, 8, NZL], F32); Si = b.sb("lSi", [128, 8, NZL], F32)
    tm = [b.sb(f"ltm{i}", [128, 8, NZL], F32) for i in range(4)]
    @block.vector
    def _(v):
        for j in range(TC1):
            u3 = u[:, :, :].rearrange("p kc (nz l8) -> p kc nz l8", l8=TC1)[:, :, :, j]
            wr_b = Wr2[:, TC1-1-j, :].unsqueeze(2).broadcast_to([128, 8, NZL])
            wi_b = Wi2[:, TC1-1-j, :].unsqueeze(2).broadcast_to([128, 8, NZL])
            if j == 0:
                b.ch(v, v.tensor_tensor(Sr[:], u3, wr_b, ALU.mult), sV)
                b.ch(v, v.tensor_tensor(Si[:], u3, wi_b, ALU.mult), sV)
            else:
                b.ch(v, v.tensor_tensor(tm[0][:], u3, wr_b, ALU.mult), sV)
                b.ch(v, v.tensor_add(Sr[:], Sr[:], tm[0][:]), sV)
                b.ch(v, v.tensor_tensor(tm[0][:], u3, wi_b, ALU.mult), sV)
                b.ch(v, v.tensor_add(Si[:], Si[:], tm[0][:]), sV)

    hr = b.sb("lhr", [128, 8, NZL], F32); hi = b.sb("lhi", [128, 8, NZL], F32)
    @block.vector
    def _(v):
        b.ch(v, v.tensor_copy(hr[:], Sr[:]), sV)
        b.ch(v, v.tensor_copy(hi[:], Si[:]), sV)
        for s in (1, 2, 4, 8):
            n = TC2 - s
            Hr_rd = hr[:, :, :].rearrange("p kc (nq c) -> p kc nq c", c=TC2)[:, :, :, 0:n]
            Hi_rd = hi[:, :, :].rearrange("p kc (nq c) -> p kc nq c", c=TC2)[:, :, :, 0:n]
            Hr_wr = hr[:, :, :].rearrange("p kc (nq c) -> p kc nq c", c=TC2)[:, :, :, s:TC2]
            Hi_wr = hi[:, :, :].rearrange("p kc (nq c) -> p kc nq c", c=TC2)[:, :, :, s:TC2]
            dr_b = D1r[:, :, s].unsqueeze(2).unsqueeze(3).broadcast_to([128, 8, NQ, n])
            di_b = D1i[:, :, s].unsqueeze(2).unsqueeze(3).broadcast_to([128, 8, NQ, n])
            t1 = tm[0][:, :, :].rearrange("p kc (nq c) -> p kc nq c", c=TC2)[:, :, :, 0:n]
            t2 = tm[1][:, :, :].rearrange("p kc (nq c) -> p kc nq c", c=TC2)[:, :, :, 0:n]
            t3 = tm[2][:, :, :].rearrange("p kc (nq c) -> p kc nq c", c=TC2)[:, :, :, 0:n]
            t4 = tm[3][:, :, :].rearrange("p kc (nq c) -> p kc nq c", c=TC2)[:, :, :, 0:n]
            b.ch(v, v.tensor_tensor(t1, Hr_rd, dr_b, ALU.mult), sV)
            b.ch(v, v.tensor_tensor(t2, Hi_rd, di_b, ALU.mult), sV)
            b.ch(v, v.tensor_tensor(t3, Hr_rd, di_b, ALU.mult), sV)
            b.ch(v, v.tensor_tensor(t4, Hi_rd, dr_b, ALU.mult), sV)
            b.ch(v, v.tensor_sub(t1, t1, t2), sV)
            b.ch(v, v.tensor_add(Hr_wr, Hr_wr, t1), sV)
            b.ch(v, v.tensor_add(t3, t3, t4), sV)
            b.ch(v, v.tensor_add(Hi_wr, Hi_wr, t3), sV)

    csr = b.sb("lcsr", [128, 8, 16], F32); csi = b.sb("lcsi", [128, 8, 16], F32)
    cpr = b.sb("lcpr", [128, 8, 16], F32); cpi = b.sb("lcpi", [128, 8, 16], F32)
    @block.vector
    def _(v):
        b.ch(v, v.tensor_copy(csr[:], hr[:, :, TC2-1::TC2]), sV)
        b.ch(v, v.tensor_copy(csi[:], hi[:, :, TC2-1::TC2]), sV)
        for s in (1, 2, 4, 8):
            n = 16 - s
            dr_b = D2qr[:, :, s].unsqueeze(2).broadcast_to([128, 8, n])
            di_b = D2qi[:, :, s].unsqueeze(2).broadcast_to([128, 8, n])
            t1 = tm[0][:, :, 0:n]; t2 = tm[1][:, :, 0:n]
            t3 = tm[2][:, :, 0:n]; t4 = tm[3][:, :, 0:n]
            b.ch(v, v.tensor_tensor(t1, csr[:, :, 0:n], dr_b, ALU.mult), sV)
            b.ch(v, v.tensor_tensor(t2, csi[:, :, 0:n], di_b, ALU.mult), sV)
            b.ch(v, v.tensor_tensor(t3, csr[:, :, 0:n], di_b, ALU.mult), sV)
            b.ch(v, v.tensor_tensor(t4, csi[:, :, 0:n], dr_b, ALU.mult), sV)
            b.ch(v, v.tensor_sub(t1, t1, t2), sV)
            b.ch(v, v.tensor_add(csr[:, :, s:16], csr[:, :, s:16], t1), sV)
            b.ch(v, v.tensor_add(t3, t3, t4), sV)
            b.ch(v, v.tensor_add(csi[:, :, s:16], csi[:, :, s:16], t3), sV)

    cst2 = b.sb("lcst", [128, 16], F32)
    alg2 = b.sb("lalg", [128, 2, 16], F32)
    @block.vector
    def _(v):
        b.ch(v, v.tensor_copy(cst2[:, 0:8], csr[:, :, 15]), sV)
        b.inc(v.tensor_copy(cst2[:, 8:16], csi[:, :, 15]), sV)
    @block.sync
    def _(sy):
        b.wait(sy, sV)
        b.dma(sy, hbm["cc_lin"], cst2[:], sDso)
    @block.gpsimd
    def _(g):
        b.wait(g, sDso)
        b.inc(g.collective_compute("AllGather", ALU.bypass, replica_groups=[list(range(NCC))],
                                   ins=[hbm["cc_lin"]], outs=[hbm["cc_lout"]]), sCC)
    @block.vector
    def _(v):
        b.ch(v, v.memset(hin[:], 0.0), sV)
    for r in range(NCC):
        @block.gpsimd
        def _(g, r=r):
            if r == 0:
                b.wait(g, sCC)
            else:
                b.wait(g, sV)
            b.dma(g, alg2[:, r % 2, :], hbm["cc_lout"][r*128:(r+1)*128, :], sD)
        @block.vector
        def _(v, r=r):
            b.wait(v, sD)
            b.ch(v, v.scalar_tensor_tensor(hin[:], alg2[:, r % 2, :], m8_128[:, r:r+1],
                                           hin[:], ALU.mult, ALU.add), sV)

    @block.vector
    def _(v):
        b.wait(v, sV)
        b.ch(v, v.memset(cpr[:, :, 0:1], 0.0), sV)
        b.ch(v, v.memset(cpi[:, :, 0:1], 0.0), sV)
        b.ch(v, v.tensor_copy(cpr[:, :, 1:16], csr[:, :, 0:15]), sV)
        b.ch(v, v.tensor_copy(cpi[:, :, 1:16], csi[:, :, 0:15]), sV)
        hr_b = hin[:, 0:8].unsqueeze(2).broadcast_to([128, 8, 16])
        hi_b = hin[:, 8:16].unsqueeze(2).broadcast_to([128, 8, 16])
        t1 = tm[0][:, :, 0:16]; t2 = tm[1][:, :, 0:16]
        b.ch(v, v.tensor_tensor(t1, D2qr[:], hr_b, ALU.mult), sV)
        b.ch(v, v.tensor_tensor(t2, D2qi[:], hi_b, ALU.mult), sV)
        b.ch(v, v.tensor_sub(t1, t1, t2), sV)
        b.ch(v, v.tensor_add(cpr[:], cpr[:], t1), sV)
        b.ch(v, v.tensor_tensor(t1, D2qr[:], hi_b, ALU.mult), sV)
        b.ch(v, v.tensor_tensor(t2, D2qi[:], hr_b, ALU.mult), sV)
        b.ch(v, v.tensor_add(t1, t1, t2), sV)
        b.ch(v, v.tensor_add(cpi[:], cpi[:], t1), sV)

    @block.vector
    def _(v):
        for kc in range(8):
            hr4 = hr[:, kc, :].rearrange("p (nq c) -> p nq c", c=TC2)
            hi4 = hi[:, kc, :].rearrange("p (nq c) -> p nq c", c=TC2)
            cmr = tm[0][:, kc, :].rearrange("p (nq c) -> p nq c", c=TC2)
            cmi = tm[1][:, kc, :].rearrange("p (nq c) -> p nq c", c=TC2)
            t1 = tm[2][:, kc, :].rearrange("p (nq c) -> p nq c", c=TC2)
            t2 = tm[3][:, kc, :].rearrange("p (nq c) -> p nq c", c=TC2)
            b.ch(v, v.tensor_copy(cmr, cpr[:, kc, :].unsqueeze(2).broadcast_to([128, 16, 16])), sV)
            b.ch(v, v.tensor_copy(cmi, cpi[:, kc, :].unsqueeze(2).broadcast_to([128, 16, 16])), sV)
            d1r_b = D1r[:, kc, 1:17].unsqueeze(1).broadcast_to([128, 16, 16])
            d1i_b = D1i[:, kc, 1:17].unsqueeze(1).broadcast_to([128, 16, 16])
            b.ch(v, v.tensor_tensor(t1, cmr, d1r_b, ALU.mult), sV)
            b.ch(v, v.tensor_tensor(t2, cmi, d1i_b, ALU.mult), sV)
            b.ch(v, v.tensor_sub(t1, t1, t2), sV)
            b.ch(v, v.tensor_add(hr4, hr4, t1), sV)
            b.ch(v, v.tensor_tensor(t1, cmi, d1r_b, ALU.mult), sV)
            b.ch(v, v.tensor_tensor(t2, cmr, d1i_b, ALU.mult), sV)
            b.ch(v, v.tensor_add(t1, t1, t2), sV)
            b.ch(v, v.tensor_add(hi4, hi4, t1), sV)

    b.free("lSr"); b.free("lSi")
    hpr = b.sb("lhpr", [128, 8, NZL], F32); hpi = b.sb("lhpi", [128, 8, NZL], F32)
    @block.vector
    def _(v):
        b.ch(v, v.tensor_copy(hpr[:, :, 1:NZL], hr[:, :, 0:NZL-1]), sV)
        b.ch(v, v.tensor_copy(hpi[:, :, 1:NZL], hi[:, :, 0:NZL-1]), sV)
        b.ch(v, v.tensor_copy(hpr[:, :, 0], hin[:, 0:8]), sV)
        b.ch(v, v.tensor_copy(hpi[:, :, 0], hin[:, 8:16]), sV)
        for l in range(TC1):
            o3 = yw[:, :, :].rearrange("p kc (nz l8) -> p kc nz l8", l8=TC1)[:, :, :, l]
            pr_b = Prt[:, l, :].unsqueeze(2).broadcast_to([128, 8, NZL])
            pn_b = Pnt[:, l, :].unsqueeze(2).broadcast_to([128, 8, NZL])
            b.ch(v, v.tensor_tensor(tm[0][:], hpr[:], pr_b, ALU.mult), sV)
            b.ch(v, v.tensor_add(o3, o3, tm[0][:]), sV)
            b.ch(v, v.tensor_tensor(tm[0][:], hpi[:], pn_b, ALU.mult), sV)
            b.ch(v, v.tensor_add(o3, o3, tm[0][:]), sV)
        for kc in range(8):
            b.ch(v, v.scalar_tensor_tensor(yw[:, kc, :], u[:, kc, :], l1D[:, kc:kc+1],
                                           yw[:, kc, :], ALU.mult, ALU.add), sV)

    for nm in ("ltm0", "ltm1", "ltm2", "ltm3"):
        b.free(nm)
    x1c = [b.sb(f"lx1{i}", [128, T], F32) for i in range(2)]
    ldc = {}; stc = {}; addc = {}
    for kc in range(8):
        @block.sync
        def _(sy, kc=kc):
            if kc == 0:
                b.wait(sy, sV)
            if kc >= 1:
                sy.wait_ge(b.h(sDs), ldc[kc-1])
            if kc >= 2:
                b.wait(sy, sV, addc[kc-2])
            ldc[kc] = b.dma(sy, x1c[kc % 2][:], hbm["x1"][:, kc, :], sDs)
        @block.vector
        def _(v, kc=kc):
            b.wait(v, sDs, ldc[kc])
            addc[kc] = b.inc(v.tensor_add(yw[:, kc, :], yw[:, kc, :], x1c[kc % 2][:]), sV)
        @block.sync
        def _(sy, kc=kc):
            b.wait(sy, sV, addc[kc])
            if kc >= 1:
                sy.wait_ge(b.h(sDso), stc[kc-1])
            stc[kc] = b.dma(sy, hbm["x2"][:, kc, :], yw[:, kc, :], sDso)
    @block.sync
    def _(sy):
        b.wait(sy, sDso)
    b.barrier()
    for nm in ("lru_u", "lApr", "lApi", "lkr", "lWr", "lWi", "lPr", "lPn",
               "lD1r", "lD1i", "lD2qr", "lD2qi", "lta8", "ltb8",
               "lCBr", "lCBi", "lhin", "lyw", "lhr", "lhi",
               "lcsr", "lcsi", "lcpr", "lcpi",
               "lhpr", "lhpi", "lcst", "lalg", "lx10", "lx11"):
        b.free(nm)
    b.barrier()


def emit_moe(b, io, hbm, C, lns, out_x3):
    nc, block = b.nc, b.block
    sD, sDs, sT, sV, sS, sG, sCC = C["sems"]
    sDw = C["sDw"]; sDw2 = C["sDw2"]; sDso = C["sDso"]
    id_f32 = C["id_f32"]; gate_bias = C["gate_bias"]

    gsb = b.sb("mgw", [128, 8, 8], BF16)
    @block.gpsimd
    def _(g):
        b.dma(g, gsb[:], io["gateT"].rearrange("(k p) j -> p k j", p=128), sD)
        g.wait_ge(b.h(sD), b.val(sD))
        b.inc(g.nop(), sG)
    b.barrier()

    for tt in range(2):
        x_ln3 = b.sb("mxln", [128, 8, TT], BF16)
        emit_ln(b, hbm["x2"][:, :, tt*TT:(tt+1)*TT], x_ln3, TT, lns, f"lnmoe{tt}")

        lg = b.sb("mlg", [8, TT], F32)
        g_em = b.sb("mgem", [8, TT], F32)
        tok = b.sb("mtok", [128, 8], F32)
        s8 = b.sb("ms8", [128, 8], F32)
        bvt = b.sb("mbv", [128, 8], F32)
        msk = b.sb("mmsk", [128, 8], F32)
        b2t = b.sb("mb2", [128, 8], F32)
        rcol = b.sb("mrc", [128, 4], F32)
        with nc.psum_tensor(f"mpslg{tt}", [8, 1024], F32) as ps_lg, \
             nc.psum_tensor(f"mpst{tt}", [128, 8], F32) as ps_t, \
             nc.psum_tensor(f"mps3{tt}", [8, 128], F32) as ps3:
            @block.tensor
            def _(t):
                b.wait(t, sV)
                b.wait(t, sG)
                for th in range(2):
                    for k in range(8):
                        ii = t.matmul(ps_lg[0:8, th*512:(th+1)*512], gsb[:, k, :],
                                      x_ln3[:, k, th*512:(th+1)*512], start=(k == 0), stop=(k == 7))
                b.inc(ii, sT)
            @block.vector
            def _(v):
                b.wait(v, sT)
                b.inc(v.tensor_copy(lg[:], ps_lg[0:8, :]), sV)
            for tb in range(8):
                @block.tensor
                def _(t, tb=tb):
                    b.wait(t, sV)
                    b.inc(t.transpose(ps_t[:], lg[0:8, tb*128:(tb+1)*128], id_f32[0:8, 0:8]), sT)
                @block.vector
                def _(v, tb=tb):
                    b.wait(v, sT)
                    b.ch(v, v.tensor_copy(tok[:], ps_t[:]), sV)
                    b.ch(v, v.reduce_max(rcol[:, 0:1], tok[:], AXX), sV)
                    b.ch(v, v.tensor_scalar(rcol[:, 1:2], rcol[:, 0:1], -1.0, None, ALU.mult), sV)
                @block.scalar
                def _(s, tb=tb):
                    b.wait(s, sV)
                    b.inc(s.activation(tok[:], tok[:], AF.Exp, bias=rcol[:, 1:2], scale=1.0), sS)
                @block.vector
                def _(v, tb=tb):
                    b.wait(v, sS)
                    b.ch(v, v.reduce_sum(rcol[:, 2:3], tok[:], AXX), sV)
                    b.ch(v, v.reciprocal(rcol[:, 3:4], rcol[:, 2:3]), sV)
                    b.ch(v, v.tensor_scalar(s8[:], tok[:], rcol[:, 3:4], None, ALU.mult), sV)
                    b.ch(v, v.tensor_add(bvt[:], s8[:], gate_bias[:, 0:8]), sV)
                    b.ch(v, v.reduce_max(rcol[:, 0:1], bvt[:], AXX), sV)
                    b.ch(v, v.tensor_scalar(msk[:], bvt[:], rcol[:, 0:1], None, ALU.is_ge), sV)
                    b.ch(v, v.tensor_mul(b2t[:], bvt[:], msk[:]), sV)
                    b.ch(v, v.tensor_sub(b2t[:], bvt[:], b2t[:]), sV)
                    b.ch(v, v.tensor_scalar(msk[:], msk[:], -1e30, None, ALU.mult), sV)
                    b.ch(v, v.tensor_add(b2t[:], b2t[:], msk[:]), sV)
                    b.ch(v, v.reduce_max(rcol[:, 0:1], b2t[:], AXX), sV)
                    b.ch(v, v.tensor_scalar(msk[:], bvt[:], rcol[:, 0:1], None, ALU.is_ge), sV)
                    b.ch(v, v.tensor_mul(tok[:], s8[:], msk[:]), sV)
                @block.tensor
                def _(t, tb=tb):
                    b.wait(t, sV)
                    b.inc(t.transpose(ps3[:], tok[:], id_f32[:, :]), sT)
                @block.vector
                def _(v, tb=tb):
                    b.wait(v, sT)
                    b.ch(v, v.tensor_copy(g_em[0:8, tb*128:(tb+1)*128], ps3[0:8, :]), sV)
        @block.gpsimd
        def _(g):
            b.wait(g, sV)
            b.dma(g, hbm["gmoe"][0:8, tt*TT:(tt+1)*TT], g_em[:], sD)
            g.wait_ge(b.h(sD), b.val(sD))
            b.inc(g.nop(), sG)
        for nm in ("mlg", "mtok", "ms8", "mbv", "mmsk", "mb2", "mrc"):
            b.free(nm)

        hs = b.sb("mhs", [128, 16, TT], BF16)
        wA = [b.sb(f"mwa{i}", [128, 8, 128], BF16) for i in range(2)]
        wB = [b.sb(f"mwb{i}", [128, 8, 128], BF16) for i in range(2)]
        sa = b.sb("msa", [128, 2, TT], F32)
        y_acc = b.sb("myacc", [128, 8, TT], F32)
        wC = [b.sb(f"mwc{i}", [128, 16, 128], BF16) for i in range(2)]
        gbc = b.sb("mgbc", [128, 2, TT], F32)
        he = b.sb("mhe", [128, 8, TT], BF16)
        tb1 = b.sb("mtb1", [128, 2, TT], F32)
        wD = [b.sb(f"mwd{i}", [128, 8, 128], BF16) for i in range(2)]
        tA = {}; sA = {}; vA = {}; tB = {}; vB = {}
        t0_ = b.val(sT)
        with nc.psum_tensor(f"mpsA{tt}", [128, 2048], F32) as psA, \
             nc.psum_tensor(f"mpsB{tt}", [128, 2048], F32) as psB:
            for m in range(16):
                mh = m % 2
                @block.gpsimd
                def _(g, m=m, mh=mh):
                    g.wait_ge(b.h(sT), tA[m-2] if m >= 2 else t0_)
                    b.dma(g, wA[mh][:], io["sw1s"][m].rearrange("(k p) j -> p k j", p=128), sDw[mh])
                    b.dma(g, wB[mh][:], io["sw3s"][m].rearrange("(k p) j -> p k j", p=128), sDw[mh])
                @block.tensor
                def _(t, m=m, mh=mh):
                    b.wait(t, sDw[mh])
                    if m >= 2:
                        t.wait_ge(b.h(sS), sA[m-2])
                        t.wait_ge(b.h(sV), vA[m-2])
                    for th in range(2):
                        o = mh*1024 + th*512
                        for k in range(8):
                            t.matmul(psA[:, o:o+512], wA[mh][:, k, :],
                                     x_ln3[:, k, th*512:(th+1)*512], start=(k == 0), stop=(k == 7))
                    for th in range(2):
                        o = mh*1024 + th*512
                        for k in range(8):
                            ii = t.matmul(psB[:, o:o+512], wB[mh][:, k, :],
                                          x_ln3[:, k, th*512:(th+1)*512], start=(k == 0), stop=(k == 7))
                    tA[m] = b.inc(ii, sT)
                @block.scalar
                def _(s, m=m, mh=mh):
                    b.wait(s, sT)
                    if m >= 2:
                        s.wait_ge(b.h(sV), vA[m-2])
                    sA[m] = b.inc(s.activation(sa[:, mh, :], psA[:, mh*1024:(mh+1)*1024],
                                               AF.Silu, bias=0.0, scale=1.0), sS)
                @block.vector
                def _(v, m=m, mh=mh):
                    b.wait(v, sS)
                    vA[m] = b.inc(v.tensor_tensor(hs[:, m, :], psB[:, mh*1024:(mh+1)*1024],
                                                  sa[:, mh, :], ALU.mult), sV)
            t0b = b.val(sT)
            for m in range(8):
                mh = m % 2
                @block.gpsimd
                def _(g, m=m, mh=mh):
                    g.wait_ge(b.h(sT), tB[m-2] if m >= 2 else t0b)
                    b.dma(g, wC[mh][:], io["sw2s"][m].rearrange("(k p) j -> p k j", p=128), sDw2[mh])
                @block.tensor
                def _(t, m=m, mh=mh):
                    b.wait(t, sDw2[mh])
                    if m < 2:
                        b.wait(t, sV)
                        b.wait(t, sS)
                    if m >= 2:
                        t.wait_ge(b.h(sV), vB[m-2])
                    for th in range(2):
                        o = mh*1024 + th*512
                        for k in range(16):
                            ii = t.matmul(psA[:, o:o+512], wC[mh][:, k, :],
                                          hs[:, k, th*512:(th+1)*512], start=(k == 0), stop=(k == 15))
                    tB[m] = b.inc(ii, sT)
                @block.vector
                def _(v, m=m, mh=mh):
                    b.wait(v, sT)
                    vB[m] = b.inc(v.tensor_copy(y_acc[:, m, :], psA[:, mh*1024:(mh+1)*1024]), sV)
            for e in range(8):
                eh = e % 2
                @block.gpsimd
                def _(g, e=e, eh=eh):
                    if e >= 2:
                        g.wait_ge(b.h(sV), vA[16 + (e-2)*8 + 7])
                    else:
                        b.wait(g, sG)
                    b.dma(g, gbc[:, eh, :],
                          hbm["gmoe"][e:e+1, tt*TT:(tt+1)*TT].broadcast_to([128, TT]), sD)
                for m in range(8):
                    aidx = 16 + e*8 + m
                    mh = aidx % 2
                    @block.gpsimd
                    def _(g, e=e, m=m, mh=mh, aidx=aidx):
                        g.wait_ge(b.h(sT), tA[aidx-2])
                        b.dma(g, wA[mh][:], io["ew1s"][e][m].rearrange("(k p) j -> p k j", p=128), sDw[mh])
                        b.dma(g, wB[mh][:], io["ew3s"][e][m].rearrange("(k p) j -> p k j", p=128), sDw[mh])
                    @block.tensor
                    def _(t, e=e, m=m, mh=mh, aidx=aidx):
                        b.wait(t, sDw[mh])
                        if m < 2:
                            b.wait(t, sV)
                        t.wait_ge(b.h(sS), sA[aidx-2])
                        t.wait_ge(b.h(sV), vA[aidx-2])
                        for th in range(2):
                            o = mh*1024 + th*512
                            for k in range(8):
                                t.matmul(psA[:, o:o+512], wA[mh][:, k, :],
                                         x_ln3[:, k, th*512:(th+1)*512], start=(k == 0), stop=(k == 7))
                        for th in range(2):
                            o = mh*1024 + th*512
                            for k in range(8):
                                ii = t.matmul(psB[:, o:o+512], wB[mh][:, k, :],
                                              x_ln3[:, k, th*512:(th+1)*512], start=(k == 0), stop=(k == 7))
                        tA[aidx] = b.inc(ii, sT)
                    @block.scalar
                    def _(s, m=m, mh=mh, aidx=aidx):
                        b.wait(s, sT)
                        s.wait_ge(b.h(sV), vA[aidx-2])
                        sA[aidx] = b.inc(s.activation(sa[:, mh, :], psA[:, mh*1024:(mh+1)*1024],
                                                      AF.Silu, bias=0.0, scale=1.0), sS)
                    @block.vector
                    def _(v, e=e, m=m, mh=mh, eh=eh, aidx=aidx):
                        b.wait(v, sS)
                        if e >= 1 and m == 0:
                            b.wait(v, sD)
                        elif e == 0 and m == 0:
                            b.wait(v, sD)
                        b.ch(v, v.tensor_tensor(tb1[:, mh, :], psB[:, mh*1024:(mh+1)*1024],
                                                gbc[:, eh, :], ALU.mult), sV)
                        vA[aidx] = b.inc(v.tensor_tensor(he[:, m, :], tb1[:, mh, :],
                                                         sa[:, mh, :], ALU.mult), sV)
                for m2 in range(8):
                    bidx = 8 + e*8 + m2
                    mh = bidx % 2
                    @block.gpsimd
                    def _(g, e=e, m2=m2, mh=mh, bidx=bidx):
                        g.wait_ge(b.h(sT), tB[bidx-2])
                        b.dma(g, wD[mh][:], io["ew2s"][e][m2].rearrange("(k p) j -> p k j", p=128), sDw2[mh])
                    @block.tensor
                    def _(t, m2=m2, mh=mh, bidx=bidx):
                        b.wait(t, sDw2[mh])
                        if m2 < 2:
                            b.wait(t, sV)
                            b.wait(t, sS)
                        t.wait_ge(b.h(sV), vB[bidx-2])
                        for th in range(2):
                            o = mh*1024 + th*512
                            for k in range(8):
                                ii = t.matmul(psA[:, o:o+512], wD[mh][:, k, :],
                                              he[:, k, th*512:(th+1)*512], start=(k == 0), stop=(k == 7))
                        tB[bidx] = b.inc(ii, sT)
                    @block.vector
                    def _(v, m2=m2, mh=mh, bidx=bidx):
                        b.wait(v, sT)
                        b.ch(v, v.tensor_add(y_acc[:, m2, :], y_acc[:, m2, :],
                                             psA[:, mh*1024:(mh+1)*1024]), sV)
                        vB[bidx] = b.val(sV)

        x2c = [b.sb(f"mx2{i}", [128, TT], F32) for i in range(2)]
        ldc = {}; stc = {}; adc = {}
        for kc in range(8):
            @block.sync
            def _(sy, kc=kc):
                if kc == 0:
                    b.wait(sy, sV)
                if kc >= 1:
                    sy.wait_ge(b.h(sDs), ldc[kc-1])
                if kc >= 2:
                    b.wait(sy, sV, adc[kc-2])
                ldc[kc] = b.dma(sy, x2c[kc % 2][:, 0:TT], hbm["x2"][:, kc, tt*TT:(tt+1)*TT], sDs)
            @block.vector
            def _(v, kc=kc):
                b.wait(v, sDs, ldc[kc])
                adc[kc] = b.inc(v.tensor_add(y_acc[:, kc, :], y_acc[:, kc, :], x2c[kc % 2][:, 0:TT]), sV)
            @block.sync
            def _(sy, kc=kc):
                b.wait(sy, sV, adc[kc])
                if kc >= 1:
                    sy.wait_ge(b.h(sDso), stc[kc-1])
                stc[kc] = b.dma(sy, out_x3[:, kc, tt*TT:(tt+1)*TT], y_acc[:, kc, :], sDso)
        @block.sync
        def _(sy):
            b.wait(sy, sDso)
        b.barrier()
        for nm in ("mxln", "mgem", "mhs", "mwa0", "mwa1", "mwb0", "mwb1", "msa",
                   "myacc", "mwc0", "mwc1", "mgbc", "mhe", "mtb1", "mwd0", "mwd1",
                   "mx20", "mx21"):
            b.free(nm)
    b.free("mgw")
    b.barrier()


def build_program():
    nc = bass.Bass("TRN2", target_bir_lowering=False)

    def din(name, shape, dt=F32):
        return nc.dram_tensor(name, shape, dt, kind="ExternalInput").ap()

    io = {}
    io["u_fm"] = din("u_fm", [128, 8, HALO + T])
    io["conv_w"] = din("conv_w", [128, 17, 4])
    io["conv_b"] = din("conv_b", [128, 17])
    io["dt_bias"] = din("dt_bias", [32, 1])
    io["A_log"] = din("A_log", [32, 1])
    io["D0"] = din("D0", [128, 16])
    io["norm_w"] = din("norm_w", [128, 16])
    io["w1s"] = din("w1s", [32, 1024, 128])
    io["w2s"] = din("w2s", [32, 1024, 128])
    io["h1s"] = din("h1s", [32, 4096, 128])
    io["h2s"] = din("h2s", [32, 4096, 128])
    io["w3s"] = din("w3s", [8, 4096, 128])
    io["wins"] = din("wins", [33, 1024, 128])
    io["windt"] = din("windt", [1024, 32])
    io["wouts"] = din("wouts", [8, 2048, 128])
    io["s1"] = din("s1", [128, 32])
    io["s2"] = din("s2", [128, 32])
    io["sh1"] = din("sh1", [128, 32])
    io["sh2"] = din("sh2", [128, 32])
    io["s3"] = din("s3", [128, 8])
    io["gateT"] = din("gateT", [1024, 8])
    io["gate_bias"] = din("gate_bias", [128, 8])
    io["ew1s"] = din("ew1s", [8, 8, 1024, 128])
    io["ew3s"] = din("ew3s", [8, 8, 1024, 128])
    io["ew2s"] = din("ew2s", [8, 8, 1024, 128])
    io["sw1s"] = din("sw1s", [16, 1024, 128])
    io["sw3s"] = din("sw3s", [16, 1024, 128])
    io["sw2s"] = din("sw2s", [8, 2048, 128])
    io["l1A"] = din("l1A", [128, 16])
    io["l1B"] = din("l1B", [128, 16])
    io["l1C"] = din("l1C", [128, 16])
    io["l1D"] = din("l1D", [128, 8])
    io["m8_64"] = din("m8_64", [64, 8])
    io["shl"] = din("shl", [64, 256])
    io["m8_128"] = din("m8_128", [128, 8])

    out_x3 = nc.dram_tensor("x3", [128, 8, T], F32, kind="ExternalOutput").ap()
    dbg = nc.dram_tensor("dbgo", [128, 18600], F32, kind="ExternalOutput").ap() if DBG else None

    x1_hbm = nc.dram_tensor("x1_hbm", [128, 8, T], F32, kind="Internal").ap()
    x2_hbm = nc.dram_tensor("x2_hbm", [128, 8, T], F32, kind="Internal").ap()
    xcv_hbm = nc.dram_tensor("xcv_hbm", [128, 17, T], BF16, kind="Internal").ap()
    acsf_hbm = nc.dram_tensor("acsf_hbm", [1, NCH_T * NH * CH], F32, kind="Internal").ap()
    acsf2_hbm = nc.dram_tensor("acsf2_hbm", [1, NCH_T * NH * CH], BF16, kind="Internal").ap()
    flat_hbm = nc.dram_tensor("flat_hbm", [1, 16384], F32, kind="Internal").ap()
    cc_min = nc.dram_tensor("cc_min", [64, NH * NS], F32, kind="Internal").ap()
    cc_mout = nc.dram_tensor("cc_mout", [NCC * 64, NH * NS], F32, kind="Internal").ap()
    cc_lin = nc.dram_tensor("cc_lin", [128, 16], F32, kind="Internal").ap()
    cc_lout = nc.dram_tensor("cc_lout", [NCC * 128, 16], F32, kind="Internal").ap()
    gmoe_hbm = nc.dram_tensor("gmoe_hbm", [8, T], F32, kind="Internal").ap()

    with nc.Block() as block:
        b = B(nc, block)
        b.init_pool(53056)
        sD = b.sem("dma_g")
        sDs = b.sem("dma_s")
        sT = b.sem("s_t")
        sV = b.sem("s_v")
        sS = b.sem("s_s")
        sG = b.sem("s_g")
        sCC = b.sem("s_cc")
        sDw = [b.sem("dma_w0"), b.sem("dma_w1")]
        sDw2 = [b.sem("dma_w2a"), b.sem("dma_w2b")]
        sDso = b.sem("dma_so")

        id_bf = b.sb("id_bf", [128, 128], BF16)
        id_f32 = b.sb("id_f32", [128, 128], F32)
        ones_bf = b.sb("ones_bf", [128, 1], BF16)
        ones_f32r = b.sb("ones_f32r", [1, 128], F32)
        maskST = b.sb("maskST", [64, 128], F32)
        conv_w = b.sb("conv_wsb", [128, 17, 4], F32)
        conv_b = b.sb("conv_bsb", [128, 17], F32)
        dt_bias = b.sb("dt_biassb", [32, 1], F32)
        A_col = b.sb("A_col", [32, 1], F32)
        D0 = b.sb("D0sb", [128, 16], F32)
        norm_w = b.sb("norm_wsb", [128, 16], F32)
        m8_64 = b.sb("m8_64sb", [64, 8], F32)
        shl_sb = b.sb("shlsb", [64, 256], F32)
        m8_128 = b.sb("m8_128sb", [128, 8], F32)
        gate_bias = b.sb("gate_biassb", [128, 8], F32)
        l1A = b.sb("l1Asb", [128, 16], F32)
        l1B = b.sb("l1Bsb", [128, 16], F32)
        l1C = b.sb("l1Csb", [128, 16], F32)
        l1D = b.sb("l1Dsb", [128, 8], F32)

        @block.vector
        def _(v):
            v.memset(ones_bf[:], 1.0)
            v.memset(ones_f32r[:], 1.0)
            v.memset(id_bf[:], 1.0)
            v.memset(id_f32[:], 1.0)
            b.inc(v.memset(maskST[:], 1.0), sV)

        @block.gpsimd
        def _(g):
            for dst, src in ((conv_w, "conv_w"), (conv_b, "conv_b"), (dt_bias, "dt_bias"),
                             (A_col, "A_log"), (D0, "D0"), (norm_w, "norm_w"),
                             (m8_64, "m8_64"), (m8_128, "m8_128"), (gate_bias, "gate_bias"),
                             (l1A, "l1A"), (l1B, "l1B"), (l1C, "l1C"), (l1D, "l1D"),
                             (shl_sb, "shl")):
                b.dma(g, dst[:], io[src], sD)
            g.wait_ge(b.h(sD), b.val(sD))
            b.wait(g, sV)
            g.affine_select(id_bf[:], id_bf[:], pattern=[[-1, 128]], compare_op=ALU.is_equal,
                            fill=0.0, base=0, channel_multiplier=1)
            g.affine_select(id_f32[:], id_f32[:], pattern=[[-1, 128]], compare_op=ALU.is_equal,
                            fill=0.0, base=0, channel_multiplier=1)
            g.affine_select(maskST[:], maskST[:], pattern=[[0, 2], [1, 64]],
                            compare_op=ALU.is_ge, fill=0.0, base=0, channel_multiplier=-1)
            b.inc(g.engine_nop(), sG)


        @block.scalar
        def _(s):
            b.wait(s, sG)
            b.ch(s, s.activation(A_col[:], A_col[:], AF.Exp, bias=0.0, scale=1.0), sS)
            b.inc(s.mul(A_col[:], A_col[:], -1.0), sS)

        b.barrier()

        W0 = HALO + T
        lns = {
            "ones_bf": ones_bf, "ones_f32r": ones_f32r,
            "sems": (sD, sDs, sT, sV, sS),
        }

        x_ln = b.sb("x_ln", [128, 8, W0], BF16)
        emit_ln(b, io["u_fm"], x_ln, W0, lns, "ln0")
        b.barrier()

        if DBG == "ln0":
            @block.gpsimd
            def _(g):
                b.wait(g, sV)
                for kc in range(8):
                    b.dma(g, dbg[:, kc*2051:(kc+1)*2051], x_ln[:, kc, :], sD)
                g.wait_ge(b.h(sD), b.val(sD))
            b.stack.close()
            return nc

        consts = {
            "sems": (sD, sDs, sT, sV, sS, sG, sCC),
            "sDw": sDw, "sDw2": sDw2, "sDso": sDso,
            "ones_bf": ones_bf, "ones_f32r": ones_f32r, "id_bf": id_bf, "id_f32": id_f32,
            "maskST": maskST, "conv_w": conv_w, "conv_b": conv_b, "dt_bias": dt_bias,
            "A_col": A_col, "D0": D0, "norm_w": norm_w, "m8_64": m8_64, "m8_128": m8_128,
            "gate_bias": gate_bias, "l1A": l1A, "l1B": l1B, "l1C": l1C, "l1D": l1D,
            "shl": shl_sb,
        }
        hbm = {
            "x1": x1_hbm, "x2": x2_hbm, "xcv": xcv_hbm, "acsf": acsf_hbm,
            "acsf2": acsf2_hbm, "flat": flat_hbm, "cc_min": cc_min, "cc_mout": cc_mout,
            "cc_lin": cc_lin, "cc_lout": cc_lout, "gmoe": gmoe_hbm,
        }

        emit_mamba(b, io, x_ln, consts, hbm, dbg)

        lvl = 99 if not DBG else {"ffnin": 1, "lruin": 2, "moein": 3}.get(DBG, 0)

        def dump_x1():
            @block.gpsimd
            def _(g):
                b.dma(g, dbg[:, 0:16384], x1_hbm[:, :, :].rearrange("p a t -> p (a t)"), sD)
                g.wait_ge(b.h(sD), b.val(sD))
                b.inc(g.nop(), sG)
            b.barrier()

        def emit_out(src):
            @block.sync
            def _(sy):
                b.wait(sy, sDso)
                b.dma(sy, out_x3[:], src, sDs)
                sy.wait_ge(b.h(sDs), b.val(sDs))

        if lvl >= 1:
            if DBG == "ffnin":
                dump_x1()
            emit_ffn0(b, io, hbm, consts, lns)
        if lvl == 1:
            emit_out(x1_hbm)
        if lvl >= 2:
            if DBG == "lruin":
                dump_x1()
            emit_lru(b, io, hbm, consts, lns)
        if lvl == 2:
            emit_out(x2_hbm)
        if lvl >= 3:
            if DBG == "moein":
                @block.gpsimd
                def _(g):
                    b.dma(g, dbg[:, 0:16384], x2_hbm[:, :, :].rearrange("p a t -> p (a t)"), sD)
                    g.wait_ge(b.h(sD), b.val(sD))
                    b.inc(g.nop(), sG)
                b.barrier()
            emit_moe(b, io, hbm, consts, lns, out_x3)

        b.stack.close()
        b.pool_guard.__exit__(None, None, None)
    return nc


def _prep_inputs(inputs):
    f = np.float32
    u = inputs["L0_attn_u"]
    Win = inputs["L0_attn_W_in"]
    winT = np.ascontiguousarray(Win.T)
    wins = np.zeros((33, 1024, 128), f)
    for m in range(32):
        c0 = m * 128
        c1 = min(c0 + 128, 4128)
        if c0 < 4128:
            wins[m, :, 0:c1-c0] = winT[:, c0:c1]
    wins[32, :, 0:16] = winT[:, 4096:4112]
    wins[32, :, 32:48] = winT[:, 4112:4128]
    windt = np.ascontiguousarray(winT[:, 4128:4160])
    wout = inputs["L0_attn_W_out"]
    woutT = np.ascontiguousarray(wout.T)
    wouts = np.ascontiguousarray(woutT.reshape(2048, 8, 128).transpose(1, 0, 2))

    def strips(wT, nm):
        return np.ascontiguousarray(wT.reshape(wT.shape[0], nm, 128).transpose(1, 0, 2))

    w1T = np.ascontiguousarray(inputs["L0_ffn_w1_w"].T)
    w2T = np.ascontiguousarray(inputs["L0_ffn_w2_w"].T)
    h1T = np.ascontiguousarray(inputs["L0_ffn_h1_w"].T)
    h2T = np.ascontiguousarray(inputs["L0_ffn_h2_w"].T)
    w3T = np.ascontiguousarray(inputs["L0_ffn_w3_w"].T)

    conv_w = np.zeros((128, 17, 4), f)
    cw = inputs["L0_attn_conv_w"][:, 0, :]
    cwp = np.zeros((17 * 128, 4), f)
    cwp[:2048] = cw[:2048]
    cwp[2048:2064] = cw[2048:2064]
    cwp[2048+32:2048+48] = cw[2064:2080]
    conv_w = np.ascontiguousarray(cwp.reshape(17, 128, 4).transpose(1, 0, 2))
    cbp = np.zeros(17 * 128, f)
    cbp[:2048] = inputs["L0_attn_conv_b"][:2048]
    cbp[2048:2064] = inputs["L0_attn_conv_b"][2048:2064]
    cbp[2048+32:2048+48] = inputs["L0_attn_conv_b"][2064:2080]
    conv_b = np.ascontiguousarray(cbp.reshape(17, 128).T)

    shl = np.zeros((64, 256), f)
    for k in range(64):
        shl[k, k] = 1.0
        shl[k, 128 + 64 + k] = 1.0

    def pack_pf(vec, ncols):
        return np.ascontiguousarray(vec.reshape(ncols, 128).T)

    D0 = pack_pf(np.repeat(inputs["L0_attn_D"], 64), 16)
    norm_w = pack_pf(inputs["L0_attn_norm_w"], 16)
    l1A = np.concatenate([pack_pf(inputs["L1_attn_A_real"], 8),
                          pack_pf(inputs["L1_attn_A_imag"], 8)], axis=1)
    l1B = np.concatenate([pack_pf(inputs["L1_attn_B_real"], 8),
                          pack_pf(inputs["L1_attn_B_imag"], 8)], axis=1)
    l1C = np.concatenate([pack_pf(inputs["L1_attn_C_real"], 8),
                          pack_pf(inputs["L1_attn_C_imag"], 8)], axis=1)
    l1D = pack_pf(inputs["L1_attn_D"], 8)
    gateT = np.ascontiguousarray(inputs["L1_ffn_gate_w"].T)
    gate_bias = np.ascontiguousarray(np.broadcast_to(inputs["L1_ffn_gate_bias"], (128, 8)))
    ew1s = np.stack([strips(np.ascontiguousarray(inputs["L1_ffn_w1"][e].T), 8) for e in range(8)])
    ew3s = np.stack([strips(np.ascontiguousarray(inputs["L1_ffn_w3"][e].T), 8) for e in range(8)])
    ew2s = np.stack([strips(np.ascontiguousarray(inputs["L1_ffn_w2"][e].T), 8) for e in range(8)])
    sw1s = strips(np.ascontiguousarray(inputs["L1_ffn_sw1"].T), 16)
    sw3s = strips(np.ascontiguousarray(inputs["L1_ffn_sw3"].T), 16)
    sw2s = strips(np.ascontiguousarray(inputs["L1_ffn_sw2"].T), 8)

    common = dict(
        wins=wins, windt=windt, wouts=wouts, shl=shl,
        conv_w=conv_w, conv_b=conv_b,
        dt_bias=inputs["L0_attn_dt_bias"].reshape(32, 1).astype(f),
        A_log=inputs["L0_attn_A_log"].reshape(32, 1).astype(f),
        D0=D0, norm_w=norm_w,
        w1s=strips(w1T, 32), w2s=strips(w2T, 32),
        h1s=strips(h1T, 32), h2s=strips(h2T, 32), w3s=strips(w3T, 8),
        s1=pack_pf(inputs["L0_ffn_w1_s"], 32), s2=pack_pf(inputs["L0_ffn_w2_s"], 32),
        sh1=pack_pf(inputs["L0_ffn_h1_s"], 32), sh2=pack_pf(inputs["L0_ffn_h2_s"], 32),
        s3=pack_pf(inputs["L0_ffn_w3_s"], 8),
        gateT=gateT, gate_bias=gate_bias,
        ew1s=ew1s, ew3s=ew3s, ew2s=ew2s, sw1s=sw1s, sw3s=sw3s, sw2s=sw2s,
        l1A=l1A, l1B=l1B, l1C=l1C, l1D=l1D,
    )

    in_maps = []
    for c in range(8):
        bb, sh = c // 2, c % 2
        t0 = sh * T
        u_slice = u[bb, t0:t0 + T]
        halo = u[bb, t0 - HALO:t0] if sh == 1 else np.zeros((HALO, DM), f)
        u_ext = np.concatenate([halo, u_slice], 0)
        u_fm = np.ascontiguousarray(u_ext.T).reshape(8, 128, HALO + T).transpose(1, 0, 2)
        u_fm = np.ascontiguousarray(u_fm)
        m8v_64 = np.zeros((64, 8), f)
        m8v_128 = np.zeros((128, 8), f)
        if sh == 1:
            m8v_64[:, c - 1] = 1.0
            m8v_128[:, c - 1] = 1.0
        m = dict(common)
        m["u_fm"] = u_fm
        m["m8_64"] = m8v_64
        m["m8_128"] = m8v_128
        in_maps.append(m)
    return in_maps


def kernel(**inputs):
    in_maps = _prep_inputs(inputs)
    nc = build_program()
    res = run_bass_kernel_spmd(nc, in_maps, core_ids=list(range(8)))
    out = np.zeros((4, 4096, 1024), np.float32)
    for c in range(8):
        bb, sh = c // 2, c % 2
        x3 = np.asarray(res.results[c]["x3"])
        out[bb, sh*T:(sh+1)*T] = x3.transpose(1, 0, 2).reshape(1024, T).T
    return out


if __name__ == "__main__":
    nc = build_program()
    print("build ok")
